# version 04
# speedup vs baseline: 1.1462x; 1.0494x over previous
import sys
if "/opt/trn_rl_repo" not in sys.path:
    sys.path.insert(0, "/opt/trn_rl_repo")

import contextlib
import numpy as np
import concourse.bass as bass
import concourse.mybir as mybir
from concourse.bass_utils import run_bass_kernel_spmd

F32 = mybir.dt.float32
BF16 = mybir.dt.bfloat16
AF = mybir.ActivationFunctionType
ALU = mybir.AluOpType
AX = mybir.AxisListType

D = 1024
T = 2048
G = 8
NTB = 4
TBW = 512
H = 4
DH = 256
C = 128
NCH = T // C
FLOOR = 0.01
HID = 4096
QH = 16
KVH = 8
DHB = 64
KV = 512

DBG = []
PHASE = 6
SCANCH = NCH
SCAN_DBG = ""
BSTAGE = 4
STATS_LIM = NCH
STATS_NOFIX = False

QA = [0, 1, 4, 5, 8, 9, 12, 13]
QB = [2, 3, 6, 7, 10, 11, 14, 15]

_PAIRS = [(0, 1), (0, 2), (0, 3), (1, 2), (1, 3), (2, 3)]


def _stat_rows():
    out = []
    for b in range(4):
        out.append(("mean", (b,), 4 * b, 1.0 / DH))
    for pi, (a, b) in enumerate(_PAIRS[:4]):
        out.append(("prod", (a, b), 16 + 4 * pi, 1.0 / DH))
    for b in range(4):
        out.append(("sq", (b,), 32 + 4 * b, 1.0 / (DH - 1)))
    for pi, (a, b) in enumerate(_PAIRS[4:]):
        out.append(("prod", (a, b), 48 + 4 * pi, 1.0 / DH))
    return out


def _feat_perm():
    our = np.zeros(56, np.int64)
    for b in range(4):
        for h in range(4):
            our[4 * b + h] = (2 * b) * 4 + h
            our[32 + 4 * b + h] = (2 * b + 1) * 4 + h
    for pi in range(4):
        for h in range(4):
            our[16 + 4 * pi + h] = 32 + 4 * pi + h
    for pi in range(2):
        for h in range(4):
            our[48 + 4 * pi + h] = 32 + (4 + pi) * 4 + h
    return our


def _retile_km(w, kc=128):
    k, m = w.shape
    assert k % kc == 0
    return np.ascontiguousarray(w.reshape(k // kc, kc, m))


def _pad_rows(w, rows):
    out = np.zeros((rows, w.shape[1]), w.dtype)
    out[: w.shape[0]] = w
    return out


def prep_inputs(inp):
    f32 = np.float32
    x = np.asarray(inp["L0_attn_a_hidden_states"], f32)
    shared = {}

    for nm in ("Wq", "Wk", "Wv"):
        w = np.asarray(inp["L0_attn_a_" + nm], f32)
        shared["a_" + nm] = np.ascontiguousarray(
            w.reshape(G, 128, G, 128).transpose(2, 0, 1, 3))
    shared["a_Wb"] = _retile_km(np.asarray(inp["L0_attn_a_Wb"], f32))
    for nm in ("conv_q_w", "conv_k_w", "conv_v_w"):
        shared["a_" + nm] = np.ascontiguousarray(
            np.asarray(inp["L0_attn_a_" + nm], f32).reshape(G, 128, 4))
    shared["a_local_w"] = np.ascontiguousarray(
        np.asarray(inp["L0_attn_a_local_w"], f32).reshape(G, 128, 7))
    shared["a_mid_w"] = np.ascontiguousarray(
        np.asarray(inp["L0_attn_a_mid_w"], f32).reshape(G, 128, 25))
    rw1 = np.asarray(inp["L0_attn_a_r_w1"], f32)
    perm = _feat_perm()
    rw1p = np.zeros((1152, 2176), f32)
    rw1p[:1024, :2160] = rw1[:1024]
    rw1p[1024 : 1024 + 56, :2160] = rw1[1024 + perm]
    rw1g = np.zeros((5, 9, 128, 512), f32)
    for mg in range(5):
        cols = rw1p[:, 512 * mg : 512 * (mg + 1)]
        rw1g[mg, :, :, : cols.shape[1]] = cols.reshape(9, 128, -1)
    shared["a_rw1"] = np.ascontiguousarray(rw1g)
    shared["a_rb1"] = np.ascontiguousarray(
        _pad_rows(np.asarray(inp["L0_attn_a_r_b1"], f32)[:, None], 2176)
        .reshape(17, 128).T)
    rw2 = _pad_rows(np.asarray(inp["L0_attn_a_r_w2"], f32), 2176)
    shared["a_rw2"] = _retile_km(rw2)
    shared["a_rb2"] = np.ascontiguousarray(
        np.asarray(inp["L0_attn_a_r_b2"], f32).reshape(16, 1))
    shared["a_Wo"] = np.ascontiguousarray(
        np.asarray(inp["L0_attn_a_Wo"], f32)
        .reshape(G, 128, G, 128).transpose(2, 0, 1, 3))
    selbank = np.zeros((128, 14 * 132), f32)
    for si, (_, _, r0, coef) in enumerate(_stat_rows()):
        selbank[:, 132 * si + r0 + 3] = coef
    shared["a_selbank"] = selbank
    psel = np.zeros((16, 16, 128), f32)
    for s in range(16):
        psel[s, s, :] = 1.0
    shared["a_psel"] = psel
    mixw = np.asarray(inp["L0_attn_a_mix_w"], f32)
    onw = np.asarray(inp["L0_attn_a_onorm_w"], f32)
    mixsel = np.zeros((G, 4, 128), f32)
    onsel = np.zeros((G, 4, 128), f32)
    for g in range(G):
        h = g // 2
        mixsel[g, h, :] = mixw[h, (g % 2) * 128 : (g % 2) * 128 + 128]
        onsel[g, h, :] = onw[(g % 2) * 128 : (g % 2) * 128 + 128]
    shared["a_mixsel"] = mixsel
    shared["a_onsel"] = onsel
    hsel = np.zeros((G, 128, 4), f32)
    for g in range(G):
        hsel[g, :, g // 2] = 1.0
    shared["a_hsel"] = hsel

    w1 = np.asarray(inp["L0_ffn_W1"], f32)
    b1cat = np.asarray(inp["L0_ffn_B1"], f32).reshape(24, HID)
    w1aug = np.concatenate([w1, _pad_rows(b1cat, 128)], 0)
    shared["f_w1"] = np.ascontiguousarray(
        w1aug.reshape(9, 128, 2, 2048).transpose(2, 0, 1, 3))
    w2 = np.asarray(inp["L0_ffn_W2"], f32)
    shared["f_w2"] = np.ascontiguousarray(w2.reshape(2, 16, 128, D))
    b2cat = np.asarray(inp["L0_ffn_B2"], f32).reshape(24, D)
    shared["f_b2cat"] = np.ascontiguousarray(_pad_rows(b2cat, 128))
    a1 = np.asarray(inp["L0_ffn_A1"], f32).transpose(1, 0, 2).reshape(D, 24)
    r1 = np.asarray(inp["L0_ffn_R1"], f32)
    shared["f_a1r1"] = _retile_km(np.concatenate([a1, r1], 1))
    a2 = np.asarray(inp["L0_ffn_A2"], f32).transpose(1, 0, 2).reshape(HID, 24)
    r2 = np.asarray(inp["L0_ffn_R2"], f32)
    shared["f_a2r2"] = np.ascontiguousarray(
        np.concatenate([a2, r2], 1).reshape(2, 16, 128, 27))
    shared["f_b1"] = np.ascontiguousarray(
        np.asarray(inp["L0_ffn_b1"], f32).reshape(32, 128).T)
    shared["f_b2"] = np.ascontiguousarray(
        np.asarray(inp["L0_ffn_b2"], f32).reshape(G, 128).T)

    wqb = np.asarray(inp["L0_attn_b_Wq"], f32)
    qperm = np.zeros(D, np.int64)
    for j in range(8):
        qperm[j * 128 : j * 128 + 64] = QA[j] * 64 + np.arange(64)
        qperm[j * 128 + 64 : (j + 1) * 128] = QB[j] * 64 + np.arange(64)
    wqb = wqb[:, qperm]
    shared["b_Wq"] = np.ascontiguousarray(
        wqb.reshape(G, 128, G, 128).transpose(2, 0, 1, 3))
    for nm, go in (("Wk", 4), ("Wv", 4)):
        w = np.asarray(inp["L0_attn_b_" + nm], f32)
        shared["b_" + nm] = np.ascontiguousarray(
            w.reshape(G, 128, go, 128).transpose(2, 0, 1, 3))
    shared["b_mem"] = _retile_km(np.asarray(inp["L0_attn_b_memory"], f32))
    shared["b_memn"] = np.ascontiguousarray(
        np.asarray(inp["L0_attn_b_memory_norm"], f32).reshape(4, 128, 1))
    shared["b_mw"] = np.asarray(
        inp["L0_attn_b_memory_weight"], f32).reshape(1, 1)
    e8x = np.zeros((32, 16), f32)
    for j in range(8):
        e8x[16 + QA[j] // 2, 2 * j] = 1.0
        e8x[16 + QB[j] // 2, 2 * j + 1] = 1.0
    shared["b_e8"] = e8x
    ones2 = np.zeros((128, 2), f32)
    ones2[:64, 0] = 1.0
    ones2[64:, 1] = 1.0
    shared["c_ones2"] = ones2

    shared["c_eye"] = np.eye(128, dtype=f32)
    shared["c_ones_col"] = np.ones((128, 1), f32)
    shared["c_ones_row"] = np.ones((1, 128), f32)
    shift32 = np.zeros((64, 64), f32)
    for j in range(16):
        shift32[j, 32 + j] = 1.0
    shared["c_shift32"] = shift32
    shared["c_sl_mask"] = np.tril(np.ones((128, 128), f32), -1)
    shared["c_ui_mask"] = np.triu(np.ones((128, 128), f32), 0)

    per_core = [{"x_in": np.ascontiguousarray(x[b].T)} for b in range(8)]
    return per_core, shared



class Buf:
    __slots__ = ("name", "w", "r")

    def __init__(self, name):
        self.name = name
        self.w = set()
        self.r = set()


class DmaStream:
    __slots__ = ("name", "sem", "cum")

    def __init__(self, name, sem):
        self.name = name
        self.sem = sem
        self.cum = 0


class KB:
    ENGS = ("pe", "dve", "act", "pool", "sp")

    def __init__(self, nc, stack):
        self.nc = nc
        self.e = dict(pe=nc.tensor, dve=nc.vector, act=nc.scalar,
                      pool=nc.gpsimd, sp=nc.sync)
        self.rec = {k: [] for k in self.ENGS}
        self.assigned = {k: [] for k in self.ENGS}
        self.esem = {k: stack.enter_context(nc.semaphore("s_" + k))
                     for k in ("pe", "dve", "act", "pool")}
        self.wm = {}
        self.streams = {}
        self.stack = stack
        self.taps = {}
        self.tap_stream = self.stream("taps")
        self.n_waits = 0
        self.barrier_toks = []
        self.barrier_gen = 0
        self.barrier_seen = {}
        self.log = []

    def stream(self, name):
        if name not in self.streams:
            sem = self.stack.enter_context(self.nc.semaphore("d_" + name))
            self.streams[name] = DmaStream(name, sem)
        return self.streams[name]

    def _resolve_eng(self, eng, idx):
        a = self.assigned[eng]
        lo, hi = 0, len(a)
        while lo < hi:
            mid = (lo + hi) // 2
            if a[mid][0] >= idx:
                hi = mid
            else:
                lo = mid + 1
        if lo < len(a):
            return ("e" + eng, self.esem[eng], a[lo][1])
        cnt = (a[-1][1] if a else 0) + 1
        self.rec[eng][idx].then_inc(self.esem[eng], 1)
        self.log.append((eng, "inc@", idx, cnt))
        a.append((idx, cnt))
        return ("e" + eng, self.esem[eng], cnt)

    def _wait(self, eng, tok, force_same=False):
        if tok[0] == "d":
            _, stream, cum = tok
            key, sem, cnt = ("d" + stream.name, stream.sem, stream.cum)
        else:
            _, peng, idx = tok
            if peng == eng and not force_same:
                return
            key, sem, cnt = self._resolve_eng(peng, idx)
        if self.wm.get((eng, key), 0) >= cnt:
            self.log.append((eng, "skipwait", key, cnt))
            return
        self.e[eng].wait_ge(sem, cnt)
        self.log.append((eng, "wait", key, cnt))
        self.n_waits += 1
        self.wm[(eng, key)] = cnt

    def _deps(self, eng, reads, writes, sreads=()):
        if self.barrier_seen.get(eng, 0) < self.barrier_gen:
            self.barrier_seen[eng] = self.barrier_gen
            for tok in self.barrier_toks:
                self._wait(eng, tok)
        toks = set()
        for b in reads:
            toks |= b.w
        for b in writes:
            toks |= b.w | b.r
        stoks = set()
        for b in sreads:
            stoks |= b.w
        nrec = len(self.rec[eng])
        for tok in sorted(stoks - toks, key=repr):
            self._wait(eng, tok, force_same=True)
        for tok in sorted(toks, key=repr):
            force = (tok in stoks) or (
                tok[0] == "e" and tok[1] == eng and tok[2] >= nrec - 8)
            self._wait(eng, tok, force_same=force)

    def mark(self, *args):
        self.log.append(("MARK",) + args)

    def barrier(self):
        toks = []
        for eng in ("pe", "dve", "act", "pool"):
            n = len(self.rec[eng])
            if n:
                toks.append(("e", eng, n - 1))
        for s in self.streams.values():
            if s.cum:
                toks.append(("d", s, s.cum))
        self.barrier_toks = toks
        self.barrier_gen += 1

    def op(self, eng, emit, reads=(), writes=(), accum=False, sreads=()):
        self._deps(eng, reads, writes, sreads)
        inst = emit(self.e[eng])
        idx = len(self.rec[eng])
        self.rec[eng].append(inst)
        self.log.append((eng, "op", idx, getattr(inst, "name", "")))
        tok = ("e", eng, idx)
        for b in list(reads) + list(sreads):
            b.r.add(tok)
        for b in writes:
            if accum:
                b.w.add(tok)
            else:
                b.w = {tok}
                b.r = set()
        return tok

    def dma(self, qeng, stream, out_ap, in_ap, reads=(), writes=(),
            accum=False):
        self._deps(qeng, reads, writes)
        inst = self.e[qeng].dma_start(out=out_ap, in_=in_ap)
        stream.cum += 16
        inst.then_inc(stream.sem, 16)
        tok = ("d", stream, stream.cum)
        for b in reads:
            b.r.add(tok)
        for b in writes:
            if accum:
                b.w.add(tok)
            else:
                b.w = {tok}
                b.r = set()
        return tok

    def tap(self, name, shape, srcs):
        if name not in DBG:
            return
        t = self.nc.dram_tensor("dbg_" + name, list(shape), F32,
                                kind="ExternalOutput").ap()
        self.taps[name] = t
        for ap, buf, slc in srcs:
            self.dma("pool", self.tap_stream, slc(t), ap, reads=[buf])

    def finish(self):
        for s in self.streams.values():
            if s.cum:
                self.e["sp"].wait_ge(s.sem, s.cum)



class _Skip(Exception):
    pass


def build_program():
    nc = bass.Bass("TRN2", target_bir_lowering=False)
    st = contextlib.ExitStack()

    dr = {}
    dr["x_in"] = nc.dram_tensor("x_in", [D, T], F32, kind="ExternalInput").ap()
    shapes = {
        "a_Wq": [G, G, 128, 128], "a_Wk": [G, G, 128, 128],
        "a_Wv": [G, G, 128, 128], "a_Wb": [G, 128, 4],
        "a_conv_q_w": [G, 128, 4], "a_conv_k_w": [G, 128, 4],
        "a_conv_v_w": [G, 128, 4],
        "a_local_w": [G, 128, 7], "a_mid_w": [G, 128, 25],
        "a_rw1": [5, 9, 128, 512], "a_rb1": [128, 17],
        "a_rw2": [17, 128, 16], "a_rb2": [16, 1],
        "a_Wo": [G, G, 128, 128],
        "a_selbank": [128, 14 * 132], "a_psel": [16, 16, 128],
        "a_mixsel": [G, 4, 128], "a_onsel": [G, 4, 128],
        "f_w1": [2, 9, 128, 2048], "f_w2": [2, 16, 128, D],
        "f_b2cat": [128, D],
        "f_a1r1": [G, 128, 27], "f_a2r2": [2, 16, 128, 27],
        "f_b1": [128, 32], "f_b2": [128, G],
        "b_Wq": [G, G, 128, 128], "b_Wk": [4, G, 128, 128],
        "b_Wv": [4, G, 128, 128],
        "b_mem": [4, 128, KV], "b_memn": [4, 128, 1], "b_mw": [1, 1],
        "b_e8": [32, 16], "c_ones2": [128, 2],
        "a_hsel": [G, 128, 4],
        "c_eye": [128, 128], "c_ones_col": [128, 1], "c_ones_row": [1, 128],
        "c_sl_mask": [128, 128], "c_ui_mask": [128, 128],
        "c_shift32": [64, 64],
    }
    for nm, shp in shapes.items():
        dr[nm] = nc.dram_tensor(nm, shp, F32, kind="ExternalInput").ap()
    out_ap = nc.dram_tensor("out", [T, D], F32, kind="ExternalOutput").ap()
    xsp = nc.dram_tensor("x_spill", [G, 128, T], F32, kind="Internal").ap()
    aout_sp = nc.dram_tensor("aout_spill", [G, 128, T], BF16,
                             kind="Internal").ap()

    kb = KB(nc, st)

    def SB(stk):
        def f(name, shape, dtype):
            t = stk.enter_context(nc.sbuf_tensor(name, shape, dtype))
            return t, Buf(name)
        return f

    def PS(stk):
        def f(name, shape, dtype):
            t = stk.enter_context(nc.psum_tensor(name, shape, dtype))
            return t, Buf(name)
        return f

    sb = SB(st)

    cst = kb.stream("consts")
    eye_bf, eye_bf_b = sb("eye_bf", [128, 128], BF16)
    eye_f, eye_f_b = sb("eye_f", [128, 128], F32)
    ones_col, ones_col_b = sb("ones_col", [128, 1], BF16)
    ones_row, ones_row_b = sb("ones_row", [1, 128], BF16)
    sl_mask, sl_mask_b = sb("sl_mask", [128, 128], BF16)
    ui_mask, ui_mask_b = sb("ui_mask", [128, 128], BF16)
    kb.dma("pool", cst, eye_bf[:], dr["c_eye"], writes=[eye_bf_b])
    kb.dma("sp", cst, eye_f[:], dr["c_eye"], writes=[eye_f_b])
    kb.dma("pool", cst, ones_col[:], dr["c_ones_col"], writes=[ones_col_b])
    kb.dma("pool", cst, ones_row[:], dr["c_ones_row"], writes=[ones_row_b])
    kb.dma("pool", cst, sl_mask[:], dr["c_sl_mask"], writes=[sl_mask_b])
    kb.dma("pool", cst, ui_mask[:], dr["c_ui_mask"], writes=[ui_mask_b])

    constf_t, constf_b = sb("constf", [128, 16], F32)
    _constf = {}

    def constf(v):
        v = float(v)
        if v not in _constf:
            i = len(_constf)
            _constf[v] = i
            kb.op("pool", lambda e, i=i, v=v: e.memset(
                constf_t[:, i : i + 1], v), writes=[constf_b], accum=True)
            nc.const_aps.register(F32, v, constf_t[:, i : i + 1])
        return constf_b

    def layernorm(tag, x_t, x_b, xb_t, xb_b, scope=None):
        lnst = contextlib.ExitStack()
        lsb, lps = SB(lnst), PS(lnst)
        xbp_t, xbp_b = lsb(f"ln{tag}_xbp", [128, TBW], BF16)
        xsq_t, xsq_b = lsb(f"ln{tag}_xsq", [128, TBW], BF16)
        ps_s, ps_s_b = lps(f"ln{tag}_ps", [1, TBW], F32)
        ps_q, ps_q_b = lps(f"ln{tag}_pq", [1, TBW], F32)
        ps_bc, ps_bc_b = lps(f"ln{tag}_pbc", [128, TBW], F32)
        row_a, row_a_b = lsb(f"ln{tag}_ra", [1, TBW], F32)
        mrow, mrow_b = lsb(f"ln{tag}_m", [1, TBW], F32)
        rrow, rrow_b = lsb(f"ln{tag}_r", [1, TBW], F32)
        rowbf, rowbf_b = lsb(f"ln{tag}_bf", [1, TBW], BF16)
        for tb in range(NTB):
            sl = slice(tb * TBW, (tb + 1) * TBW)
            for g in range(G):
                kb.op("dve", lambda e, g=g, sl=sl: e.tensor_copy(
                    xbp_t[:], x_t[g][:, sl]), reads=[x_b[g]], writes=[xbp_b])
                kb.op("act", lambda e, g=g, sl=sl: e.activation(
                    xsq_t[:], x_t[g][:, sl], AF.Square),
                    reads=[x_b[g]], writes=[xsq_b])
                kb.op("pe", lambda e, g=g: e.matmul(
                    ps_s[:], ones_col[:], xbp_t[:],
                    start=(g == 0), stop=(g == G - 1)),
                    reads=[ones_col_b, xbp_b], writes=[ps_s_b], accum=(g > 0))
                kb.op("pe", lambda e, g=g: e.matmul(
                    ps_q[:], ones_col[:], xsq_t[:],
                    start=(g == 0), stop=(g == G - 1)),
                    reads=[ones_col_b, xsq_b], writes=[ps_q_b], accum=(g > 0))
            kb.op("dve", lambda e: e.tensor_scalar(
                mrow[:], ps_s[:], 1.0 / D, None, ALU.mult),
                reads=[ps_s_b], writes=[mrow_b])
            kb.op("act", lambda e: e.activation(row_a[:], mrow[:], AF.Square),
                  reads=[mrow_b], writes=[row_a_b])
            kb.op("dve", lambda e: e.scalar_tensor_tensor(
                rrow[:], ps_q[:], 1.0 / D, row_a[:], ALU.mult, ALU.subtract),
                reads=[ps_q_b, row_a_b], writes=[rrow_b])
            kb.op("act", lambda e: e.activation(
                rrow[:], rrow[:], AF.Sqrt, bias=1e-5),
                reads=[rrow_b, constf(1e-5)], writes=[rrow_b], accum=True)
            kb.op("dve", lambda e: e.reciprocal(rrow[:], rrow[:]),
                  reads=[rrow_b], writes=[rrow_b], accum=True)
            kb.op("dve", lambda e: e.tensor_scalar(
                rowbf[:], mrow[:], -1.0, None, ALU.mult),
                reads=[mrow_b], writes=[rowbf_b])
            kb.op("pe", lambda e: e.matmul(
                ps_bc[:], ones_row[:], rowbf[0:1, :], start=True, stop=True),
                reads=[ones_row_b, rowbf_b], writes=[ps_bc_b])
            for g in range(G):
                kb.op("dve", lambda e, g=g, sl=sl: e.tensor_tensor(
                    xb_t[g][:, sl], x_t[g][:, sl], ps_bc[:], ALU.add),
                    reads=[x_b[g], ps_bc_b], writes=[xb_b[g]], accum=True)
            kb.op("dve", lambda e: e.tensor_copy(rowbf[:], rrow[:]),
                  reads=[rrow_b], writes=[rowbf_b])
            kb.op("pe", lambda e: e.matmul(
                ps_bc[:], ones_row[:], rowbf[0:1, :], start=True, stop=True),
                reads=[ones_row_b, rowbf_b], writes=[ps_bc_b])
            for g in range(G):
                kb.op("dve", lambda e, g=g, sl=sl: e.tensor_tensor(
                    xb_t[g][:, sl], xb_t[g][:, sl], ps_bc[:], ALU.mult),
                    reads=[ps_bc_b] + [xb_b[g]], writes=[xb_b[g]], accum=True)
        kb.tap(f"ln{tag}", [D, T], [(xb_t[g][:], xb_b[g],
                                     (lambda t, g=g: t[g * 128:(g + 1) * 128]))
                                    for g in range(G)])
        kb.barrier()
        lnst.close()

    def proj_mms(pt, pb, wtile, wbuf, rhs_t, rhs_b, sl, nk=G):
        for gi in range(nk):
            kb.op("pe", lambda e, gi=gi: e.matmul(
                pt, wtile[:, gi * 128:(gi + 1) * 128], rhs_t[gi][:, sl],
                start=(gi == 0), stop=(gi == nk - 1)),
                reads=[wbuf, rhs_b[gi]], writes=[pb], accum=(gi > 0))

    s1 = st.enter_context(contextlib.ExitStack())
    sb1 = SB(s1)
    xb_t = []
    xb_b = []
    for g in range(G):
        t, b = sb1(f"xb{g}", [128, T], BF16)
        xb_t.append(t)
        xb_b.append(b)
    v_t = []
    v_b = []
    for g in range(G):
        t, b = sb1(f"v{g}", [128, T], BF16)
        v_t.append(t)
        v_b.append(b)
    dl_t = []
    dl_b = []
    for g in range(G):
        t, b = sb1(f"dl{g}", [128, T], BF16)
        dl_t.append(t)
        dl_b.append(b)
    beta_t, beta_b = sb1("beta", [128, 4 * NCH], F32)

    with contextlib.ExitStack() as sA:
        lsb, lps = SB(sA), PS(sA)
        x_t = []
        x_b = []
        for g in range(G):
            t, b = lsb(f"x{g}", [128, T], F32)
            x_t.append(t)
            x_b.append(b)
        xs = kb.stream("x_io")
        xin3 = dr["x_in"].rearrange("(g p) t -> g p t", p=128)
        for g in range(G):
            kb.dma("sp", xs, x_t[g][:], xin3[g], writes=[x_b[g]])

        layernorm("1", x_t, x_b, xb_t, xb_b, sA)

        ws = kb.stream("wstr")
        wst = [lsb("a1_wst0", [128, D], BF16), lsb("a1_wst1", [128, D], BF16)]
        wb_tiles = []
        for g in range(G):
            t, b = lsb(f"a_Wb_{g}", [128, 4], BF16)
            kb.dma("pool", ws, t[:], dr["a_Wb"][g], writes=[b])
            wb_tiles.append((t, b))
        cvw = []
        for g in range(G):
            t, b = lsb(f"cvw{g}", [128, 4], F32)
            kb.dma("sp", ws, t[:], dr["a_conv_v_w"][g], writes=[b])
            cvw.append((t, b))

        pp, pp_b = lps("a1_pp", [128, TBW], F32)
        pp2, pp2_b = lps("a1_pp2", [128, TBW], F32)
        pbeta, pbeta_b = lps("a1_pbeta", [128, 8], F32)
        cpre_t, cpre_b = lsb("a1_cpre", [128, T + 4], BF16)
        acc0_t, acc0_b = lsb("a1_acc0", [128, T], BF16)

        for go in range(G):
            wtile, wbuf = wst[go % 2]
            kb.dma("pool", kb.stream(f"wq{go % 2}"),
                   wtile[:].rearrange("p (gi m) -> p gi m", m=128),
                   dr["a_Wv"][go].rearrange("gi p m -> p gi m"),
                   writes=[wbuf])
            pbank = [(pp, pp_b), (pp2, pp2_b)]
            for tb in range(NTB):
                pt, pb = pbank[tb % 2]
                sl = slice(tb * TBW, (tb + 1) * TBW)
                proj_mms(pt[:], pb, wtile, wbuf, xb_t, xb_b, sl)
                kb.op("dve", lambda e, tb=tb, pt=pt: e.tensor_copy(
                    cpre_t[:, 4 + tb * TBW : 4 + (tb + 1) * TBW], pt[:]),
                    reads=[pb], writes=[cpre_b], accum=True)
            kb.op("dve", lambda e: e.memset(cpre_t[:, 0:4], 0.0),
                  writes=[cpre_b], accum=True)
            ctile, cbuf = cvw[go]
            kb.op("dve", lambda e, ctile=ctile: e.tensor_scalar(
                acc0_t[:], cpre_t[:, 4 : 4 + T], ctile[:, 3:4], None,
                ALU.mult), reads=[cpre_b, cbuf], writes=[acc0_b])
            accs = [(acc0_t, acc0_b), (v_t[go], v_b[go])]
            for tau in range(1, 4):
                src_t, src_b2 = accs[(tau + 1) % 2]
                dst_t, dst_b2 = accs[tau % 2]
                kb.op("dve", lambda e, tau=tau, ctile=ctile, src_t=src_t,
                      dst_t=dst_t: e.scalar_tensor_tensor(
                          dst_t[:], cpre_t[:, 4 - tau : 4 - tau + T],
                          ctile[:, 3 - tau : 4 - tau], src_t[:],
                          ALU.mult, ALU.add),
                      reads=[cpre_b, cbuf, src_b2], writes=[dst_b2])
            kb.op("act", lambda e, go=go: e.activation(
                v_t[go][:], v_t[go][:], AF.Silu),
                reads=[v_b[go]], writes=[v_b[go]], accum=True)

        for tc in range(NCH):
            for gi in range(G):
                wtile, wbuf = wb_tiles[gi]
                kb.op("pe", lambda e, gi=gi, tc=tc, wtile=wtile: e.matmul(
                    pbeta[:, 0:4], xb_t[gi][:, tc * 128 : (tc + 1) * 128],
                    wtile[:], start=(gi == 0), stop=(gi == G - 1)),
                    reads=[wbuf, xb_b[gi]], writes=[pbeta_b], accum=(gi > 0))
            kb.op("act", lambda e, tc=tc: e.activation(
                beta_t[:, 4 * tc : 4 * tc + 4], pbeta[:, 0:4], AF.Sigmoid),
                reads=[pbeta_b], writes=[beta_b], accum=True)

        for g in range(G):
            kb.dma("sp", xs, xsp[g], x_t[g][:], reads=[x_b[g]])
        kb.tap("v", [D, T], [(v_t[g][:], v_b[g],
                              (lambda t, g=g: t[g * 128:(g + 1) * 128]))
                             for g in range(G)])
        kb.tap("beta", [128, 4 * NCH], [(beta_t[:], beta_b, lambda t: t)])
        kb.barrier()

    with contextlib.suppress(_Skip), contextlib.ExitStack() as sS:
        if PHASE < 2:
            raise _Skip()
        lsb, lps = SB(sS), PS(sS)
        ws = kb.stream("wstr")
        wq2 = [lsb("s_wq0", [128, D], BF16), lsb("s_wq1", [128, D], BF16)]
        cqw = []
        ckw = []
        for g in range(G):
            t, b = lsb(f"cqw{g}", [128, 4], F32)
            kb.dma("sp", ws, t[:], dr["a_conv_q_w"][g], writes=[b])
            cqw.append((t, b))
            t, b = lsb(f"ckw{g}", [128, 4], F32)
            kb.dma("sp", ws, t[:], dr["a_conv_k_w"][g], writes=[b])
            ckw.append((t, b))
        qg_t = []
        qg_b = []
        kg_t = []
        kg_b = []
        cprq = []
        cprk = []
        for g in range(G):
            t, b = lsb(f"qg{g}", [128, TBW], BF16)
            qg_t.append(t)
            qg_b.append(b)
            t, b = lsb(f"kg{g}", [128, TBW], BF16)
            kg_t.append(t)
            kg_b.append(b)
            cprq.append(lsb(f"cprq{g}", [128, TBW + 4], BF16))
            cprk.append(lsb(f"cprk{g}", [128, TBW + 4], BF16))
        S_t = []
        S_b = []
        for h in range(H):
            for c2 in range(2):
                t, b = lsb(f"S{h}_{c2}", [128, DH], F32)
                S_t.append(t)
                S_b.append(b)
        Sbf_t, Sbf_b = lsb("Sbf", [128, 2 * DH], BF16)
        ktm3 = [lsb(f"ktm{i}", [128, DH], BF16) for i in range(3)]
        kbm3 = [lsb(f"kbm{i}", [128, DH], BF16) for i in range(3)]
        vcm3 = [lsb(f"vcm{i}", [128, DH], BF16) for i in range(3)]
        rq3 = [lsb(f"rqc{i}", [128, 2], F32) for i in range(3)]
        at3 = [lsb(f"at{i}", [128, 128], BF16) for i in range(3)]
        vtm_t, vtm_b = lsb("vtm", [128, DH], BF16)
        uib_t, uib_b = lsb("uib", [128, DH], BF16)
        nw_t, nw_b = lsb("nw", [128, DH], BF16)
        otmp_t, otmp_b = lsb("otmp", [128, DH], BF16)
        y02 = [lsb(f"y0{i}", [128, 128], F32) for i in range(2)]
        cn2 = [[lsb(f"cn{i}_{j}", [128, 128], F32) for j in range(2)]
               for i in range(2)]
        cp2 = [[lsb(f"cp{i}_{j}", [128, 128], F32) for j in range(2)]
               for i in range(2)]
        rsb_t, rsb_b = lsb("rsb", [128, 128], F32)
        rsbbf2 = [lsb(f"rsbbf{i}", [128, 128], BF16) for i in range(2)]
        sq_t, sq_b = lsb("sqs", [128, 256], BF16)
        bkr_t, bkr_b = lsb("bkr", [128, 1], F32)
        pAG, pAG_b = lps("pAG", [128, 512], F32)
        ptm, ptm_b = lps("ptm", [128, 512], BF16)
        psq, psq_b = lps("psq", [128, 128], F32)
        pR, pR_b = lps("pR", [128, 128], F32)
        pUO, pUO_b = lps("pUO", [128, DH], F32)
        pW, pW_b = lps("pW", [128, DH], F32)
        pOT, pOT_b = lps("pOT", [128, 256], BF16)
        pS, pS_b = lps("pS", [128, 512], F32)
        pproj, pproj_b = pS, pS_b

        for h in range(H):
            for c2 in range(2):
                kb.op("dve", lambda e, i=2 * h + c2: e.memset(S_t[i][:], 0.0),
                      writes=[S_b[2 * h + c2]])

        def conv_group(gq, which, wst_idx, cpr, cw, out_t, out_b, wnm, cnmb):
            sl = slice(gq * TBW, (gq + 1) * TBW)
            for go in range(G):
                wtile, wbuf = wq2[(wst_idx + go) % 2]
                kb.dma("pool", kb.stream(f"wq{(wst_idx + go) % 2}"),
                       wtile[:].rearrange("p (gi m) -> p gi m", m=128),
                       dr[wnm][go].rearrange("gi p m -> p gi m"),
                       writes=[wbuf])
                proj_mms(pproj[:], pproj_b, wtile, wbuf, xb_t, xb_b, sl)
                ct, cb = cpr[go]
                if gq == 0:
                    kb.op("dve", lambda e, ct=ct: e.memset(ct[:, 0:4], 0.0),
                          writes=[cb], accum=True)
                kb.op("dve", lambda e, ct=ct: e.tensor_copy(
                    ct[:, 4 : 4 + TBW], pproj[:]),
                    reads=[pproj_b], writes=[cb], accum=True)
                wtile2, cbuf2 = cw[go]
                kb.op("dve", lambda e, ct=ct, w=wtile2: e.tensor_scalar(
                    acc2_t[:], ct[:, 4 : 4 + TBW], w[:, 3:4], None, ALU.mult),
                    reads=[cb, cbuf2], writes=[acc2_b])
                accs = [(acc2_t, acc2_b), (out_t[go], out_b[go])]
                for tau in range(1, 4):
                    src_t, src_b2 = accs[(tau + 1) % 2]
                    dst_t, dst_b2 = accs[tau % 2]
                    kb.op("dve", lambda e, tau=tau, ct=ct, w=wtile2,
                          src_t=src_t, dst_t=dst_t: e.scalar_tensor_tensor(
                              dst_t[:], ct[:, 4 - tau : 4 - tau + TBW],
                              w[:, 3 - tau : 4 - tau], src_t[:],
                              ALU.mult, ALU.add),
                          reads=[cb, cbuf2, src_b2], writes=[dst_b2])
                kb.op("act", lambda e, o=out_t[go]: e.activation(
                    o[:], o[:], AF.Silu),
                    reads=[out_b[go]], writes=[out_b[go]], accum=True)
                kb.op("dve", lambda e, ct=ct: e.tensor_copy(
                    ct[:, 0:4], ct[:, TBW : TBW + 4]),
                    reads=[cb], writes=[cb], accum=True)

        acc2_t, acc2_b = lsb("s_acc2", [128, TBW], BF16)

        qf_ap = kf_ap = None
        if "qf" in DBG:
            qf_ap = nc.dram_tensor("dbg_qf", [D, T], F32,
                                   kind="ExternalOutput").ap()
            kb.taps["qf"] = qf_ap
        if "kf" in DBG:
            kf_ap = nc.dram_tensor("dbg_kf", [D, T], F32,
                                   kind="ExternalOutput").ap()
            kb.taps["kf"] = kf_ap

        def pre_gen(i, c, h):
            s3, s2 = i % 3, i % 2
            co = (c % 4) * 128
            csl = slice(co, co + 128)
            ga, gb = 2 * h, 2 * h + 1
            bcol = beta_t[:, 4 * c + h : 4 * c + h + 1]
            ktm_t, ktm_b = ktm3[s3]
            kbm_t, kbm_b = kbm3[s3]
            vcm_t, vcm_b = vcm3[s3]
            rq_t, rq_b = rq3[s3]
            at_t, at_b = at3[s3]
            y0_t, y0_b = y02[s2]
            cn, cp = cn2[s2], cp2[s2]
            for idx, (src_t, src_b) in enumerate(
                    [(qg_t, qg_b), (kg_t, kg_b)]):
                kb.op("act", lambda e, s=src_t[ga]: e.activation(
                    sq_t[:, 0:128], s[:, csl], AF.Square),
                    reads=[src_b[ga]], writes=[sq_b])
                yield
                kb.op("act", lambda e, s=src_t[gb]: e.activation(
                    sq_t[:, 128:256], s[:, csl], AF.Square),
                    reads=[src_b[gb]], writes=[sq_b], accum=True)
                yield
                kb.op("pe", lambda e, idx=idx: e.matmul(
                    pAG[:, 256 + idx : 257 + idx], sq_t[:, 0:128],
                    ones_col[:], start=True, stop=False),
                    reads=[sq_b, ones_col_b], writes=[pAG_b], accum=True)
                yield
                kb.op("pe", lambda e, idx=idx: e.matmul(
                    pAG[:, 256 + idx : 257 + idx], sq_t[:, 128:256],
                    ones_col[:], start=False, stop=True),
                    reads=[sq_b, ones_col_b], writes=[pAG_b], accum=True)
                yield
            kb.op("dve", lambda e, rq_t=rq_t: e.tensor_scalar(
                rq_t[:], pAG[:, 256:258], 1e-12, None, ALU.add),
                reads=[pAG_b], writes=[rq_b])
            yield
            kb.op("act", lambda e, rq_t=rq_t: e.activation(
                rq_t[:], rq_t[:], AF.Sqrt),
                reads=[rq_b], writes=[rq_b], accum=True)
            yield
            kb.op("dve", lambda e, rq_t=rq_t: e.reciprocal(
                rq_t[:], rq_t[:]),
                reads=[rq_b], writes=[rq_b], accum=True)
            yield
            kb.op("pe", lambda e: e.transpose(
                ptm[:, 0:128], kg_t[ga][:, csl], eye_bf[:]),
                reads=[kg_b[ga], eye_bf_b], writes=[ptm_b])
            yield
            kb.op("pe", lambda e: e.transpose(
                ptm[:, 128:256], kg_t[gb][:, csl], eye_bf[:]),
                reads=[kg_b[gb], eye_bf_b], writes=[ptm_b], accum=True)
            yield
            kb.op("pe", lambda e: e.transpose(
                ptm[:, 256:384], v_t[ga][:, c * 128 : c * 128 + 128],
                eye_bf[:]),
                reads=[v_b[ga], eye_bf_b], writes=[ptm_b], accum=True)
            yield
            kb.op("pe", lambda e: e.transpose(
                ptm[:, 384:512], v_t[gb][:, c * 128 : c * 128 + 128],
                eye_bf[:]),
                reads=[v_b[gb], eye_bf_b], writes=[ptm_b], accum=True)
            yield
            kb.op("dve", lambda e, ktm_t=ktm_t: e.tensor_copy(
                ktm_t[:], ptm[:, 0:256]),
                reads=[ptm_b], writes=[ktm_b])
            yield
            kb.op("dve", lambda e: e.tensor_copy(
                vtm_t[:], ptm[:, 256:512]),
                reads=[ptm_b], writes=[vtm_b])
            yield
            kb.op("dve", lambda e, bcol=bcol, rq_t=rq_t:
                  e.scalar_tensor_tensor(
                      bkr_t[:], rq_t[:, 1:2], -1.0, bcol, ALU.mult,
                      ALU.mult),
                  reads=[rq_b, beta_b], writes=[bkr_b])
            yield
            kb.op("dve", lambda e, kbm_t=kbm_t, ktm_t=ktm_t: e.tensor_scalar(
                kbm_t[:], ktm_t[:], bkr_t[:], None, ALU.mult),
                reads=[ktm_b], sreads=[bkr_b], writes=[kbm_b])
            yield
            kb.op("dve", lambda e, bcol=bcol, vcm_t=vcm_t: e.tensor_scalar(
                vcm_t[:], vtm_t[:], bcol, None, ALU.mult),
                reads=[vtm_b], sreads=[beta_b], writes=[vcm_b])
            yield
            kb.op("pe", lambda e: e.matmul(
                pAG[:, 0:128], kg_t[ga][:, csl], kg_t[ga][:, csl],
                start=True, stop=False),
                reads=[kg_b[ga]], writes=[pAG_b], accum=True)
            yield
            kb.op("pe", lambda e: e.matmul(
                pAG[:, 0:128], kg_t[gb][:, csl], kg_t[gb][:, csl],
                start=False, stop=True),
                reads=[kg_b[gb]], writes=[pAG_b], accum=True)
            yield
            kb.op("dve", lambda e, y0_t=y0_t: e.tensor_scalar(
                y0_t[:], pAG[:, 0:128], bkr_t[:], None, ALU.mult),
                reads=[pAG_b], sreads=[bkr_b], writes=[y0_b])
            yield
            kb.op("dve", lambda e, y0_t=y0_t: e.tensor_tensor(
                y0_t[:], y0_t[:], sl_mask[:], ALU.mult),
                reads=[y0_b, sl_mask_b], writes=[y0_b], accum=True)
            yield
            kb.op("pe", lambda e, y0_t=y0_t: e.transpose(
                pAG[:, 320:448], y0_t[:], eye_f[:]),
                reads=[y0_b, eye_f_b], writes=[pAG_b], accum=True)
            yield
            kb.op("dve", lambda e, cp=cp, rq_t=rq_t: e.tensor_scalar(
                cp[0][0][:], pAG[:, 320:448], rq_t[:, 1:2], None,
                ALU.mult),
                reads=[pAG_b], sreads=[rq_b], writes=[cp[0][1]])
            yield
            kb.op("pe", lambda e, cp=cp: e.transpose(
                pAG[:, 320:448], cp[0][0][:], eye_f[:]),
                reads=[cp[0][1], eye_f_b], writes=[pAG_b], accum=True)
            yield
            kb.op("dve", lambda e, cn=cn: e.tensor_copy(
                cn[0][0][:], pAG[:, 320:448]),
                reads=[pAG_b], writes=[cn[0][1]])
            yield
            kb.op("pe", lambda e: e.matmul(
                pAG[:, 128:256], kg_t[ga][:, csl], qg_t[ga][:, csl],
                start=True, stop=False),
                reads=[kg_b[ga], qg_b[ga]], writes=[pAG_b], accum=True)
            yield
            kb.op("pe", lambda e: e.matmul(
                pAG[:, 128:256], kg_t[gb][:, csl], qg_t[gb][:, csl],
                start=False, stop=True),
                reads=[kg_b[gb], qg_b[gb]], writes=[pAG_b], accum=True)
            yield
            kb.op("dve", lambda e, at_t=at_t: e.tensor_tensor(
                at_t[:], pAG[:, 128:256], ui_mask[:], ALU.mult),
                reads=[pAG_b, ui_mask_b], writes=[at_b])
            yield

        def chain_gen(i, c, h):
            s2 = i % 2
            cn, cp = cn2[s2], cp2[s2]
            rsbbf_t, rsbbf_b = rsbbf2[s2]
            kb.op("dve", lambda e, cp=cp: e.tensor_tensor(
                rsb_t[:], cp[0][0][:], eye_f[:], ALU.add),
                reads=[cp[0][1], eye_f_b], writes=[rsb_b])
            yield
            kb.op("pe", lambda e: e.matmul(
                pR[:], eye_f[:], rsb_t[:], start=True, stop=False),
                reads=[eye_f_b, rsb_b], writes=[pR_b])
            yield
            cur = 0
            for lev in range(1, 7):
                nxt = 1 - cur
                kb.op("pe", lambda e, cur=cur, cp=cp, cn=cn: e.matmul(
                    psq[:], cp[cur][0][:], cn[cur][0][:],
                    start=True, stop=True),
                    reads=[cp[cur][1], cn[cur][1]], writes=[psq_b])
                yield
                kb.op("dve", lambda e, nxt=nxt, cn=cn: e.tensor_copy(
                    cn[nxt][0][:], psq[:]),
                    reads=[psq_b], writes=[cn[nxt][1]])
                yield
                if lev < 6:
                    kb.op("pe", lambda e, cur=cur, cp=cp, cn=cn: e.matmul(
                        psq[:], cn[cur][0][:], cp[cur][0][:],
                        start=True, stop=True),
                        reads=[cn[cur][1], cp[cur][1]], writes=[psq_b])
                    yield
                    kb.op("dve", lambda e, nxt=nxt, cp=cp: e.tensor_copy(
                        cp[nxt][0][:], psq[:]),
                        reads=[psq_b], writes=[cp[nxt][1]])
                    yield
                cur = nxt
                kb.op("pe", lambda e, cur=cur, cn=cn, last=(lev == 6):
                      e.matmul(pR[:], cn[cur][0][:], rsb_t[:],
                               start=False, stop=last),
                      reads=[cn[cur][1], rsb_b], writes=[pR_b], accum=True)
                yield
                kb.op("dve", lambda e: e.tensor_copy(rsb_t[:], pR[:]),
                      reads=[pR_b], writes=[rsb_b])
                yield
            kb.op("dve", lambda e, rsbbf_t=rsbbf_t: e.tensor_copy(
                rsbbf_t[:], rsb_t[:]),
                reads=[rsb_b], writes=[rsbbf_b])
            yield

        def post_gen(i, c, h):
            s3, s2 = i % 3, i % 2
            ga, gb = 2 * h, 2 * h + 1
            co = (c % 4) * 128
            csl = slice(co, co + 128)
            ktm_t, ktm_b = ktm3[s3]
            kbm_t, kbm_b = kbm3[s3]
            vcm_t, vcm_b = vcm3[s3]
            rq_t, rq_b = rq3[s3]
            at_t, at_b = at3[s3]
            rsbbf_t, rsbbf_b = rsbbf2[s2]
            first = (c == 0)
            if not first:
                kb.op("dve", lambda e, h=h: e.tensor_copy(
                    Sbf_t[:, 0:DH], S_t[2 * h][:]),
                    reads=[S_b[2 * h]], writes=[Sbf_b])
                yield
                kb.op("dve", lambda e, h=h: e.tensor_copy(
                    Sbf_t[:, DH : 2 * DH], S_t[2 * h + 1][:]),
                    reads=[S_b[2 * h + 1]], writes=[Sbf_b], accum=True)
                yield
            kb.op("pe", lambda e, first=first, rsbbf_t=rsbbf_t,
                  vcm_t=vcm_t: e.matmul(
                      pUO[:], rsbbf_t[:], vcm_t[:], start=True, stop=first),
                  reads=[rsbbf_b, vcm_b], writes=[pUO_b])
            yield
            if not first:
                kb.op("pe", lambda e, kbm_t=kbm_t, rsbbf_t=rsbbf_t:
                      e.matmul(pW[:, 0:128], kbm_t[:, 0:128], rsbbf_t[:],
                               start=True, stop=True),
                      reads=[kbm_b, rsbbf_b], writes=[pW_b])
                yield
                kb.op("pe", lambda e, kbm_t=kbm_t, rsbbf_t=rsbbf_t:
                      e.matmul(pW[:, 128:256], kbm_t[:, 128:256],
                               rsbbf_t[:], start=True, stop=True),
                      reads=[kbm_b, rsbbf_b], writes=[pW_b], accum=True)
                yield
                kb.op("dve", lambda e: e.tensor_copy(nw_t[:], pW[:]),
                      reads=[pW_b], writes=[nw_b])
                yield
                kb.op("pe", lambda e: e.matmul(
                    pUO[:], nw_t[:, 0:128], Sbf_t[:, 0:DH],
                    start=False, stop=False),
                    reads=[nw_b, Sbf_b], writes=[pUO_b], accum=True)
                yield
                kb.op("pe", lambda e: e.matmul(
                    pUO[:], nw_t[:, 128:256], Sbf_t[:, DH : 2 * DH],
                    start=False, stop=True),
                    reads=[nw_b, Sbf_b], writes=[pUO_b], accum=True)
                yield
            kb.op("dve", lambda e, rq_t=rq_t: e.tensor_scalar(
                uib_t[:], pUO[:], rq_t[:, 1:2], None, ALU.mult),
                reads=[pUO_b], sreads=[rq_b], writes=[uib_b])
            yield
            if not first:
                kb.op("pe", lambda e: e.matmul(
                    pUO[:], qg_t[ga][:, csl], Sbf_t[:, 0:DH],
                    start=True, stop=False),
                    reads=[qg_b[ga], Sbf_b], writes=[pUO_b])
                yield
                kb.op("pe", lambda e: e.matmul(
                    pUO[:], qg_t[gb][:, csl], Sbf_t[:, DH : 2 * DH],
                    start=False, stop=False),
                    reads=[qg_b[gb], Sbf_b], writes=[pUO_b], accum=True)
                yield
            kb.op("pe", lambda e, first=first, at_t=at_t: e.matmul(
                pUO[:], at_t[:], uib_t[:], start=first, stop=True),
                reads=[at_b, uib_b], writes=[pUO_b], accum=(not first))
            yield
            kb.op("dve", lambda e, rq_t=rq_t: e.tensor_scalar(
                otmp_t[:], pUO[:], rq_t[:, 0:1], None, ALU.mult),
                reads=[pUO_b], sreads=[rq_b], writes=[otmp_b])
            yield
            kb.op("pe", lambda e: e.transpose(
                pOT[:, 0:128], otmp_t[:, 0:128], eye_bf[:]),
                reads=[otmp_b, eye_bf_b], writes=[pOT_b])
            yield
            kb.op("pe", lambda e: e.transpose(
                pOT[:, 128:256], otmp_t[:, 128:256], eye_bf[:]),
                reads=[otmp_b, eye_bf_b], writes=[pOT_b], accum=True)
            yield
            kb.op("dve", lambda e, c=c: e.tensor_copy(
                dl_t[ga][:, c * 128 : c * 128 + 128], pOT[:, 0:128]),
                reads=[pOT_b], writes=[dl_b[ga]], accum=True)
            yield
            kb.op("dve", lambda e, c=c: e.tensor_copy(
                dl_t[gb][:, c * 128 : c * 128 + 128], pOT[:, 128:256]),
                reads=[pOT_b], writes=[dl_b[gb]], accum=True)
            yield
            if c < NCH - 1:
                kb.op("pe", lambda e, ktm_t=ktm_t: e.matmul(
                    pS[:, 0:256], ktm_t[:, 0:128], uib_t[:],
                    start=True, stop=True),
                    reads=[ktm_b, uib_b], writes=[pS_b])
                yield
                kb.op("pe", lambda e, ktm_t=ktm_t: e.matmul(
                    pS[:, 256:512], ktm_t[:, 128:256], uib_t[:],
                    start=False, stop=True),
                    reads=[ktm_b, uib_b], writes=[pS_b], accum=True)
                yield
                kb.op("dve", lambda e, h=h: e.tensor_tensor(
                    S_t[2 * h][:], S_t[2 * h][:], pS[:, 0:256], ALU.add),
                    reads=[S_b[2 * h], pS_b], writes=[S_b[2 * h]],
                    accum=True)
                yield
                kb.op("dve", lambda e, h=h: e.tensor_tensor(
                    S_t[2 * h + 1][:], S_t[2 * h + 1][:], pS[:, 256:512],
                    ALU.add),
                    reads=[S_b[2 * h + 1], pS_b], writes=[S_b[2 * h + 1]],
                    accum=True)
                yield

        gidx = 0
        for gq in range(4):
            conv_group(gq, "q", 0, cprq, cqw, qg_t, qg_b, "a_Wq", None)
            conv_group(gq, "k", G, cprk, ckw, kg_t, kg_b, "a_Wk", None)
            for go in range(G):
                if qf_ap is not None:
                    kb.dma("pool", kb.tap_stream,
                           qf_ap[go * 128 : (go + 1) * 128,
                                 gq * TBW : (gq + 1) * TBW],
                           qg_t[go][:], reads=[qg_b[go]])
                if kf_ap is not None:
                    kb.dma("pool", kb.tap_stream,
                           kf_ap[go * 128 : (go + 1) * 128,
                                 gq * TBW : (gq + 1) * TBW],
                           kg_t[go][:], reads=[kg_b[go]])
            its = [(gq * 4 + cc, h) for cc in range(4) for h in range(H)]
            M = len(its)
            for step in range(M + 2):
                batch = []
                if step < M:
                    batch.append(pre_gen(gidx + step, *its[step]))
                if 0 <= step - 1 < M:
                    batch.append(chain_gen(gidx + step - 1, *its[step - 1]))
                if 0 <= step - 2 < M:
                    batch.append(post_gen(gidx + step - 2, *its[step - 2]))
                while batch:
                    for g2 in list(batch):
                        try:
                            next(g2)
                        except StopIteration:
                            batch.remove(g2)
            gidx += M
        kb.tap("delta", [D, T], [(dl_t[g][:], dl_b[g],
                                  (lambda t, g=g: t[g * 128:(g + 1) * 128]))
                                 for g in range(G)])
        kb.tap("Sfin", [8 * 128, DH],
               [(S_t[i][:], S_b[i],
                 (lambda t, i=i: t[i * 128:(i + 1) * 128])) for i in range(8)])
        kb.barrier()

    with contextlib.suppress(_Skip), contextlib.ExitStack() as sB:
        if PHASE < 3:
            raise _Skip()
        lsb = SB(sB)
        ws = kb.stream("wstr")
        lo_t, lo_b, mi_t, mi_b = [], [], [], []
        for g in range(G):
            t, b = lsb(f"lo{g}", [128, T], BF16)
            lo_t.append(t)
            lo_b.append(b)
            t, b = lsb(f"mi{g}", [128, T], BF16)
            mi_t.append(t)
            mi_b.append(b)
        selb_t, selb_b = lsb("selb", [128, 14 * 132], BF16)
        kb.dma("pool", ws, selb_t[:], dr["a_selbank"], writes=[selb_b])
        psel_t, psel_b = [], []
        for s in range(16):
            t, b = lsb(f"psel{s}", [16, 128], BF16)
            kb.dma("pool", ws, t[:], dr["a_psel"][s], writes=[b])
            psel_t.append(t)
            psel_b.append(b)
        mixsel_t, mixsel_b, onsel_t, onsel_b = [], [], [], []
        hsel_t, hsel_b = [], []
        for g in range(G):
            t, b = lsb(f"mxs{g}", [4, 128], BF16)
            kb.dma("pool", ws, t[:], dr["a_mixsel"][g], writes=[b])
            mixsel_t.append(t)
            mixsel_b.append(b)
            t, b = lsb(f"ons{g}", [4, 128], BF16)
            kb.dma("pool", ws, t[:], dr["a_onsel"][g], writes=[b])
            onsel_t.append(t)
            onsel_b.append(b)
            t, b = lsb(f"hsl{g}", [128, 4], BF16)
            kb.dma("pool", ws, t[:], dr["a_hsel"][g], writes=[b])
            hsel_t.append(t)
            hsel_b.append(b)
        rb1_t, rb1_b = lsb("rb1", [128, 17], F32)
        kb.dma("sp", ws, rb1_t[:], dr["a_rb1"], writes=[rb1_b])
        rb2_t, rb2_b = lsb("rb2", [16, 1], F32)
        kb.dma("sp", ws, rb2_t[:], dr["a_rb2"], writes=[rb2_b])
        rw2_t, rw2_b = lsb("rw2", [128, 17 * 16], BF16)
        kb.dma("pool", ws, rw2_t[:].rearrange("p (m n) -> p m n", n=16),
               dr["a_rw2"].rearrange("m p n -> p m n"), writes=[rw2_b])
        clw, cmw = [], []
        for g in range(G):
            t, b = lsb(f"clw{g}", [128, 7], F32)
            kb.dma("sp", ws, t[:], dr["a_local_w"][g], writes=[b])
            clw.append((t, b))
            t, b = lsb(f"cmw{g}", [128, 25], F32)
            kb.dma("sp", ws, t[:], dr["a_mid_w"][g], writes=[b])
            cmw.append((t, b))
        feats_t, feats_b = lsb("feats", [128, T], BF16)
        sh32_t, sh32_b = lsb("sh32", [64, 64], BF16)
        kb.dma("pool", ws, sh32_t[:], dr["c_shift32"], writes=[sh32_b])
        lgbf_t, lgbf_b = lsb("lgbf", [16, T], BF16)
        pfm_t, pfm_b = lsb("pfm", [16, T], BF16)

        for g in range(G):
            for (out_t2, out_b2, cw2, K2) in ((lo_t[g], lo_b[g], clw[g], 7),
                                              (mi_t[g], mi_b[g], cmw[g], 25)):
                ct2, cb2 = cw2
                kb.op("dve", lambda e, g=g, ct2=ct2, o=out_t2, K2=K2:
                      e.tensor_scalar(o[:], v_t[g][:],
                                      ct2[:, K2 - 1 : K2], None, ALU.mult),
                      reads=[v_b[g], cb2], writes=[out_b2])
                for tau in range(1, K2):
                    kb.op("dve", lambda e, g=g, tau=tau, ct2=ct2, o=out_t2,
                          K2=K2: e.scalar_tensor_tensor(
                              o[:, tau:T], v_t[g][:, 0 : T - tau],
                              ct2[:, K2 - 1 - tau : K2 - tau], o[:, tau:T],
                              ALU.mult, ALU.add),
                          reads=[v_b[g], cb2, out_b2], writes=[out_b2],
                          accum=True)
        kb.tap("local", [D, T], [(lo_t[g][:], lo_b[g],
                                  (lambda t, g=g: t[g * 128:(g + 1) * 128]))
                                 for g in range(G)])
        kb.tap("mid", [D, T], [(mi_t[g][:], mi_b[g],
                                (lambda t, g=g: t[g * 128:(g + 1) * 128]))
                               for g in range(G)])

        branches = [(lo_t, lo_b), (mi_t, mi_b), (dl_t, dl_b), (v_t, v_b)]

        with contextlib.suppress(_Skip), contextlib.ExitStack() as sBs:
            if BSTAGE < 1:
                raise _Skip()
            lps1, lsb1 = PS(sBs), SB(sBs)
            pF, pF_b = lps1("pF", [128, TBW], F32)
            psh, psh_b = lps1("psh", [64, TBW], F32)
            prods = [lsb1(f"prd{i}", [128, TBW], BF16) for i in range(3)]
            msq_t, msq_b = lsb1("msq", [64, TBW], BF16)
            fcp_t, fcp_b = lsb1("fcp", [64, TBW], F32)
            kb.op("dve", lambda e: e.memset(feats_t[64:128, :], 0.0),
                  writes=[feats_b], accum=True)
            srows = _stat_rows()
            for tb in range(NTB):
                sl = slice(tb * TBW, (tb + 1) * TBW)
                nst = 0
                pi = 0
                for si, (kind, ops, r0, coef) in enumerate(srows):
                    for g in range(G):
                        h = g // 2
                        selap = selb_t[:, 132 * si + 3 - h :
                                       132 * si + 131 - h]
                        if kind == "mean":
                            bt, bb = branches[ops[0]]
                            rt_ap = bt[g][:, sl]
                            rdeps = [bb[g]]
                        else:
                            pt2, pb2 = prods[pi % 3]
                            pi += 1
                            if kind == "sq":
                                bt, bb = branches[ops[0]]
                                kb.op("act", lambda e, bt=bt, g=g, sl=sl,
                                      pt2=pt2, bb=bb: e.activation(
                                          pt2[:], bt[g][:, sl], AF.Square),
                                      reads=[bb[g]], writes=[pb2])
                            else:
                                at2, ab2 = branches[ops[0]]
                                bt2, bb2 = branches[ops[1]]
                                kb.op("dve", lambda e, at2=at2, bt2=bt2,
                                      g=g, sl=sl, pt2=pt2, ab2=ab2, bb2=bb2:
                                      e.tensor_tensor(
                                          pt2[:], at2[g][:, sl],
                                          bt2[g][:, sl], ALU.mult),
                                      reads=[ab2[g], bb2[g]], writes=[pb2])
                            rt_ap = pt2[:]
                            rdeps = [pb2]
                        kb.op("pe", lambda e, selap=selap, rt_ap=rt_ap,
                              nst=nst: e.matmul(
                                  pF[:], selap, rt_ap,
                                  start=(nst == 0), stop=(nst == 111)),
                              reads=[selb_b] + rdeps, writes=[pF_b],
                              accum=(nst > 0))
                        nst += 1
                kb.op("dve", lambda e: e.tensor_copy(fcp_t[:], pF[0:64, :]),
                      reads=[pF_b], writes=[fcp_b])
                kb.op("act", lambda e: e.activation(
                    msq_t[:], fcp_t[:], AF.Square,
                    scale=float(np.sqrt(DH / (DH - 1.0)))),
                    reads=[fcp_b], writes=[msq_b])
                kb.op("pe", lambda e: e.matmul(
                    psh[:], sh32_t[:], msq_t[:], start=True, stop=True),
                    reads=[sh32_b, msq_b], writes=[psh_b])
                kb.op("dve", lambda e, sl=sl: e.tensor_tensor(
                    feats_t[0:64, sl], fcp_t[:], psh[:], ALU.subtract),
                    reads=[fcp_b, psh_b], writes=[feats_b], accum=True)
            kb.tap("feats", [128, T], [(feats_t[:], feats_b, lambda t: t)])
            kb.barrier()

        with contextlib.suppress(_Skip), contextlib.ExitStack() as sBr:
            if BSTAGE < 2:
                raise _Skip()
            lps2, lsb2 = PS(sBr), SB(sBr)
            rw1s_t, rw1s_b = lsb2("rw1s", [128, 9 * 512], BF16)
            h1bf_t, h1bf_b = lsb2("h1bf", [128, TBW], BF16)
            pH = [lps2(f"pH{i}", [128, TBW], F32) for i in range(4)]
            pL = [lps2(f"pL{i}", [16, TBW], F32) for i in range(NTB)]
            rhs_all = xb_t + [feats_t]
            rhsb_all = xb_b + [feats_b]
            for mg in range(5):
                kb.dma("pool", kb.stream("rw1"),
                       rw1s_t[:].rearrange("p (kg n) -> p kg n", n=512),
                       dr["a_rw1"][mg].rearrange("kg p n -> p kg n"),
                       writes=[rw1s_b])
                nm_ch = 4 if mg < 4 else 1
                for tb in range(NTB):
                    sl = slice(tb * TBW, (tb + 1) * TBW)
                    for mi in range(nm_ch):
                        m = 4 * mg + mi
                        pt, pb = pH[mi]
                        for kg in range(9):
                            kb.op("pe", lambda e, kg=kg, mi=mi, pt=pt, sl=sl:
                                  e.matmul(
                                      pt[:],
                                      rw1s_t[:, kg * 512 + mi * 128 :
                                             kg * 512 + (mi + 1) * 128],
                                      rhs_all[kg][:, sl],
                                      start=(kg == 0), stop=(kg == 8)),
                                  reads=[rw1s_b, rhsb_all[kg]], writes=[pb],
                                  accum=(kg > 0))
                        kb.op("act", lambda e, m=m, pt=pt: e.activation(
                            h1bf_t[:], pt[:], AF.Silu,
                            bias=rb1_t[:, m : m + 1]),
                            reads=[pb, rb1_b], writes=[h1bf_b])
                        kb.op("pe", lambda e, m=m, tb=tb: e.matmul(
                            pL[tb][0][:], rw2_t[:, m * 16 : (m + 1) * 16],
                            h1bf_t[:], start=(m == 0), stop=(m == 16)),
                            reads=[rw2_b, h1bf_b], writes=[pL[tb][1]],
                            accum=(m > 0))
            for tb in range(NTB):
                sl = slice(tb * TBW, (tb + 1) * TBW)
                kb.op("dve", lambda e, tb=tb, sl=sl: e.tensor_scalar(
                    lgbf_t[:, sl], pL[tb][0][:], rb2_t[:], None, ALU.add),
                    reads=[pL[tb][1], rb2_b], writes=[lgbf_b], accum=True)
            kb.tap("logits", [16, T], [(lgbf_t[:], lgbf_b, lambda t: t)])
            kb.barrier()

        with contextlib.suppress(_Skip), contextlib.ExitStack() as sBm:
            if BSTAGE < 3:
                raise _Skip()
            lps3, lsb3 = PS(sBm), SB(sBm)
            ptt, ptt_b = lps3("ptt", [128, 512], BF16)
            pbc, pbc_b = lps3("pbc", [128, TBW], F32)
            prms, prms_b = lps3("prms", [4, 128], F32)
            pgn, pgn_b = lps3("pgn", [128, 128], F32)
            pwo, pwo_b = lps3("pwo", [128, TBW], F32)
            pwo2, pwo2_b = lps3("pwo2", [128, TBW], F32)
            ptm_sm, ptm_sm_b = lsb3("ptm_sm", [128, 16], F32)
            exp_sm, exp_sm_b = lsb3("exp_sm", [128, 16], F32)
            col_sm, col_sm_b = lsb3("col_sm", [128, 4], F32)
            pback_t, pback_b = lsb3("pback", [128, 16], BF16)
            for tc in range(NCH):
                kb.op("pe", lambda e, tc=tc: e.transpose(
                    ptt[:, 0:16],
                    lgbf_t[:, tc * 128 : (tc + 1) * 128],
                    eye_bf[0:16, 0:16]),
                    reads=[lgbf_b, eye_bf_b], writes=[ptt_b])
                kb.op("dve", lambda e: e.tensor_copy(
                    ptm_sm[:], ptt[:, 0:16]),
                    reads=[ptt_b], writes=[ptm_sm_b])
                for h in range(H):
                    hs = slice(4 * h, 4 * h + 4)
                    kb.op("dve", lambda e, hs=hs, h=h: e.tensor_reduce(
                        col_sm[:, h : h + 1], ptm_sm[:, hs], AX.X, ALU.max,
                        negate=True),
                        reads=[ptm_sm_b], writes=[col_sm_b], accum=True)
                for h in range(H):
                    hs = slice(4 * h, 4 * h + 4)
                    kb.op("act", lambda e, hs=hs, h=h: e.activation(
                        exp_sm[:, hs], ptm_sm[:, hs], AF.Exp,
                        bias=col_sm[:, h : h + 1]),
                        reads=[ptm_sm_b, col_sm_b], writes=[exp_sm_b],
                        accum=True)
                for h in range(H):
                    hs = slice(4 * h, 4 * h + 4)
                    kb.op("dve", lambda e, hs=hs, h=h: e.tensor_reduce(
                        col_sm[:, h : h + 1], exp_sm[:, hs], AX.X, ALU.add),
                        reads=[exp_sm_b], writes=[col_sm_b], accum=True)
                kb.op("dve", lambda e: e.reciprocal(col_sm[:], col_sm[:]),
                      reads=[col_sm_b], writes=[col_sm_b], accum=True)
                for h in range(H):
                    hs = slice(4 * h, 4 * h + 4)
                    kb.op("dve", lambda e, hs=hs, h=h: e.tensor_scalar(
                        exp_sm[:, hs], exp_sm[:, hs], col_sm[:, h : h + 1],
                        1.0 - 4 * FLOOR, ALU.mult, ALU.mult),
                        reads=[exp_sm_b], sreads=[col_sm_b],
                        writes=[exp_sm_b], accum=True)
                kb.op("dve", lambda e: e.tensor_scalar(
                    pback_t[:], exp_sm[:], FLOOR, None, ALU.add),
                    reads=[exp_sm_b], writes=[pback_b])
                if tc == 0 and "smdbg" in DBG:
                    smd = nc.dram_tensor("dbg_smdbg", [128, 52], F32,
                                         kind="ExternalOutput").ap()
                    kb.taps["smdbg"] = smd
                    kb.dma("pool", kb.tap_stream, smd[:, 0:16], ptm_sm[:],
                           reads=[ptm_sm_b])
                    kb.dma("pool", kb.tap_stream, smd[:, 16:32], exp_sm[:],
                           reads=[exp_sm_b])
                    kb.dma("pool", kb.tap_stream, smd[:, 32:36], col_sm[:],
                           reads=[col_sm_b])
                    kb.dma("pool", kb.tap_stream, smd[:, 36:52], pback_t[:],
                           reads=[pback_b])
                kb.op("pe", lambda e: e.transpose(
                    ptt[0:16, 128:256], pback_t[:], eye_bf[:]),
                    reads=[pback_b, eye_bf_b], writes=[ptt_b], accum=True)
                kb.op("dve", lambda e, tc=tc: e.tensor_copy(
                    pfm_t[:, tc * 128 : (tc + 1) * 128],
                    ptt[0:16, 128:256]),
                    reads=[ptt_b], writes=[pfm_b], accum=True)
            kb.tap("p", [16, T], [(pfm_t[:], pfm_b, lambda t: t)])

            pbcsb_t, pbcsb_b = lsb3("pbcsb", [128, TBW], BF16)
            mxt, mxt_b = lsb3("mxt", [128, TBW], BF16)
            sqm_t, sqm_b = lsb3("sqm", [128, 128], BF16)
            rmsbf_t, rmsbf_b = lsb3("rmsbf", [4, 128], BF16)
            rmsrow_t, rmsrow_b = lsb3("rmsrow", [4, 128], F32)
            mix_t = xb_t
            mix_b = xb_b
            for tb in range(NTB):
                sl = slice(tb * TBW, (tb + 1) * TBW)
                for h in range(H):
                    for i in range(4):
                        slot = 4 * h + i
                        kb.op("pe", lambda e, slot=slot, sl=sl: e.matmul(
                            pbc[:], psel_t[slot][:], pfm_t[:, sl],
                            start=True, stop=True),
                            reads=[psel_b[slot], pfm_b], writes=[pbc_b])
                        kb.op("dve", lambda e: e.tensor_copy(
                            pbcsb_t[:], pbc[:]),
                            reads=[pbc_b], writes=[pbcsb_b])
                        for g in (2 * h, 2 * h + 1):
                            bt, bb = branches[i]
                            if i == 0:
                                kb.op("dve", lambda e, g=g, bt=bt, sl=sl,
                                      bb=bb: e.tensor_tensor(
                                          mix_t[g][:, 0:TBW], bt[g][:, sl],
                                          pbcsb_t[:], ALU.mult),
                                      reads=[bb[g], pbcsb_b],
                                      writes=[mix_b[g]])
                            else:
                                kb.op("dve", lambda e, g=g, bt=bt, sl=sl,
                                      bb=bb: e.tensor_tensor(
                                          mxt[:], bt[g][:, sl], pbcsb_t[:],
                                          ALU.mult),
                                      reads=[bb[g], pbcsb_b], writes=[mxt_b])
                                kb.op("dve", lambda e, g=g: e.tensor_tensor(
                                    mix_t[g][:, 0:TBW], mix_t[g][:, 0:TBW],
                                    mxt[:], ALU.add),
                                    reads=[mix_b[g], mxt_b],
                                    writes=[mix_b[g]], accum=True)
                for stage in range(2):
                    selset = mixsel_t if stage == 0 else onsel_t
                    selsetb = mixsel_b if stage == 0 else onsel_b
                    for sc in range(4):
                        scl = slice(sc * 128, (sc + 1) * 128)
                        for g in range(G):
                            kb.op("act", lambda e, g=g, scl=scl:
                                  e.activation(sqm_t[:], mix_t[g][:, scl],
                                               AF.Square),
                                  reads=[mix_b[g]], writes=[sqm_b])
                            kb.op("pe", lambda e, g=g: e.matmul(
                                prms[:], hsel_t[g][:], sqm_t[:],
                                start=(g == 0), stop=(g == G - 1)),
                                reads=[hsel_b[g], sqm_b], writes=[prms_b],
                                accum=(g > 0))
                        kb.op("dve", lambda e: e.tensor_scalar(
                            rmsrow_t[:], prms[:], 1.0 / DH, None, ALU.mult),
                            reads=[prms_b], writes=[rmsrow_b])
                        kb.op("act", lambda e: e.activation(
                            rmsrow_t[:], rmsrow_t[:], AF.Sqrt, bias=1e-5),
                            reads=[rmsrow_b, constf(1e-5)],
                            writes=[rmsrow_b], accum=True)
                        kb.op("dve", lambda e: e.reciprocal(
                            rmsrow_t[:], rmsrow_t[:]),
                            reads=[rmsrow_b], writes=[rmsrow_b], accum=True)
                        kb.op("dve", lambda e: e.tensor_copy(
                            rmsbf_t[:], rmsrow_t[:]),
                            reads=[rmsrow_b], writes=[rmsbf_b])
                        for g in range(G):
                            kb.op("pe", lambda e, g=g, selset=selset:
                                  e.matmul(pgn[:], selset[g][:], rmsbf_t[:],
                                           start=True, stop=True),
                                  reads=[selsetb[g], rmsbf_b],
                                  writes=[pgn_b])
                            if stage == 0:
                                kb.op("dve", lambda e, g=g, scl=scl:
                                      e.tensor_tensor(
                                          mix_t[g][:, scl],
                                          mix_t[g][:, scl], pgn[:],
                                          ALU.mult),
                                      reads=[mix_b[g], pgn_b],
                                      writes=[mix_b[g]], accum=True)
                            else:
                                osl = slice(tb * TBW + sc * 128,
                                            tb * TBW + (sc + 1) * 128)
                                kb.op("dve", lambda e, g=g, scl=scl, osl=osl:
                                      e.tensor_tensor(
                                          dl_t[g][:, osl],
                                          mix_t[g][:, scl], pgn[:],
                                          ALU.mult),
                                      reads=[mix_b[g], pgn_b],
                                      writes=[dl_b[g]], accum=True)
            kb.tap("o_n", [D, T], [(dl_t[g][:], dl_b[g],
                                    (lambda t, g=g:
                                     t[g * 128:(g + 1) * 128]))
                                   for g in range(G)])

            with contextlib.suppress(_Skip), contextlib.ExitStack() as sWo:
                if BSTAGE < 4:
                    raise _Skip()
                lsbW = SB(sWo)
                wos = [lsbW("wo0", [128, D], BF16),
                       lsbW("wo1", [128, D], BF16)]
                for go in range(G):
                    wtile, wbuf = wos[go % 2]
                    kb.dma("pool", kb.stream(f"wq{go % 2}"),
                           wtile[:].rearrange("p (gi m) -> p gi m", m=128),
                           dr["a_Wo"][go].rearrange("gi p m -> p gi m"),
                           writes=[wbuf])
                    for tb in range(NTB):
                        pt, pb = (pwo, pwo_b) if tb % 2 == 0 else \
                            (pwo2, pwo2_b)
                        sl = slice(tb * TBW, (tb + 1) * TBW)
                        proj_mms(pt[:], pb, wtile, wbuf, dl_t, dl_b, sl)
                        kb.op("act", lambda e, go=go, sl=sl, pt=pt:
                              e.activation(xb_t[go][:, sl], pt[:],
                                           AF.Identity),
                              reads=[pb], writes=[xb_b[go]], accum=True)
                for g in range(G):
                    kb.dma("sp", kb.stream("x_io"), aout_sp[g], xb_t[g][:],
                           reads=[xb_b[g]])
                kb.tap("aout", [D, T], [(xb_t[g][:], xb_b[g],
                                         (lambda t, g=g:
                                          t[g * 128:(g + 1) * 128]))
                                        for g in range(G)])
            kb.barrier()
        kb.barrier()

    s1.close()

    if PHASE < 4:
        with contextlib.ExitStack() as sO:
            lsb4 = SB(sO)
            dum_t, dum_b = lsb4("dum", [128, D], F32)
            kb.op("dve", lambda e: e.memset(dum_t[:], 0.0), writes=[dum_b])
            outs = kb.stream("out_store")
            for tc in range(NCH):
                kb.dma("sp", outs, out_ap[tc * 128 : (tc + 1) * 128, :],
                       dum_t[:], reads=[dum_b])
        kb.finish()
        st.close()
        return kb

    sR = st.enter_context(contextlib.ExitStack())
    sbR = SB(sR)
    x2_t, x2_b, xb2_t, xb2_b = [], [], [], []
    for g in range(G):
        t, b = sbR(f"x2_{g}", [128, T], F32)
        x2_t.append(t)
        x2_b.append(b)
    for g in range(G):
        t, b = sbR(f"xb2_{g}", [128, T], BF16)
        xb2_t.append(t)
        xb2_b.append(b)
    xio = kb.stream("x_io")
    for g in range(G):
        kb.dma("sp", xio, x2_t[g][:], xsp[g], writes=[x2_b[g]])
        kb.dma("sp", xio, xb2_t[g][:], aout_sp[g], writes=[xb2_b[g]])
    for g in range(G):
        kb.op("dve", lambda e, g=g: e.tensor_tensor(
            x2_t[g][:], x2_t[g][:], xb2_t[g][:], ALU.add),
            reads=[x2_b[g], xb2_b[g]], writes=[x2_b[g]], accum=True)
    kb.tap("x1", [D, T], [(x2_t[g][:], x2_b[g],
                           (lambda t, g=g: t[g * 128:(g + 1) * 128]))
                          for g in range(G)])

    def ffn_phase(tag):
        with contextlib.ExitStack() as sF:
            lsb, lps = SB(sF), PS(sF)
            layernorm(tag, x2_t, x2_b, xb2_t, xb2_b, sF)
            ws = kb.stream("wstr")
            a1r1 = []
            for g in range(G):
                t, b = lsb(f"f{tag}a1r1_{g}", [128, 27], BF16)
                kb.dma("pool", ws, t[:], dr["f_a1r1"][g], writes=[b])
                a1r1.append((t, b))
            b2c_t, b2c_b = lsb(f"f{tag}b2c", [128, D], BF16)
            kb.dma("pool", ws, b2c_t[:], dr["f_b2cat"], writes=[b2c_b])
            b2_t, b2_b = lsb(f"f{tag}b2", [128, G], F32)
            kb.dma("sp", ws, b2_t[:], dr["f_b2"], writes=[b2_b])
            w1h_t, w1h_b = lsb(f"f{tag}w1h", [128, 9 * 2048], BF16)
            w2h_t, w2h_b = lsb(f"f{tag}w2h", [128, 16 * 1024], BF16)
            a2r2_t, a2r2_b = lsb(f"f{tag}a2r2", [128, 2 * 16 * 27], BF16)
            b1_t, b1_b = lsb(f"f{tag}b1", [128, 32], F32)
            kb.dma("sp", ws, b1_t[:], dr["f_b1"], writes=[b1_b])
            kb.dma("pool", ws, a2r2_t[:].rearrange("p (h m n) -> p h m n", m=16, n=27),
                   dr["f_a2r2"].rearrange("h m p n -> p h m n"),
                   writes=[a2r2_b])
            zgfm_t, zgfm_b = lsb(f"f{tag}zgfm", [128, T], BF16)
            zg2fm_t, zg2fm_b = lsb(f"f{tag}zg2fm", [128, 512], BF16)
            z1sb_t, z1sb_b = lsb(f"f{tag}z1sb", [128, 27], F32)
            gt_t, gt_b = lsb(f"f{tag}gt", [128, 4], F32)
            zgtm_t, zgtm_b = lsb(f"f{tag}zgtm", [128, 24], BF16)
            z2sb = [lsb(f"f{tag}z2_{i}", [128, 108], F32) for i in range(4)]
            hsb = [lsb(f"f{tag}hsb{m}", [128, 512], BF16) for m in range(16)]
            pZ, pZ_b = lps(f"f{tag}pZ", [128, 108], F32)
            pH1, pH1_b = lps(f"f{tag}pH1", [128, 512], F32)
            pH2, pH2_b = lps(f"f{tag}pH2", [128, 512], F32)
            pTZ, pTZ_b = lps(f"f{tag}pTZ", [128, 128], BF16)
            pOg = [lps(f"f{tag}pO{i}", [128, 512], F32) for i in range(2)]

            kb.op("dve", lambda e: e.memset(zgfm_t[:], 0.0), writes=[zgfm_b])
            kb.op("dve", lambda e: e.memset(zg2fm_t[:], 0.0),
                  writes=[zg2fm_b])

            def gates_zg(z_ap, z_deps, dst_ap, dst_b):
                kb.op("dve", lambda e: e.tensor_copy(z1sb_t[:], z_ap),
                      reads=z_deps, writes=[z1sb_b])
                kb.op("dve", lambda e: e.tensor_reduce(
                    gt_t[:, 3:4], z1sb_t[:, 24:27], AX.X, ALU.max,
                    negate=True),
                    reads=[z1sb_b], writes=[gt_b])
                kb.op("act", lambda e: e.activation(
                    gt_t[:, 0:3], z1sb_t[:, 24:27], AF.Exp,
                    bias=gt_t[:, 3:4]),
                    reads=[z1sb_b, gt_b], writes=[gt_b], accum=True)
                kb.op("dve", lambda e: e.tensor_reduce(
                    gt_t[:, 3:4], gt_t[:, 0:3], AX.X, ALU.add),
                    reads=[gt_b], writes=[gt_b], accum=True)
                kb.op("dve", lambda e: e.reciprocal(
                    gt_t[:, 3:4], gt_t[:, 3:4]),
                    reads=[gt_b], writes=[gt_b], accum=True)
                for ei in range(3):
                    kb.op("dve", lambda e, ei=ei: e.tensor_scalar(
                        zgtm_t[:, 8 * ei : 8 * ei + 8],
                        z1sb_t[:, 8 * ei : 8 * ei + 8],
                        gt_t[:, ei : ei + 1], gt_t[:, 3:4],
                        ALU.mult, ALU.mult),
                        reads=[z1sb_b], sreads=[gt_b], writes=[zgtm_b],
                        accum=True)
                kb.op("dve", lambda e: e.tensor_scalar(
                    zgtm_t[:], zgtm_t[:], 0.125, None, ALU.mult),
                    reads=[zgtm_b], writes=[zgtm_b], accum=True)
                kb.op("pe", lambda e: e.transpose(
                    pTZ[0:24, 0:128], zgtm_t[:], eye_bf[:]),
                    reads=[zgtm_b, eye_bf_b], writes=[pTZ_b])
                kb.op("act", lambda e: e.activation(
                    dst_ap, pTZ[0:24, 0:128], AF.Identity),
                    reads=[pTZ_b], writes=[dst_b], accum=True)

            for tc in range(NCH):
                sl1 = slice(tc * 128, (tc + 1) * 128)
                for gi in range(G):
                    t2, b2cc = a1r1[gi]
                    kb.op("pe", lambda e, gi=gi, sl1=sl1, t2=t2: e.matmul(
                        pZ[:, 0:27], xb2_t[gi][:, sl1], t2[:],
                        start=(gi == 0), stop=(gi == G - 1)),
                        reads=[b2cc, xb2_b[gi]], writes=[pZ_b],
                        accum=(gi > 0))
                gates_zg(pZ[:, 0:27], [pZ_b],
                         zgfm_t[0:24, tc * 128 : (tc + 1) * 128], zgfm_b)

            kb.tap(f"zgfm{tag}", [128, T],
                   [(zgfm_t[:], zgfm_b, lambda t: t)])
            rhs9 = xb2_t + [zgfm_t]
            rhs9b = xb2_b + [zgfm_b]
            for half in range(2):
                kb.dma("pool", kb.stream("fw1"),
                       w1h_t[:].rearrange("p (kg n) -> p kg n", n=2048),
                       dr["f_w1"][half].rearrange("kg p n -> p kg n"),
                       writes=[w1h_b])
                kb.dma("pool", kb.stream("fw2"),
                       w2h_t[:].rearrange("p (m n) -> p m n", n=1024),
                       dr["f_w2"][half].rearrange("m p n -> p m n"),
                       writes=[w2h_b])
                for tb in range(4):
                    sl = slice(tb * 512, (tb + 1) * 512)
                    z2t, z2b = z2sb[tb]
                    for m in range(16):
                        pt, pb = (pH1, pH1_b) if m % 2 == 0 else (pH2, pH2_b)
                        ht, hb = hsb[m]
                        for kg in range(9):
                            kb.op("pe", lambda e, kg=kg, m=m, sl=sl, pt=pt:
                                  e.matmul(
                                      pt[:],
                                      w1h_t[:, kg * 2048 + m * 128 :
                                            kg * 2048 + (m + 1) * 128],
                                      rhs9[kg][:, sl],
                                      start=(kg == 0), stop=(kg == 8)),
                                  reads=[w1h_b, rhs9b[kg]], writes=[pb],
                                  accum=(kg > 0))
                        kb.op("act", lambda e, half=half, m=m, pt=pt, ht=ht:
                              e.activation(
                                  ht[:], pt[:], AF.Relu,
                                  bias=b1_t[:, 16 * half + m :
                                            16 * half + m + 1]),
                              reads=[pb, b1_b], writes=[hb])
                        for ts2 in range(4):
                            kb.op("pe", lambda e, ts2=ts2, half=half, m=m,
                                  ht=ht: e.matmul(
                                      pZ[:, 27 * ts2 : 27 * ts2 + 27],
                                      ht[:, 128 * ts2 : 128 * ts2 + 128],
                                      a2r2_t[:, (16 * half + m) * 27 :
                                             (16 * half + m + 1) * 27],
                                      start=(m == 0 and ts2 == 0),
                                      stop=(m == 15 and ts2 == 3)),
                                  reads=[hb, a2r2_b], writes=[pZ_b],
                                  accum=(not (m == 0 and ts2 == 0)))
                    if half == 0:
                        kb.op("dve", lambda e, z2t=z2t: e.tensor_copy(
                            z2t[:], pZ[:]), reads=[pZ_b], writes=[z2b])
                    else:
                        kb.op("dve", lambda e, z2t=z2t: e.tensor_tensor(
                            z2t[:], z2t[:], pZ[:], ALU.add),
                            reads=[z2b, pZ_b], writes=[z2b], accum=True)
                        for ts2 in range(4):
                            gates_zg(z2t[:, 27 * ts2 : 27 * ts2 + 27], [z2b],
                                     zg2fm_t[0:24, 128 * ts2 :
                                             128 * ts2 + 128], zg2fm_b)
                    for go in range(G):
                        pt2, pb2 = pOg[go % 2]
                        for m in range(16):
                            kb.op("pe", lambda e, go=go, m=m, pt2=pt2,
                                  half=half: e.matmul(
                                      pt2[:],
                                      w2h_t[:, m * 1024 + go * 128 :
                                            m * 1024 + (go + 1) * 128],
                                      hsb[m][0][:],
                                      start=(m == 0),
                                      stop=(half == 0 and m == 15)),
                                  reads=[w2h_b, hsb[m][1]],
                                  writes=[pb2], accum=(m > 0))
                        if half == 0:
                            kb.op("dve", lambda e, go=go, sl=sl, pt2=pt2:
                                  e.tensor_tensor(
                                      x2_t[go][:, sl], x2_t[go][:, sl],
                                      pt2[:], ALU.add),
                                  reads=[pb2, x2_b[go]], writes=[x2_b[go]],
                                  accum=True)
                        else:
                            kb.op("pe", lambda e, go=go, pt2=pt2: e.matmul(
                                      pt2[:],
                                      b2c_t[:, go * 128 : (go + 1) * 128],
                                      zg2fm_t[:],
                                      start=False, stop=True),
                                  reads=[b2c_b, zg2fm_b], writes=[pb2],
                                  accum=True)
                            kb.op("dve", lambda e, go=go, sl=sl, pt2=pt2:
                                  e.scalar_tensor_tensor(
                                      x2_t[go][:, sl], pt2[:],
                                      b2_t[:, go : go + 1],
                                      x2_t[go][:, sl], ALU.add, ALU.add),
                                  reads=[pb2, b2_b, x2_b[go]],
                                  writes=[x2_b[go]], accum=True)
            kb.tap(f"x_ffn{tag}", [D, T],
                   [(x2_t[g][:], x2_b[g],
                     (lambda t, g=g: t[g * 128:(g + 1) * 128]))
                    for g in range(G)])
            kb.barrier()

    if PHASE >= 4:
        ffn_phase("2")

    def attnb_phase():
        qb_t, qb_b, kbh_t, kbh_b = [], [], [], []
        with contextlib.ExitStack() as sD:
            lsb, lps = SB(sD), PS(sD)
            layernorm("3", x2_t, x2_b, xb2_t, xb2_b, sD)
            ws = kb.stream("wstr")
            for g in range(G):
                t, b = lsb(f"bq{g}", [128, T], BF16)
                qb_t.append(t)
                qb_b.append(b)
            for g in range(4):
                t, b = lsb(f"bk{g}", [128, T], BF16)
                kbh_t.append(t)
                kbh_b.append(b)
            vtm_t, vtm_b = [], []
            for kvh in range(KVH):
                t, b = lsb(f"bvt{kvh}", [128, NCH * 65], BF16)
                vtm_t.append(t)
                vtm_b.append(b)
                kb.op("dve", lambda e, t=t: e.memset(t[:], 1.0), writes=[b])
            wst = [lsb("bwst0", [128, D], BF16), lsb("bwst1", [128, D], BF16)]
            vfs_t, vfs_b = lsb("bvfs", [128, TBW], BF16)
            sqb_t, sqb_b = lsb("bsqb", [128, TBW], BF16)
            nrm_t, nrm_b = lsb("bnrm", [128, 32], F32)
            nbb_t, nbb_b = lsb("bnbb", [128, 16], F32)
            wcol_t, wcol_b = lsb("bwcol", [128, 2], F32)
            mw_t, mw_b = lsb("bmw", [1, 1], F32)
            mwb_t, mwb_b = lsb("bmwb", [1, 1], BF16)
            ones2_t, ones2_b = lsb("bones2", [128, 2], BF16)
            e8_t, e8_b = lsb("be8", [32, 16], BF16)
            kb.dma("sp", ws, mw_t[:], dr["b_mw"], writes=[mw_b])
            kb.dma("pool", ws, ones2_t[:], dr["c_ones2"], writes=[ones2_b])
            kb.dma("pool", ws, e8_t[:], dr["b_e8"], writes=[e8_b])

            sD1 = contextlib.ExitStack()
            lps1 = PS(sD1)
            pprj, pprj_b = lps1("bpprj", [128, TBW], F32)
            pprj2, pprj2_b = lps1("bpprj2", [128, TBW], F32)
            ptv, ptv_b = lps1("bptv", [128, 1024], BF16)
            pnrm, pnrm_b = lps1("bpnrm", [128, 32], F32)

            def stream_proj(wnm, ngo, outfn):
                for go in range(ngo):
                    wtile, wbuf = wst[go % 2]
                    kb.dma("pool", ws, wtile[:].rearrange("p (gi m) -> p gi m", m=128),
                           dr[wnm][go].rearrange("gi p m -> p gi m"),
                           writes=[wbuf])
                    for tb in range(NTB):
                        pt, pb = (pprj, pprj_b) if tb % 2 == 0 else \
                            (pprj2, pprj2_b)
                        sl = slice(tb * TBW, (tb + 1) * TBW)
                        proj_mms(pt[:], pb, wtile, wbuf, xb2_t, xb2_b, sl)
                        outfn(go, tb, sl, pt, pb)

            stream_proj("b_Wq", G, lambda go, tb, sl, pt, pb: kb.op(
                "act", lambda e, go=go, sl=sl, pt=pt: e.activation(
                    qb_t[go][:, sl], pt[:], AF.Identity),
                reads=[pb], writes=[qb_b[go]], accum=True))
            stream_proj("b_Wk", 4, lambda go, tb, sl, pt, pb: kb.op(
                "act", lambda e, go=go, sl=sl, pt=pt: e.activation(
                    kbh_t[go][:, sl], pt[:], AF.Identity),
                reads=[pb], writes=[kbh_b[go]], accum=True))

            def v_out(go, tb, sl, pt, pb):
                kb.op("act", lambda e, pt=pt: e.activation(
                    vfs_t[:], pt[:], AF.Identity),
                    reads=[pb], writes=[vfs_b])
                for j in range(4):
                    tc = tb * 4 + j
                    kb.op("pe", lambda e, j=j: e.transpose(
                        ptv[:, j * 256 : j * 256 + 128],
                        vfs_t[:, j * 128 : (j + 1) * 128], eye_bf[:]),
                        reads=[vfs_b, eye_bf_b], writes=[ptv_b], accum=True)
                    for half in range(2):
                        kvh = 2 * go + half
                        kb.op("dve", lambda e, j=j, half=half, kvh=kvh,
                              tc=tc: e.tensor_copy(
                                  vtm_t[kvh][:, tc * 65 : tc * 65 + 64],
                                  ptv[:, j * 256 + half * 64 :
                                      j * 256 + half * 64 + 64]),
                              reads=[ptv_b], writes=[vtm_b[kvh]],
                              accum=True)
            stream_proj("b_Wv", 4, v_out)

            kb.op("dve", lambda e: e.memset(nrm_t[:], 0.0), writes=[nrm_b])
            for src_t, src_b, ncols, base in ((qb_t, qb_b, G, 0),
                                              (kbh_t, kbh_b, 4, 16)):
                for g in range(ncols):
                    for tb in range(NTB):
                        sl = slice(tb * TBW, (tb + 1) * TBW)
                        kb.op("act", lambda e, g=g, sl=sl, src_t=src_t:
                              e.activation(sqb_t[:], src_t[g][:, sl],
                                           AF.Square),
                              reads=[src_b[g]], writes=[sqb_b])
                        for j in range(4):
                            kb.op("pe", lambda e, j=j, g=g, base=base:
                                  e.matmul(
                                      pnrm[:, base + 2 * g :
                                           base + 2 * g + 2],
                                      sqb_t[:, j * 128 : (j + 1) * 128],
                                      ones2_t[:],
                                      start=True, stop=True),
                                  reads=[sqb_b, ones2_b], writes=[pnrm_b])
                            kb.op("dve", lambda e, g=g, base=base:
                                  e.tensor_tensor(
                                      nrm_t[:, base + 2 * g :
                                            base + 2 * g + 2],
                                      nrm_t[:, base + 2 * g :
                                            base + 2 * g + 2],
                                      pnrm[:, base + 2 * g :
                                           base + 2 * g + 2], ALU.max),
                                  reads=[nrm_b, pnrm_b], writes=[nrm_b],
                                  accum=True)
            kb.op("dve", lambda e: e.tensor_copy(sqb_t[:, 0:32], nrm_t[:]),
                  reads=[nrm_b], writes=[sqb_b])
            kb.op("pe", lambda e: e.transpose(
                ptv[0:32, 0:128], sqb_t[:, 0:32], eye_bf[:]),
                reads=[sqb_b, eye_bf_b], writes=[ptv_b])
            kb.op("dve", lambda e: e.tensor_reduce(
                nrm_t[0:32, 0:1], ptv[0:32, 0:128], AX.X, ALU.max),
                reads=[ptv_b], writes=[nrm_b])
            kb.op("act", lambda e: e.activation(
                nrm_t[0:32, 0:1], nrm_t[0:32, 0:1], AF.Sqrt),
                reads=[nrm_b], writes=[nrm_b], accum=True)
            kb.op("dve", lambda e: e.tensor_copy(
                sqb_t[0:32, 0:1], nrm_t[0:32, 0:1]),
                reads=[nrm_b], writes=[sqb_b])
            kb.op("pe", lambda e: e.matmul(
                pnrm[0:16, 2:3], e8_t[:], sqb_t[0:32, 0:1],
                start=True, stop=True),
                reads=[e8_b, sqb_b], writes=[pnrm_b])
            kb.op("dve", lambda e: e.tensor_tensor(
                nrm_t[0:16, 1:2], nrm_t[0:16, 0:1], pnrm[0:16, 2:3],
                ALU.mult),
                reads=[nrm_b, pnrm_b], writes=[nrm_b], accum=True)
            kb.op("dve", lambda e: e.tensor_scalar(
                nrm_t[0:16, 2:3], nrm_t[0:16, 1:2], -0.125, -1.0,
                ALU.mult, ALU.add),
                reads=[nrm_b], writes=[nrm_b], accum=True)
            kb.op("dve", lambda e: e.tensor_copy(
                sqb_t[0:16, 0:1], nrm_t[0:16, 2:3]),
                reads=[nrm_b], writes=[sqb_b])
            kb.op("pe", lambda e: e.transpose(
                ptv[0:1, 0:16], sqb_t[0:16, 0:1], eye_bf[0:16, 0:16]),
                reads=[sqb_b, eye_bf_b], writes=[ptv_b], accum=True)
            kb.op("dve", lambda e: e.tensor_copy(
                sqb_t[0:1, 0:16], ptv[0:1, 0:16]),
                reads=[ptv_b], writes=[sqb_b])
            kb.op("pe", lambda e: e.matmul(
                pnrm[:, 16:32], ones_row[:], sqb_t[0:1, 0:16],
                start=True, stop=True),
                reads=[ones_row_b, sqb_b], writes=[pnrm_b])
            kb.op("dve", lambda e: e.tensor_copy(nbb_t[:], pnrm[:, 16:32]),
                  reads=[pnrm_b], writes=[nbb_b])
            kb.op("act", lambda e: e.activation(
                mwb_t[:], mw_t[:], AF.Sigmoid), reads=[mw_b], writes=[mwb_b])
            kb.op("pe", lambda e: e.matmul(
                pnrm[:, 0:1], ones_row[:], mwb_t[:], start=True, stop=True),
                reads=[ones_row_b, mwb_b], writes=[pnrm_b])
            kb.op("dve", lambda e: e.tensor_copy(
                wcol_t[:, 0:1], pnrm[:, 0:1]),
                reads=[pnrm_b], writes=[wcol_b])
            kb.op("dve", lambda e: e.tensor_scalar(
                wcol_t[:, 1:2], wcol_t[:, 0:1], -1.0, 1.0,
                ALU.mult, ALU.add),
                reads=[wcol_b], writes=[wcol_b], accum=True)
            kb.tap("bounds", [128, 16], [(nbb_t[:], nbb_b, lambda t: t)])
            kb.barrier()
            sD1.close()

            with contextlib.ExitStack() as sD2:
                lsb2, lps2 = SB(sD2), PS(sD2)
                mem_tiles = []
                for kc in range(4):
                    t, b = lsb2(f"bmem{kc}", [128, KV], BF16)
                    kb.dma("pool", ws, t[:], dr["b_mem"][kc], writes=[b])
                    mem_tiles.append((t, b))
                memn_tiles = []
                for kc in range(4):
                    t, b = lsb2(f"bmemn{kc}", [128, 1], BF16)
                    kb.dma("pool", ws, t[:], dr["b_memn"][kc], writes=[b])
                    memn_tiles.append((t, b))
                el1_t, el1_b = lsb2("bel1", [128, T], BF16)
                numt_t, numt_b = lsb2("bnum", [128, NCH * KV], BF16)
                rden_t, rden_b = lsb2("brden", [128, NCH], F32)
                etile_t, etile_b = lsb2("betile", [128, T], BF16)
                asb_t, asb_b = lsb2("basb", [128, NCH * 128], F32)
                rdn_t, rdn_b = lsb2("brdn", [128, NCH], F32)
                t1_t, t1_b = lsb2("bt1", [128, DHB], BF16)
                t2_t, t2_b = lsb2("bt2", [128, DHB], BF16)
                psc = [lps2("bpsc0", [128, TBW], F32)]
                padot, padot_b = lps2("bpadot", [128, NCH * 128], F32)
                pnum, pnum_b = lps2("bpnum", [128, KV], F32)
                pden, pden_b = lps2("bpden", [128, NCH], F32)
                ptrx, ptrx_b = lps2("bptrx", [128, 128], BF16)

                qloc = {}
                for j in range(G):
                    qloc[QA[j]] = (j, 0)
                    qloc[QB[j]] = (j, 64)

                sigma_t = xb2_t[:4]
                sigma_b = xb2_b[:4]
                for nq in range(2):
                    for kc in range(4):
                        for half in range(2):
                            o = 8 * nq + 2 * kc + half
                            j, base = qloc[o]
                            qsl = qb_t[j][base : base + 64, :]
                            e1 = el1_t[base : base + 64, :]
                            dsl = sigma_t[kc][64 * half : 64 * half + 64, :]
                            kb.op("dve", lambda e, qsl=qsl, e1=e1:
                                  e.tensor_scalar(
                                      e1, qsl, 0.0, None, ALU.min),
                                  reads=[qb_b[j]], writes=[el1_b])
                            kb.op("act", lambda e, e1=e1: e.activation(
                                e1, e1, AF.Exp),
                                reads=[el1_b], writes=[el1_b], accum=True)
                            kb.op("dve", lambda e, qsl=qsl, e1=e1:
                                  e.scalar_tensor_tensor(
                                      e1, qsl, 0.0, e1, ALU.max, ALU.add),
                                  reads=[qb_b[j], el1_b], writes=[el1_b],
                                  accum=True)
                            kb.op("dve", lambda e, e1=e1, dsl=dsl:
                                  e.tensor_scalar(
                                      dsl, e1, -1.0, None, ALU.add),
                                  reads=[el1_b], writes=[sigma_b[kc]],
                                  accum=True)
                    for tc in range(NCH):
                        tsl = slice(tc * 128, (tc + 1) * 128)
                        for kc in range(4):
                            mt, mb = mem_tiles[kc]
                            kb.op("pe", lambda e, kc=kc, tsl=tsl, mt=mt:
                                  e.matmul(pnum[:], sigma_t[kc][:, tsl],
                                           mt[:], start=(kc == 0),
                                           stop=(kc == 3)),
                                  reads=[sigma_b[kc], mb], writes=[pnum_b],
                                  accum=(kc > 0))
                            nt, nb = memn_tiles[kc]
                            kb.op("pe", lambda e, kc=kc, tc=tc, tsl=tsl,
                                  nt=nt: e.matmul(
                                      pden[:, tc : tc + 1],
                                      sigma_t[kc][:, tsl], nt[:],
                                      start=(kc == 0), stop=(kc == 3)),
                                  reads=[sigma_b[kc], nb], writes=[pden_b],
                                  accum=(kc > 0 or tc > 0))
                        kb.op("dve", lambda e, tc=tc: e.tensor_copy(
                            numt_t[:, tc * KV : (tc + 1) * KV], pnum[:]),
                            reads=[pnum_b], writes=[numt_b], accum=True)
                    kb.op("dve", lambda e: e.reciprocal(
                        rden_t[:], pden[:]),
                        reads=[pden_b], writes=[rden_b])
                    for qh8 in range(8):
                        o = 8 * nq + qh8
                        kvh = o // 2
                        j, base = qloc[o]
                        slot = 2 * j + (0 if base == 0 else 1)
                        for tc in range(NCH):
                            tsl = slice(tc * 128, (tc + 1) * 128)
                            for sb2 in range(4):
                                ssl = slice(sb2 * TBW, (sb2 + 1) * TBW)
                                pt, pb = psc[0]
                                kb.op("pe", lambda e, tsl=tsl, ssl=ssl,
                                      j=j, base=base, kvh=kvh, pt=pt:
                                      e.matmul(
                                          pt[:],
                                          kbh_t[kvh // 2][
                                              (kvh % 2) * 64 :
                                              (kvh % 2) * 64 + 64, tsl],
                                          qb_t[j][base : base + 64, ssl],
                                          start=True, stop=True),
                                      reads=[kbh_b[kvh // 2], qb_b[j]],
                                      writes=[pb])
                                kb.op("act", lambda e, ssl=ssl, pt=pt,
                                      slot=slot: e.activation(
                                          etile_t[:, ssl], pt[:], AF.Exp,
                                          bias=nbb_t[:, slot : slot + 1],
                                          scale=0.125),
                                      reads=[pb, nbb_b], writes=[etile_b],
                                      accum=True)
                            for sc in range(NCH):
                                first = (tc == 0 and sc % 4 == 0)
                                kb.op("pe", lambda e, sc=sc, tc=tc, kvh=kvh,
                                      first=first: e.matmul(
                                          padot[:, sc * 128 : sc * 128 + 65],
                                          etile_t[:, sc * 128 :
                                                  (sc + 1) * 128],
                                          vtm_t[kvh][:, tc * 65 :
                                                     (tc + 1) * 65],
                                          start=first,
                                          stop=(tc == NCH - 1)),
                                      reads=[etile_b, vtm_b[kvh]],
                                      writes=[padot_b], accum=(not first))
                        kb.op("dve", lambda e: e.tensor_copy(
                            asb_t[:], padot[:]),
                            reads=[padot_b], writes=[asb_b])
                        kb.op("dve", lambda e: e.reciprocal(
                            rdn_t[:],
                            asb_t[:].rearrange("p (s c) -> p s c", c=128)
                            [:, :, 64]),
                            reads=[asb_b], writes=[rdn_b])
                        for sc in range(NCH):
                            kb.op("dve", lambda e, sc=sc: e.tensor_scalar(
                                t1_t[:], asb_t[:, sc * 128 : sc * 128 + 64],
                                rdn_t[:, sc : sc + 1], wcol_t[:, 1:2],
                                ALU.mult, ALU.mult),
                                reads=[asb_b], sreads=[rdn_b, wcol_b],
                                writes=[t1_b])
                            kb.op("dve", lambda e, sc=sc, o=o:
                                  e.tensor_scalar(
                                      t2_t[:],
                                      numt_t[:, sc * KV + (o % 8) * 64 :
                                             sc * KV + (o % 8) * 64 + 64],
                                      rden_t[:, sc : sc + 1],
                                      wcol_t[:, 0:1], ALU.mult, ALU.mult),
                                  reads=[numt_b],
                                  sreads=[rden_b, wcol_b],
                                  writes=[t2_b])
                            kb.op("dve", lambda e: e.tensor_tensor(
                                t1_t[:], t1_t[:], t2_t[:], ALU.add),
                                reads=[t1_b, t2_b], writes=[t1_b],
                                accum=True)
                            kb.op("pe", lambda e: e.transpose(
                                ptrx[0:64, 0:128], t1_t[:], eye_bf[:]),
                                reads=[t1_b, eye_bf_b], writes=[ptrx_b])
                            g = o // 2
                            rbase = (o % 2) * 64
                            kb.op("dve", lambda e, g=g, rbase=rbase, sc=sc:
                                  e.tensor_tensor(
                                      x2_t[g][rbase : rbase + 64,
                                              sc * 128 : (sc + 1) * 128],
                                      x2_t[g][rbase : rbase + 64,
                                              sc * 128 : (sc + 1) * 128],
                                      ptrx[0:64, 0:128], ALU.add),
                                  reads=[x2_b[g], ptrx_b],
                                  writes=[x2_b[g]], accum=True)
                kb.barrier()
            kb.barrier()

    if PHASE >= 5:
        attnb_phase()
    if PHASE >= 6:
        ffn_phase("4")

    kb.tap("x3", [D, T], [(x2_t[g][:], x2_b[g],
                           (lambda t, g=g: t[g * 128:(g + 1) * 128]))
                          for g in range(G)])

    with contextlib.ExitStack() as sO:
        lps4, lsb4 = PS(sO), SB(sO)
        pout = [lps4("pout0", [128, 512], F32), lps4("pout1", [128, 512], F32)]
        otile = [lsb4("otile0", [128, D], F32), lsb4("otile1", [128, D], F32)]
        outs = kb.stream("out_store")
        for tc in range(NCH):
            ot, ob = otile[tc % 2]
            for g in range(G):
                pt, pb = pout[g % 2]
                kb.op("pe", lambda e, tc=tc, g=g, pt=pt: e.transpose(
                    pt[:, 0:128], x2_t[g][:, tc * 128 : (tc + 1) * 128],
                    eye_f[:]),
                    reads=[x2_b[g], eye_f_b], writes=[pb])
                kb.op("dve", lambda e, g=g, pt=pt, ot=ot: e.tensor_copy(
                    ot[:, g * 128 : (g + 1) * 128], pt[:, 0:128]),
                    reads=[pb], writes=[ob], accum=True)
            kb.dma("sp", outs, out_ap[tc * 128 : (tc + 1) * 128, :],
                   ot[:], reads=[ob])
    kb.finish()
    st.close()
    return kb


def kernel(**inputs):
    per_core, shared = prep_inputs(inputs)
    kb = build_program()
    in_maps = [dict(pc, **shared) for pc in per_core]
    res = run_bass_kernel_spmd(kb.nc, in_maps, core_ids=list(range(8)))
    out = np.stack([r["out"] for r in res.results], axis=0)
    if DBG:
        kernel.dbg = [{k: r["dbg_" + k] for k in DBG} for r in res.results]
    return out


# version 07
# speedup vs baseline: 1.2408x; 1.0826x over previous
import sys
if "/opt/trn_rl_repo" not in sys.path:
    sys.path.insert(0, "/opt/trn_rl_repo")

import contextlib
import numpy as np
import concourse.bass as bass
import concourse.mybir as mybir
from concourse.bass_utils import run_bass_kernel_spmd

F32 = mybir.dt.float32
BF16 = mybir.dt.bfloat16
AF = mybir.ActivationFunctionType
ALU = mybir.AluOpType
AX = mybir.AxisListType

D = 1024
T = 2048
G = 8
NTB = 4
TBW = 512
H = 4
DH = 256
C = 128
NCH = T // C
FLOOR = 0.01
HID = 4096
QH = 16
KVH = 8
DHB = 64
KV = 512

DBG = []
PHASE = 6
SCANCH = NCH
SCAN_DBG = ""
BSTAGE = 4
STATS_LIM = NCH
STATS_NOFIX = False

QA = [0, 1, 4, 5, 8, 9, 12, 13]
QB = [2, 3, 6, 7, 10, 11, 14, 15]

_PAIRS = [(0, 1), (0, 2), (0, 3), (1, 2), (1, 3), (2, 3)]


def _stat_rows():
    out = []
    for b in range(4):
        out.append(("mean", (b,), 4 * b, 1.0 / DH))
    for pi, (a, b) in enumerate(_PAIRS[:4]):
        out.append(("prod", (a, b), 16 + 4 * pi, 1.0 / DH))
    for b in range(4):
        out.append(("sq", (b,), 32 + 4 * b, 1.0 / (DH - 1)))
    for pi, (a, b) in enumerate(_PAIRS[4:]):
        out.append(("prod", (a, b), 48 + 4 * pi, 1.0 / DH))
    return out


def _feat_perm():
    our = np.zeros(56, np.int64)
    for b in range(4):
        for h in range(4):
            our[4 * b + h] = (2 * b) * 4 + h
            our[32 + 4 * b + h] = (2 * b + 1) * 4 + h
    for pi in range(4):
        for h in range(4):
            our[16 + 4 * pi + h] = 32 + 4 * pi + h
    for pi in range(2):
        for h in range(4):
            our[48 + 4 * pi + h] = 32 + (4 + pi) * 4 + h
    return our


def _retile_km(w, kc=128):
    k, m = w.shape
    assert k % kc == 0
    return np.ascontiguousarray(w.reshape(k // kc, kc, m))


def _pad_rows(w, rows):
    out = np.zeros((rows, w.shape[1]), w.dtype)
    out[: w.shape[0]] = w
    return out


def prep_inputs(inp):
    f32 = np.float32
    x = np.asarray(inp["L0_attn_a_hidden_states"], f32)
    shared = {}

    for nm in ("Wq", "Wk", "Wv"):
        w = np.asarray(inp["L0_attn_a_" + nm], f32)
        shared["a_" + nm] = np.ascontiguousarray(
            w.reshape(G, 128, G, 128).transpose(2, 0, 1, 3))
    shared["a_Wb"] = _retile_km(np.asarray(inp["L0_attn_a_Wb"], f32))
    for nm in ("conv_q_w", "conv_k_w", "conv_v_w"):
        shared["a_" + nm] = np.ascontiguousarray(
            np.asarray(inp["L0_attn_a_" + nm], f32).reshape(G, 128, 4))
    shared["a_local_w"] = np.ascontiguousarray(
        np.asarray(inp["L0_attn_a_local_w"], f32).reshape(G, 128, 7))
    shared["a_mid_w"] = np.ascontiguousarray(
        np.asarray(inp["L0_attn_a_mid_w"], f32).reshape(G, 128, 25))
    rw1 = np.asarray(inp["L0_attn_a_r_w1"], f32)
    perm = _feat_perm()
    rw1p = np.zeros((1152, 2176), f32)
    rw1p[:1024, :2160] = rw1[:1024]
    rw1p[1024 : 1024 + 56, :2160] = rw1[1024 + perm]
    rw1g = np.zeros((5, 9, 128, 512), f32)
    for mg in range(5):
        cols = rw1p[:, 512 * mg : 512 * (mg + 1)]
        rw1g[mg, :, :, : cols.shape[1]] = cols.reshape(9, 128, -1)
    shared["a_rw1"] = np.ascontiguousarray(rw1g)
    shared["a_rb1"] = np.ascontiguousarray(
        _pad_rows(np.asarray(inp["L0_attn_a_r_b1"], f32)[:, None], 2176)
        .reshape(17, 128).T)
    rw2 = _pad_rows(np.asarray(inp["L0_attn_a_r_w2"], f32), 2176)
    shared["a_rw2"] = _retile_km(rw2)
    shared["a_rb2"] = np.ascontiguousarray(
        np.asarray(inp["L0_attn_a_r_b2"], f32).reshape(16, 1))
    shared["a_Wo"] = np.ascontiguousarray(
        np.asarray(inp["L0_attn_a_Wo"], f32)
        .reshape(G, 128, G, 128).transpose(2, 0, 1, 3))
    selbank = np.zeros((128, 14 * 132), f32)
    for si, (_, _, r0, coef) in enumerate(_stat_rows()):
        selbank[:, 132 * si + r0 + 3] = coef
    shared["a_selbank"] = selbank
    psel = np.zeros((16, 16, 128), f32)
    for s in range(16):
        psel[s, s, :] = 1.0
    shared["a_psel"] = psel
    mixw = np.asarray(inp["L0_attn_a_mix_w"], f32)
    onw = np.asarray(inp["L0_attn_a_onorm_w"], f32)
    mixsel = np.zeros((G, 4, 128), f32)
    onsel = np.zeros((G, 4, 128), f32)
    for g in range(G):
        h = g // 2
        mixsel[g, h, :] = mixw[h, (g % 2) * 128 : (g % 2) * 128 + 128]
        onsel[g, h, :] = onw[(g % 2) * 128 : (g % 2) * 128 + 128]
    shared["a_mixsel"] = mixsel
    shared["a_onsel"] = onsel
    hsel = np.zeros((G, 128, 4), f32)
    for g in range(G):
        hsel[g, :, g // 2] = 1.0
    shared["a_hsel"] = hsel

    w1 = np.asarray(inp["L0_ffn_W1"], f32)
    b1cat = np.asarray(inp["L0_ffn_B1"], f32).reshape(24, HID)
    w1aug = np.concatenate([w1, _pad_rows(b1cat, 128)], 0)
    shared["f_w1"] = np.ascontiguousarray(
        w1aug.reshape(9, 128, 2, 2048).transpose(2, 0, 1, 3))
    w2 = np.asarray(inp["L0_ffn_W2"], f32)
    shared["f_w2"] = np.ascontiguousarray(w2.reshape(2, 16, 128, D))
    b2cat = np.asarray(inp["L0_ffn_B2"], f32).reshape(24, D)
    shared["f_b2cat"] = np.ascontiguousarray(_pad_rows(b2cat, 128))
    a1 = np.asarray(inp["L0_ffn_A1"], f32).transpose(1, 0, 2).reshape(D, 24)
    r1 = np.asarray(inp["L0_ffn_R1"], f32)
    shared["f_a1r1"] = _retile_km(np.concatenate([a1, r1], 1))
    a2 = np.asarray(inp["L0_ffn_A2"], f32).transpose(1, 0, 2).reshape(HID, 24)
    r2 = np.asarray(inp["L0_ffn_R2"], f32)
    shared["f_a2r2"] = np.ascontiguousarray(
        np.concatenate([a2, r2], 1).reshape(2, 16, 128, 27))
    shared["f_b1"] = np.ascontiguousarray(
        np.asarray(inp["L0_ffn_b1"], f32).reshape(32, 128).T)
    shared["f_b2"] = np.ascontiguousarray(
        np.asarray(inp["L0_ffn_b2"], f32).reshape(G, 128).T)

    wqb = np.asarray(inp["L0_attn_b_Wq"], f32)
    qperm = np.zeros(D, np.int64)
    for j in range(8):
        qperm[j * 128 : j * 128 + 64] = QA[j] * 64 + np.arange(64)
        qperm[j * 128 + 64 : (j + 1) * 128] = QB[j] * 64 + np.arange(64)
    wqb = wqb[:, qperm]
    shared["b_Wq"] = np.ascontiguousarray(
        wqb.reshape(G, 128, G, 128).transpose(2, 0, 1, 3))
    for nm, go in (("Wk", 4), ("Wv", 4)):
        w = np.asarray(inp["L0_attn_b_" + nm], f32)
        shared["b_" + nm] = np.ascontiguousarray(
            w.reshape(G, 128, go, 128).transpose(2, 0, 1, 3))
    shared["b_mem"] = _retile_km(np.asarray(inp["L0_attn_b_memory"], f32))
    shared["b_memn"] = np.ascontiguousarray(
        np.asarray(inp["L0_attn_b_memory_norm"], f32).reshape(4, 128, 1))
    shared["b_mw"] = np.asarray(
        inp["L0_attn_b_memory_weight"], f32).reshape(1, 1)
    e8x = np.zeros((32, 16), f32)
    for j in range(8):
        e8x[16 + QA[j] // 2, 2 * j] = 1.0
        e8x[16 + QB[j] // 2, 2 * j + 1] = 1.0
    shared["b_e8"] = e8x
    ones2 = np.zeros((128, 2), f32)
    ones2[:64, 0] = 1.0
    ones2[64:, 1] = 1.0
    shared["c_ones2"] = ones2

    shared["c_eye"] = np.eye(128, dtype=f32)
    shared["c_ones_col"] = np.ones((128, 1), f32)
    shared["c_ones_row"] = np.ones((1, 128), f32)
    shift32 = np.zeros((64, 64), f32)
    for j in range(16):
        shift32[j, 32 + j] = 1.0
    shared["c_shift32"] = shift32
    shared["c_sl_mask"] = np.tril(np.ones((128, 128), f32), -1)
    shared["c_ui_mask"] = np.triu(np.ones((128, 128), f32), 0)

    per_core = [{"x_in": np.ascontiguousarray(x[b].T)} for b in range(8)]
    return per_core, shared



class Buf:
    __slots__ = ("name", "w", "r")

    def __init__(self, name):
        self.name = name
        self.w = set()
        self.r = set()


class DmaStream:
    __slots__ = ("name", "sem", "cum")

    def __init__(self, name, sem):
        self.name = name
        self.sem = sem
        self.cum = 0


class KB:
    ENGS = ("pe", "dve", "act", "pool", "sp")

    def __init__(self, nc, stack):
        self.nc = nc
        self.e = dict(pe=nc.tensor, dve=nc.vector, act=nc.scalar,
                      pool=nc.gpsimd, sp=nc.sync)
        self.rec = {k: [] for k in self.ENGS}
        self.assigned = {k: [] for k in self.ENGS}
        self.esem = {k: stack.enter_context(nc.semaphore("s_" + k))
                     for k in ("pe", "dve", "act", "pool")}
        self.wm = {}
        self.streams = {}
        self.stack = stack
        self.taps = {}
        self.tap_stream = self.stream("taps")
        self.n_waits = 0
        self.barrier_toks = []
        self.barrier_gen = 0
        self.barrier_seen = {}
        self.log = []

    def stream(self, name):
        if name not in self.streams:
            sem = self.stack.enter_context(self.nc.semaphore("d_" + name))
            self.streams[name] = DmaStream(name, sem)
        return self.streams[name]

    def _resolve_eng(self, eng, idx):
        a = self.assigned[eng]
        lo, hi = 0, len(a)
        while lo < hi:
            mid = (lo + hi) // 2
            if a[mid][0] >= idx:
                hi = mid
            else:
                lo = mid + 1
        if lo < len(a):
            return ("e" + eng, self.esem[eng], a[lo][1])
        cnt = (a[-1][1] if a else 0) + 1
        self.rec[eng][idx].then_inc(self.esem[eng], 1)
        self.log.append((eng, "inc@", idx, cnt))
        a.append((idx, cnt))
        return ("e" + eng, self.esem[eng], cnt)

    def _wait(self, eng, tok, force_same=False):
        if tok[0] == "d":
            _, stream, cum = tok
            key, sem, cnt = ("d" + stream.name, stream.sem, stream.cum)
        else:
            _, peng, idx = tok
            if peng == eng and not force_same:
                return
            key, sem, cnt = self._resolve_eng(peng, idx)
        if self.wm.get((eng, key), 0) >= cnt:
            self.log.append((eng, "skipwait", key, cnt))
            return
        self.e[eng].wait_ge(sem, cnt)
        self.log.append((eng, "wait", key, cnt))
        self.n_waits += 1
        self.wm[(eng, key)] = cnt

    def _deps(self, eng, reads, writes, sreads=()):
        if self.barrier_seen.get(eng, 0) < self.barrier_gen:
            self.barrier_seen[eng] = self.barrier_gen
            for tok in self.barrier_toks:
                self._wait(eng, tok)
        toks = set()
        for b in reads:
            toks |= b.w
        for b in writes:
            toks |= b.w | b.r
        stoks = set()
        for b in sreads:
            stoks |= b.w
        nrec = len(self.rec[eng])
        for tok in sorted(stoks - toks, key=repr):
            self._wait(eng, tok, force_same=True)
        for tok in sorted(toks, key=repr):
            force = (tok in stoks) or (
                tok[0] == "e" and tok[1] == eng and tok[2] >= nrec - 8)
            self._wait(eng, tok, force_same=force)

    def mark(self, *args):
        self.log.append(("MARK",) + args)

    def barrier(self):
        toks = []
        for eng in ("pe", "dve", "act", "pool"):
            n = len(self.rec[eng])
            if n:
                toks.append(("e", eng, n - 1))
        for s in self.streams.values():
            if s.cum:
                toks.append(("d", s, s.cum))
        self.barrier_toks = toks
        self.barrier_gen += 1

    def op(self, eng, emit, reads=(), writes=(), accum=False, sreads=()):
        self._deps(eng, reads, writes, sreads)
        inst = emit(self.e[eng])
        idx = len(self.rec[eng])
        self.rec[eng].append(inst)
        self.log.append((eng, "op", idx, getattr(inst, "name", "")))
        tok = ("e", eng, idx)
        for b in list(reads) + list(sreads):
            b.r.add(tok)
        for b in writes:
            if accum:
                b.w.add(tok)
            else:
                b.w = {tok}
                b.r = set()
        return tok

    def dma(self, qeng, stream, out_ap, in_ap, reads=(), writes=(),
            accum=False):
        self._deps(qeng, reads, writes)
        inst = self.e[qeng].dma_start(out=out_ap, in_=in_ap)
        stream.cum += 16
        inst.then_inc(stream.sem, 16)
        tok = ("d", stream, stream.cum)
        for b in reads:
            b.r.add(tok)
        for b in writes:
            if accum:
                b.w.add(tok)
            else:
                b.w = {tok}
                b.r = set()
        return tok

    def tap(self, name, shape, srcs):
        if name not in DBG:
            return
        t = self.nc.dram_tensor("dbg_" + name, list(shape), F32,
                                kind="ExternalOutput").ap()
        self.taps[name] = t
        for ap, buf, slc in srcs:
            self.dma("pool", self.tap_stream, slc(t), ap, reads=[buf])

    def finish(self):
        for s in self.streams.values():
            if s.cum:
                self.e["sp"].wait_ge(s.sem, s.cum)



class _Skip(Exception):
    pass


def build_program():
    nc = bass.Bass("TRN2", target_bir_lowering=False)
    st = contextlib.ExitStack()

    dr = {}
    dr["x_in"] = nc.dram_tensor("x_in", [D, T], F32, kind="ExternalInput").ap()
    shapes = {
        "a_Wq": [G, G, 128, 128], "a_Wk": [G, G, 128, 128],
        "a_Wv": [G, G, 128, 128], "a_Wb": [G, 128, 4],
        "a_conv_q_w": [G, 128, 4], "a_conv_k_w": [G, 128, 4],
        "a_conv_v_w": [G, 128, 4],
        "a_local_w": [G, 128, 7], "a_mid_w": [G, 128, 25],
        "a_rw1": [5, 9, 128, 512], "a_rb1": [128, 17],
        "a_rw2": [17, 128, 16], "a_rb2": [16, 1],
        "a_Wo": [G, G, 128, 128],
        "a_selbank": [128, 14 * 132], "a_psel": [16, 16, 128],
        "a_mixsel": [G, 4, 128], "a_onsel": [G, 4, 128],
        "f_w1": [2, 9, 128, 2048], "f_w2": [2, 16, 128, D],
        "f_b2cat": [128, D],
        "f_a1r1": [G, 128, 27], "f_a2r2": [2, 16, 128, 27],
        "f_b1": [128, 32], "f_b2": [128, G],
        "b_Wq": [G, G, 128, 128], "b_Wk": [4, G, 128, 128],
        "b_Wv": [4, G, 128, 128],
        "b_mem": [4, 128, KV], "b_memn": [4, 128, 1], "b_mw": [1, 1],
        "b_e8": [32, 16], "c_ones2": [128, 2],
        "a_hsel": [G, 128, 4],
        "c_eye": [128, 128], "c_ones_col": [128, 1], "c_ones_row": [1, 128],
        "c_sl_mask": [128, 128], "c_ui_mask": [128, 128],
        "c_shift32": [64, 64],
    }
    for nm, shp in shapes.items():
        dr[nm] = nc.dram_tensor(nm, shp, F32, kind="ExternalInput").ap()
    out_ap = nc.dram_tensor("out", [T, D], F32, kind="ExternalOutput").ap()
    xsp = nc.dram_tensor("x_spill", [G, 128, T], F32, kind="Internal").ap()
    aout_sp = nc.dram_tensor("aout_spill", [G, 128, T], BF16,
                             kind="Internal").ap()

    kb = KB(nc, st)

    def SB(stk):
        def f(name, shape, dtype):
            t = stk.enter_context(nc.sbuf_tensor(name, shape, dtype))
            return t, Buf(name)
        return f

    def PS(stk):
        def f(name, shape, dtype):
            t = stk.enter_context(nc.psum_tensor(name, shape, dtype))
            return t, Buf(name)
        return f

    sb = SB(st)

    cst = kb.stream("consts")
    eye_bf, eye_bf_b = sb("eye_bf", [128, 128], BF16)
    eye_f, eye_f_b = sb("eye_f", [128, 128], F32)
    ones_col, ones_col_b = sb("ones_col", [128, 1], BF16)
    ones_row, ones_row_b = sb("ones_row", [1, 128], BF16)
    sl_mask, sl_mask_b = sb("sl_mask", [128, 128], BF16)
    ui_mask, ui_mask_b = sb("ui_mask", [128, 128], BF16)
    kb.dma("pool", cst, eye_bf[:], dr["c_eye"], writes=[eye_bf_b])
    kb.dma("sp", cst, eye_f[:], dr["c_eye"], writes=[eye_f_b])
    kb.dma("pool", cst, ones_col[:], dr["c_ones_col"], writes=[ones_col_b])
    kb.dma("pool", cst, ones_row[:], dr["c_ones_row"], writes=[ones_row_b])
    kb.dma("pool", cst, sl_mask[:], dr["c_sl_mask"], writes=[sl_mask_b])
    kb.dma("pool", cst, ui_mask[:], dr["c_ui_mask"], writes=[ui_mask_b])

    constf_t, constf_b = sb("constf", [128, 16], F32)
    _constf = {}

    def constf(v):
        v = float(v)
        if v not in _constf:
            i = len(_constf)
            _constf[v] = i
            kb.op("pool", lambda e, i=i, v=v: e.memset(
                constf_t[:, i : i + 1], v), writes=[constf_b], accum=True)
            nc.const_aps.register(F32, v, constf_t[:, i : i + 1])
        return constf_b

    def layernorm(tag, x_t, x_b, xb_t, xb_b, scope=None):
        lnst = contextlib.ExitStack()
        lsb, lps = SB(lnst), PS(lnst)
        xbp_t, xbp_b = lsb(f"ln{tag}_xbp", [128, TBW], BF16)
        xsq_t, xsq_b = lsb(f"ln{tag}_xsq", [128, TBW], BF16)
        ps_s, ps_s_b = lps(f"ln{tag}_ps", [1, TBW], F32)
        ps_q, ps_q_b = lps(f"ln{tag}_pq", [1, TBW], F32)
        ps_bc, ps_bc_b = lps(f"ln{tag}_pbc", [128, TBW], F32)
        row_a, row_a_b = lsb(f"ln{tag}_ra", [1, TBW], F32)
        mrow, mrow_b = lsb(f"ln{tag}_m", [1, TBW], F32)
        rrow, rrow_b = lsb(f"ln{tag}_r", [1, TBW], F32)
        rowbf, rowbf_b = lsb(f"ln{tag}_bf", [1, TBW], BF16)
        for tb in range(NTB):
            sl = slice(tb * TBW, (tb + 1) * TBW)
            for g in range(G):
                kb.op("dve", lambda e, g=g, sl=sl: e.tensor_copy(
                    xbp_t[:], x_t[g][:, sl]), reads=[x_b[g]], writes=[xbp_b])
                kb.op("act", lambda e, g=g, sl=sl: e.activation(
                    xsq_t[:], x_t[g][:, sl], AF.Square),
                    reads=[x_b[g]], writes=[xsq_b])
                kb.op("pe", lambda e, g=g: e.matmul(
                    ps_s[:], ones_col[:], xbp_t[:],
                    start=(g == 0), stop=(g == G - 1)),
                    reads=[ones_col_b, xbp_b], writes=[ps_s_b], accum=(g > 0))
                kb.op("pe", lambda e, g=g: e.matmul(
                    ps_q[:], ones_col[:], xsq_t[:],
                    start=(g == 0), stop=(g == G - 1)),
                    reads=[ones_col_b, xsq_b], writes=[ps_q_b], accum=(g > 0))
            kb.op("dve", lambda e: e.tensor_scalar(
                mrow[:], ps_s[:], 1.0 / D, None, ALU.mult),
                reads=[ps_s_b], writes=[mrow_b])
            kb.op("act", lambda e: e.activation(row_a[:], mrow[:], AF.Square),
                  reads=[mrow_b], writes=[row_a_b])
            kb.op("dve", lambda e: e.scalar_tensor_tensor(
                rrow[:], ps_q[:], 1.0 / D, row_a[:], ALU.mult, ALU.subtract),
                reads=[ps_q_b, row_a_b], writes=[rrow_b])
            kb.op("act", lambda e: e.activation(
                rrow[:], rrow[:], AF.Sqrt, bias=1e-5),
                reads=[rrow_b, constf(1e-5)], writes=[rrow_b], accum=True)
            kb.op("dve", lambda e: e.reciprocal(rrow[:], rrow[:]),
                  reads=[rrow_b], writes=[rrow_b], accum=True)
            kb.op("dve", lambda e: e.tensor_scalar(
                rowbf[:], mrow[:], -1.0, None, ALU.mult),
                reads=[mrow_b], writes=[rowbf_b])
            kb.op("pe", lambda e: e.matmul(
                ps_bc[:], ones_row[:], rowbf[0:1, :], start=True, stop=True),
                reads=[ones_row_b, rowbf_b], writes=[ps_bc_b])
            for g in range(G):
                kb.op("dve", lambda e, g=g, sl=sl: e.tensor_tensor(
                    xb_t[g][:, sl], x_t[g][:, sl], ps_bc[:], ALU.add),
                    reads=[x_b[g], ps_bc_b], writes=[xb_b[g]], accum=True)
            kb.op("dve", lambda e: e.tensor_copy(rowbf[:], rrow[:]),
                  reads=[rrow_b], writes=[rowbf_b])
            kb.op("pe", lambda e: e.matmul(
                ps_bc[:], ones_row[:], rowbf[0:1, :], start=True, stop=True),
                reads=[ones_row_b, rowbf_b], writes=[ps_bc_b])
            for g in range(G):
                kb.op("dve", lambda e, g=g, sl=sl: e.tensor_tensor(
                    xb_t[g][:, sl], xb_t[g][:, sl], ps_bc[:], ALU.mult),
                    reads=[ps_bc_b] + [xb_b[g]], writes=[xb_b[g]], accum=True)
        kb.tap(f"ln{tag}", [D, T], [(xb_t[g][:], xb_b[g],
                                     (lambda t, g=g: t[g * 128:(g + 1) * 128]))
                                    for g in range(G)])
        kb.barrier()
        lnst.close()

    def proj_mms(pt, pb, wtile, wbuf, rhs_t, rhs_b, sl, nk=G):
        for gi in range(nk):
            kb.op("pe", lambda e, gi=gi: e.matmul(
                pt, wtile[:, gi * 128:(gi + 1) * 128], rhs_t[gi][:, sl],
                start=(gi == 0), stop=(gi == nk - 1)),
                reads=[wbuf, rhs_b[gi]], writes=[pb], accum=(gi > 0))

    s1 = st.enter_context(contextlib.ExitStack())
    sb1 = SB(s1)
    xb_t = []
    xb_b = []
    for g in range(G):
        t, b = sb1(f"xb{g}", [128, T], BF16)
        xb_t.append(t)
        xb_b.append(b)
    v_t = []
    v_b = []
    for g in range(G):
        t, b = sb1(f"v{g}", [128, T], BF16)
        v_t.append(t)
        v_b.append(b)
    dl_t = []
    dl_b = []
    for g in range(G):
        t, b = sb1(f"dl{g}", [128, T], BF16)
        dl_t.append(t)
        dl_b.append(b)
    beta_t, beta_b = sb1("beta", [128, 4 * NCH], F32)

    with contextlib.ExitStack() as sA:
        lsb, lps = SB(sA), PS(sA)
        x_t = []
        x_b = []
        for g in range(G):
            t, b = lsb(f"x{g}", [128, T], F32)
            x_t.append(t)
            x_b.append(b)
        xs = kb.stream("x_io")
        xin3 = dr["x_in"].rearrange("(g p) t -> g p t", p=128)
        for g in range(G):
            kb.dma("sp", xs, x_t[g][:], xin3[g], writes=[x_b[g]])

        layernorm("1", x_t, x_b, xb_t, xb_b, sA)

        ws = kb.stream("wstr")
        wst = [lsb("a1_wst0", [128, D], BF16), lsb("a1_wst1", [128, D], BF16)]
        wb_tiles = []
        for g in range(G):
            t, b = lsb(f"a_Wb_{g}", [128, 4], BF16)
            kb.dma("pool", ws, t[:], dr["a_Wb"][g], writes=[b])
            wb_tiles.append((t, b))
        cvw = []
        for g in range(G):
            t, b = lsb(f"cvw{g}", [128, 4], F32)
            kb.dma("sp", ws, t[:], dr["a_conv_v_w"][g], writes=[b])
            cvw.append((t, b))

        pp, pp_b = lps("a1_pp", [128, TBW], F32)
        pp2, pp2_b = lps("a1_pp2", [128, TBW], F32)
        pbeta, pbeta_b = lps("a1_pbeta", [128, 8], F32)
        cpre_t, cpre_b = lsb("a1_cpre", [128, T + 4], BF16)
        acc0_t, acc0_b = lsb("a1_acc0", [128, T], BF16)

        for go in range(G):
            wtile, wbuf = wst[go % 2]
            kb.dma("pool", kb.stream(f"wq{go % 2}"),
                   wtile[:].rearrange("p (gi m) -> p gi m", m=128),
                   dr["a_Wv"][go].rearrange("gi p m -> p gi m"),
                   writes=[wbuf])
            pbank = [(pp, pp_b), (pp2, pp2_b)]
            for tb in range(NTB):
                pt, pb = pbank[tb % 2]
                sl = slice(tb * TBW, (tb + 1) * TBW)
                proj_mms(pt[:], pb, wtile, wbuf, xb_t, xb_b, sl)
                kb.op("dve", lambda e, tb=tb, pt=pt: e.tensor_copy(
                    cpre_t[:, 4 + tb * TBW : 4 + (tb + 1) * TBW], pt[:]),
                    reads=[pb], writes=[cpre_b], accum=True)
            kb.op("dve", lambda e: e.memset(cpre_t[:, 0:4], 0.0),
                  writes=[cpre_b], accum=True)
            ctile, cbuf = cvw[go]
            kb.op("dve", lambda e, ctile=ctile: e.tensor_scalar(
                acc0_t[:], cpre_t[:, 4 : 4 + T], ctile[:, 3:4], None,
                ALU.mult), reads=[cpre_b, cbuf], writes=[acc0_b])
            accs = [(acc0_t, acc0_b), (v_t[go], v_b[go])]
            for tau in range(1, 4):
                src_t, src_b2 = accs[(tau + 1) % 2]
                dst_t, dst_b2 = accs[tau % 2]
                kb.op("dve", lambda e, tau=tau, ctile=ctile, src_t=src_t,
                      dst_t=dst_t: e.scalar_tensor_tensor(
                          dst_t[:], cpre_t[:, 4 - tau : 4 - tau + T],
                          ctile[:, 3 - tau : 4 - tau], src_t[:],
                          ALU.mult, ALU.add),
                      reads=[cpre_b, cbuf, src_b2], writes=[dst_b2])
            kb.op("act", lambda e, go=go: e.activation(
                v_t[go][:], v_t[go][:], AF.Silu),
                reads=[v_b[go]], writes=[v_b[go]], accum=True)

        for tc in range(NCH):
            for gi in range(G):
                wtile, wbuf = wb_tiles[gi]
                kb.op("pe", lambda e, gi=gi, tc=tc, wtile=wtile: e.matmul(
                    pbeta[:, 0:4], xb_t[gi][:, tc * 128 : (tc + 1) * 128],
                    wtile[:], start=(gi == 0), stop=(gi == G - 1)),
                    reads=[wbuf, xb_b[gi]], writes=[pbeta_b], accum=(gi > 0))
            kb.op("act", lambda e, tc=tc: e.activation(
                beta_t[:, 4 * tc : 4 * tc + 4], pbeta[:, 0:4], AF.Sigmoid),
                reads=[pbeta_b], writes=[beta_b], accum=True)

        for g in range(G):
            kb.dma("sp", xs, xsp[g], x_t[g][:], reads=[x_b[g]])
        kb.tap("v", [D, T], [(v_t[g][:], v_b[g],
                              (lambda t, g=g: t[g * 128:(g + 1) * 128]))
                             for g in range(G)])
        kb.tap("beta", [128, 4 * NCH], [(beta_t[:], beta_b, lambda t: t)])
        kb.barrier()

    with contextlib.suppress(_Skip), contextlib.ExitStack() as sS:
        if PHASE < 2:
            raise _Skip()
        lsb, lps = SB(sS), PS(sS)
        ws = kb.stream("wstr")
        wq2 = [lsb("s_wq0", [128, D], BF16), lsb("s_wq1", [128, D], BF16)]
        cqw = []
        ckw = []
        for g in range(G):
            t, b = lsb(f"cqw{g}", [128, 4], F32)
            kb.dma("sp", ws, t[:], dr["a_conv_q_w"][g], writes=[b])
            cqw.append((t, b))
            t, b = lsb(f"ckw{g}", [128, 4], F32)
            kb.dma("sp", ws, t[:], dr["a_conv_k_w"][g], writes=[b])
            ckw.append((t, b))
        qg_t = []
        qg_b = []
        kg_t = []
        kg_b = []
        cprq = []
        cprk = []
        for g in range(G):
            t, b = lsb(f"qg{g}", [128, TBW], BF16)
            qg_t.append(t)
            qg_b.append(b)
            t, b = lsb(f"kg{g}", [128, TBW], BF16)
            kg_t.append(t)
            kg_b.append(b)
            cprq.append(lsb(f"cprq{g}", [128, TBW + 4], BF16))
            cprk.append(lsb(f"cprk{g}", [128, TBW + 4], BF16))
        S_t = []
        S_b = []
        for h in range(H):
            for c2 in range(2):
                t, b = lsb(f"S{h}_{c2}", [128, DH], F32)
                S_t.append(t)
                S_b.append(b)
        Sbf_t, Sbf_b = lsb("Sbf", [128, 2 * DH], BF16)
        ktm3 = [lsb(f"ktm{i}", [128, DH], BF16) for i in range(3)]
        kbm3 = [lsb(f"kbm{i}", [128, DH], BF16) for i in range(3)]
        vcm3 = [lsb(f"vcm{i}", [128, DH], BF16) for i in range(3)]
        rq3 = [lsb(f"rqc{i}", [128, 2], F32) for i in range(3)]
        at3 = [lsb(f"at{i}", [128, 128], BF16) for i in range(3)]
        vtm_t, vtm_b = lsb("vtm", [128, DH], BF16)
        uib_t, uib_b = lsb("uib", [128, DH], BF16)
        nw_t, nw_b = lsb("nw", [128, DH], BF16)
        otmp_t, otmp_b = lsb("otmp", [128, DH], BF16)
        y02 = [lsb(f"y0{i}", [128, 128], F32) for i in range(2)]
        cn2 = [[lsb(f"cn{i}_{j}", [128, 128], F32) for j in range(2)]
               for i in range(2)]
        cp2 = [[lsb(f"cp{i}_{j}", [128, 128], F32) for j in range(2)]
               for i in range(2)]
        rsb_t, rsb_b = lsb("rsb", [128, 128], F32)
        rsbbf2 = [lsb(f"rsbbf{i}", [128, 128], BF16) for i in range(2)]
        sq_t, sq_b = lsb("sqs", [128, 256], BF16)
        bkr_t, bkr_b = lsb("bkr", [128, 1], F32)
        pAG, pAG_b = lps("pAG", [128, 512], F32)
        ptm, ptm_b = lps("ptm", [128, 512], BF16)
        psq, psq_b = lps("psq", [128, 128], F32)
        pR, pR_b = lps("pR", [128, 128], F32)
        pUO, pUO_b = lps("pUO", [128, DH], F32)
        pW, pW_b = lps("pW", [128, DH], F32)
        pOT, pOT_b = lps("pOT", [128, 256], BF16)
        pS, pS_b = lps("pS", [128, 512], F32)
        pproj, pproj_b = pS, pS_b

        for h in range(H):
            for c2 in range(2):
                kb.op("dve", lambda e, i=2 * h + c2: e.memset(S_t[i][:], 0.0),
                      writes=[S_b[2 * h + c2]])

        def conv_group(gq, which, wst_idx, cpr, cw, out_t, out_b, wnm, cnmb):
            sl = slice(gq * TBW, (gq + 1) * TBW)
            for go in range(G):
                wtile, wbuf = wq2[(wst_idx + go) % 2]
                kb.dma("pool", kb.stream(f"wq{(wst_idx + go) % 2}"),
                       wtile[:].rearrange("p (gi m) -> p gi m", m=128),
                       dr[wnm][go].rearrange("gi p m -> p gi m"),
                       writes=[wbuf])
                proj_mms(pproj[:], pproj_b, wtile, wbuf, xb_t, xb_b, sl)
                ct, cb = cpr[go]
                if gq == 0:
                    kb.op("dve", lambda e, ct=ct: e.memset(ct[:, 0:4], 0.0),
                          writes=[cb], accum=True)
                kb.op("dve", lambda e, ct=ct: e.tensor_copy(
                    ct[:, 4 : 4 + TBW], pproj[:]),
                    reads=[pproj_b], writes=[cb], accum=True)
                wtile2, cbuf2 = cw[go]
                kb.op("dve", lambda e, ct=ct, w=wtile2: e.tensor_scalar(
                    acc2_t[:], ct[:, 4 : 4 + TBW], w[:, 3:4], None, ALU.mult),
                    reads=[cb, cbuf2], writes=[acc2_b])
                accs = [(acc2_t, acc2_b), (out_t[go], out_b[go])]
                for tau in range(1, 4):
                    src_t, src_b2 = accs[(tau + 1) % 2]
                    dst_t, dst_b2 = accs[tau % 2]
                    kb.op("dve", lambda e, tau=tau, ct=ct, w=wtile2,
                          src_t=src_t, dst_t=dst_t: e.scalar_tensor_tensor(
                              dst_t[:], ct[:, 4 - tau : 4 - tau + TBW],
                              w[:, 3 - tau : 4 - tau], src_t[:],
                              ALU.mult, ALU.add),
                          reads=[cb, cbuf2, src_b2], writes=[dst_b2])
                kb.op("act", lambda e, o=out_t[go]: e.activation(
                    o[:], o[:], AF.Silu),
                    reads=[out_b[go]], writes=[out_b[go]], accum=True)
                kb.op("dve", lambda e, ct=ct: e.tensor_copy(
                    ct[:, 0:4], ct[:, TBW : TBW + 4]),
                    reads=[cb], writes=[cb], accum=True)

        acc2_t, acc2_b = lsb("s_acc2", [128, TBW], BF16)

        qf_ap = kf_ap = None
        if "qf" in DBG:
            qf_ap = nc.dram_tensor("dbg_qf", [D, T], F32,
                                   kind="ExternalOutput").ap()
            kb.taps["qf"] = qf_ap
        if "kf" in DBG:
            kf_ap = nc.dram_tensor("dbg_kf", [D, T], F32,
                                   kind="ExternalOutput").ap()
            kb.taps["kf"] = kf_ap

        def pre_gen(i, c, h):
            s3, s2 = i % 3, i % 2
            co = (c % 4) * 128
            csl = slice(co, co + 128)
            ga, gb = 2 * h, 2 * h + 1
            bcol = beta_t[:, 4 * c + h : 4 * c + h + 1]
            ktm_t, ktm_b = ktm3[s3]
            kbm_t, kbm_b = kbm3[s3]
            vcm_t, vcm_b = vcm3[s3]
            rq_t, rq_b = rq3[s3]
            at_t, at_b = at3[s3]
            y0_t, y0_b = y02[s2]
            cn, cp = cn2[s2], cp2[s2]
            for idx, (src_t, src_b) in enumerate(
                    [(qg_t, qg_b), (kg_t, kg_b)]):
                kb.op("act", lambda e, s=src_t[ga]: e.activation(
                    sq_t[:, 0:128], s[:, csl], AF.Square),
                    reads=[src_b[ga]], writes=[sq_b])
                yield
                kb.op("act", lambda e, s=src_t[gb]: e.activation(
                    sq_t[:, 128:256], s[:, csl], AF.Square),
                    reads=[src_b[gb]], writes=[sq_b], accum=True)
                yield
                kb.op("pe", lambda e, idx=idx: e.matmul(
                    pAG[:, 256 + idx : 257 + idx], sq_t[:, 0:128],
                    ones_col[:], start=True, stop=False),
                    reads=[sq_b, ones_col_b], writes=[pAG_b], accum=True)
                yield
                kb.op("pe", lambda e, idx=idx: e.matmul(
                    pAG[:, 256 + idx : 257 + idx], sq_t[:, 128:256],
                    ones_col[:], start=False, stop=True),
                    reads=[sq_b, ones_col_b], writes=[pAG_b], accum=True)
                yield
            kb.op("dve", lambda e, rq_t=rq_t: e.tensor_scalar(
                rq_t[:], pAG[:, 256:258], 1e-12, None, ALU.add),
                reads=[pAG_b], writes=[rq_b])
            yield
            kb.op("act", lambda e, rq_t=rq_t: e.activation(
                rq_t[:], rq_t[:], AF.Sqrt),
                reads=[rq_b], writes=[rq_b], accum=True)
            yield
            kb.op("dve", lambda e, rq_t=rq_t: e.reciprocal(
                rq_t[:], rq_t[:]),
                reads=[rq_b], writes=[rq_b], accum=True)
            yield
            kb.op("pe", lambda e: e.transpose(
                ptm[:, 0:128], kg_t[ga][:, csl], eye_bf[:]),
                reads=[kg_b[ga], eye_bf_b], writes=[ptm_b])
            yield
            kb.op("pe", lambda e: e.transpose(
                ptm[:, 128:256], kg_t[gb][:, csl], eye_bf[:]),
                reads=[kg_b[gb], eye_bf_b], writes=[ptm_b], accum=True)
            yield
            kb.op("pe", lambda e: e.transpose(
                ptm[:, 256:384], v_t[ga][:, c * 128 : c * 128 + 128],
                eye_bf[:]),
                reads=[v_b[ga], eye_bf_b], writes=[ptm_b], accum=True)
            yield
            kb.op("pe", lambda e: e.transpose(
                ptm[:, 384:512], v_t[gb][:, c * 128 : c * 128 + 128],
                eye_bf[:]),
                reads=[v_b[gb], eye_bf_b], writes=[ptm_b], accum=True)
            yield
            kb.op("dve", lambda e, ktm_t=ktm_t: e.tensor_copy(
                ktm_t[:], ptm[:, 0:256]),
                reads=[ptm_b], writes=[ktm_b])
            yield
            kb.op("dve", lambda e: e.tensor_copy(
                vtm_t[:], ptm[:, 256:512]),
                reads=[ptm_b], writes=[vtm_b])
            yield
            kb.op("dve", lambda e, bcol=bcol, rq_t=rq_t:
                  e.scalar_tensor_tensor(
                      bkr_t[:], rq_t[:, 1:2], -1.0, bcol, ALU.mult,
                      ALU.mult),
                  reads=[rq_b, beta_b], writes=[bkr_b])
            yield
            kb.op("dve", lambda e, kbm_t=kbm_t, ktm_t=ktm_t: e.tensor_scalar(
                kbm_t[:], ktm_t[:], bkr_t[:], None, ALU.mult),
                reads=[ktm_b], sreads=[bkr_b], writes=[kbm_b])
            yield
            kb.op("dve", lambda e, bcol=bcol, vcm_t=vcm_t: e.tensor_scalar(
                vcm_t[:], vtm_t[:], bcol, None, ALU.mult),
                reads=[vtm_b], sreads=[beta_b], writes=[vcm_b])
            yield
            kb.op("pe", lambda e: e.matmul(
                pAG[:, 0:128], kg_t[ga][:, csl], kg_t[ga][:, csl],
                start=True, stop=False),
                reads=[kg_b[ga]], writes=[pAG_b], accum=True)
            yield
            kb.op("pe", lambda e: e.matmul(
                pAG[:, 0:128], kg_t[gb][:, csl], kg_t[gb][:, csl],
                start=False, stop=True),
                reads=[kg_b[gb]], writes=[pAG_b], accum=True)
            yield
            kb.op("dve", lambda e, y0_t=y0_t: e.tensor_scalar(
                y0_t[:], pAG[:, 0:128], bkr_t[:], None, ALU.mult),
                reads=[pAG_b], sreads=[bkr_b], writes=[y0_b])
            yield
            kb.op("dve", lambda e, y0_t=y0_t: e.tensor_tensor(
                y0_t[:], y0_t[:], sl_mask[:], ALU.mult),
                reads=[y0_b, sl_mask_b], writes=[y0_b], accum=True)
            yield
            kb.op("pe", lambda e, y0_t=y0_t: e.transpose(
                pAG[:, 320:448], y0_t[:], eye_f[:]),
                reads=[y0_b, eye_f_b], writes=[pAG_b], accum=True)
            yield
            kb.op("dve", lambda e, cp=cp, rq_t=rq_t: e.tensor_scalar(
                cp[0][0][:], pAG[:, 320:448], rq_t[:, 1:2], None,
                ALU.mult),
                reads=[pAG_b], sreads=[rq_b], writes=[cp[0][1]])
            yield
            kb.op("pe", lambda e, cp=cp: e.transpose(
                pAG[:, 320:448], cp[0][0][:], eye_f[:]),
                reads=[cp[0][1], eye_f_b], writes=[pAG_b], accum=True)
            yield
            kb.op("dve", lambda e, cn=cn: e.tensor_copy(
                cn[0][0][:], pAG[:, 320:448]),
                reads=[pAG_b], writes=[cn[0][1]])
            yield
            kb.op("pe", lambda e: e.matmul(
                pAG[:, 128:256], kg_t[ga][:, csl], qg_t[ga][:, csl],
                start=True, stop=False),
                reads=[kg_b[ga], qg_b[ga]], writes=[pAG_b], accum=True)
            yield
            kb.op("pe", lambda e: e.matmul(
                pAG[:, 128:256], kg_t[gb][:, csl], qg_t[gb][:, csl],
                start=False, stop=True),
                reads=[kg_b[gb], qg_b[gb]], writes=[pAG_b], accum=True)
            yield
            kb.op("dve", lambda e, at_t=at_t: e.tensor_tensor(
                at_t[:], pAG[:, 128:256], ui_mask[:], ALU.mult),
                reads=[pAG_b, ui_mask_b], writes=[at_b])
            yield

        def chain_gen(i, c, h):
            s2 = i % 2
            cn, cp = cn2[s2], cp2[s2]
            rsbbf_t, rsbbf_b = rsbbf2[s2]
            kb.op("dve", lambda e, cp=cp: e.tensor_tensor(
                rsb_t[:], cp[0][0][:], eye_f[:], ALU.add),
                reads=[cp[0][1], eye_f_b], writes=[rsb_b])
            yield
            kb.op("pe", lambda e: e.matmul(
                pR[:], eye_f[:], rsb_t[:], start=True, stop=False),
                reads=[eye_f_b, rsb_b], writes=[pR_b])
            yield
            cur = 0
            for lev in range(1, 7):
                nxt = 1 - cur
                kb.op("pe", lambda e, cur=cur, cp=cp, cn=cn: e.matmul(
                    psq[:], cp[cur][0][:], cn[cur][0][:],
                    start=True, stop=True),
                    reads=[cp[cur][1], cn[cur][1]], writes=[psq_b])
                yield
                kb.op("dve", lambda e, nxt=nxt, cn=cn: e.tensor_copy(
                    cn[nxt][0][:], psq[:]),
                    reads=[psq_b], writes=[cn[nxt][1]])
                yield
                if lev < 6:
                    kb.op("pe", lambda e, cur=cur, cp=cp, cn=cn: e.matmul(
                        psq[:], cn[cur][0][:], cp[cur][0][:],
                        start=True, stop=True),
                        reads=[cn[cur][1], cp[cur][1]], writes=[psq_b])
                    yield
                    kb.op("dve", lambda e, nxt=nxt, cp=cp: e.tensor_copy(
                        cp[nxt][0][:], psq[:]),
                        reads=[psq_b], writes=[cp[nxt][1]])
                    yield
                cur = nxt
                kb.op("pe", lambda e, cur=cur, cn=cn, last=(lev == 6):
                      e.matmul(pR[:], cn[cur][0][:], rsb_t[:],
                               start=False, stop=last),
                      reads=[cn[cur][1], rsb_b], writes=[pR_b], accum=True)
                yield
                kb.op("dve", lambda e: e.tensor_copy(rsb_t[:], pR[:]),
                      reads=[pR_b], writes=[rsb_b])
                yield
            kb.op("dve", lambda e, rsbbf_t=rsbbf_t: e.tensor_copy(
                rsbbf_t[:], rsb_t[:]),
                reads=[rsb_b], writes=[rsbbf_b])
            yield

        def post_gen(i, c, h):
            s3, s2 = i % 3, i % 2
            ga, gb = 2 * h, 2 * h + 1
            co = (c % 4) * 128
            csl = slice(co, co + 128)
            ktm_t, ktm_b = ktm3[s3]
            kbm_t, kbm_b = kbm3[s3]
            vcm_t, vcm_b = vcm3[s3]
            rq_t, rq_b = rq3[s3]
            at_t, at_b = at3[s3]
            rsbbf_t, rsbbf_b = rsbbf2[s2]
            first = (c == 0)
            if not first:
                kb.op("dve", lambda e, h=h: e.tensor_copy(
                    Sbf_t[:, 0:DH], S_t[2 * h][:]),
                    reads=[S_b[2 * h]], writes=[Sbf_b])
                yield
                kb.op("dve", lambda e, h=h: e.tensor_copy(
                    Sbf_t[:, DH : 2 * DH], S_t[2 * h + 1][:]),
                    reads=[S_b[2 * h + 1]], writes=[Sbf_b], accum=True)
                yield
            kb.op("pe", lambda e, first=first, rsbbf_t=rsbbf_t,
                  vcm_t=vcm_t: e.matmul(
                      pUO[:], rsbbf_t[:], vcm_t[:], start=True, stop=first),
                  reads=[rsbbf_b, vcm_b], writes=[pUO_b])
            yield
            if not first:
                kb.op("pe", lambda e, kbm_t=kbm_t, rsbbf_t=rsbbf_t:
                      e.matmul(pW[:, 0:128], kbm_t[:, 0:128], rsbbf_t[:],
                               start=True, stop=True),
                      reads=[kbm_b, rsbbf_b], writes=[pW_b])
                yield
                kb.op("pe", lambda e, kbm_t=kbm_t, rsbbf_t=rsbbf_t:
                      e.matmul(pW[:, 128:256], kbm_t[:, 128:256],
                               rsbbf_t[:], start=True, stop=True),
                      reads=[kbm_b, rsbbf_b], writes=[pW_b], accum=True)
                yield
                kb.op("dve", lambda e: e.tensor_copy(nw_t[:], pW[:]),
                      reads=[pW_b], writes=[nw_b])
                yield
                kb.op("pe", lambda e: e.matmul(
                    pUO[:], nw_t[:, 0:128], Sbf_t[:, 0:DH],
                    start=False, stop=False),
                    reads=[nw_b, Sbf_b], writes=[pUO_b], accum=True)
                yield
                kb.op("pe", lambda e: e.matmul(
                    pUO[:], nw_t[:, 128:256], Sbf_t[:, DH : 2 * DH],
                    start=False, stop=True),
                    reads=[nw_b, Sbf_b], writes=[pUO_b], accum=True)
                yield
            kb.op("dve", lambda e, rq_t=rq_t: e.tensor_scalar(
                uib_t[:], pUO[:], rq_t[:, 1:2], None, ALU.mult),
                reads=[pUO_b], sreads=[rq_b], writes=[uib_b])
            yield
            if not first:
                kb.op("pe", lambda e: e.matmul(
                    pUO[:], qg_t[ga][:, csl], Sbf_t[:, 0:DH],
                    start=True, stop=False),
                    reads=[qg_b[ga], Sbf_b], writes=[pUO_b])
                yield
                kb.op("pe", lambda e: e.matmul(
                    pUO[:], qg_t[gb][:, csl], Sbf_t[:, DH : 2 * DH],
                    start=False, stop=False),
                    reads=[qg_b[gb], Sbf_b], writes=[pUO_b], accum=True)
                yield
            kb.op("pe", lambda e, first=first, at_t=at_t: e.matmul(
                pUO[:], at_t[:], uib_t[:], start=first, stop=True),
                reads=[at_b, uib_b], writes=[pUO_b], accum=(not first))
            yield
            kb.op("dve", lambda e, rq_t=rq_t: e.tensor_scalar(
                otmp_t[:], pUO[:], rq_t[:, 0:1], None, ALU.mult),
                reads=[pUO_b], sreads=[rq_b], writes=[otmp_b])
            yield
            kb.op("pe", lambda e: e.transpose(
                pOT[:, 0:128], otmp_t[:, 0:128], eye_bf[:]),
                reads=[otmp_b, eye_bf_b], writes=[pOT_b])
            yield
            kb.op("pe", lambda e: e.transpose(
                pOT[:, 128:256], otmp_t[:, 128:256], eye_bf[:]),
                reads=[otmp_b, eye_bf_b], writes=[pOT_b], accum=True)
            yield
            kb.op("dve", lambda e, c=c: e.tensor_copy(
                dl_t[ga][:, c * 128 : c * 128 + 128], pOT[:, 0:128]),
                reads=[pOT_b], writes=[dl_b[ga]], accum=True)
            yield
            kb.op("dve", lambda e, c=c: e.tensor_copy(
                dl_t[gb][:, c * 128 : c * 128 + 128], pOT[:, 128:256]),
                reads=[pOT_b], writes=[dl_b[gb]], accum=True)
            yield
            if c < NCH - 1:
                kb.op("pe", lambda e, ktm_t=ktm_t: e.matmul(
                    pS[:, 0:256], ktm_t[:, 0:128], uib_t[:],
                    start=True, stop=True),
                    reads=[ktm_b, uib_b], writes=[pS_b])
                yield
                kb.op("pe", lambda e, ktm_t=ktm_t: e.matmul(
                    pS[:, 256:512], ktm_t[:, 128:256], uib_t[:],
                    start=False, stop=True),
                    reads=[ktm_b, uib_b], writes=[pS_b], accum=True)
                yield
                kb.op("dve", lambda e, h=h: e.tensor_tensor(
                    S_t[2 * h][:], S_t[2 * h][:], pS[:, 0:256], ALU.add),
                    reads=[S_b[2 * h], pS_b], writes=[S_b[2 * h]],
                    accum=True)
                yield
                kb.op("dve", lambda e, h=h: e.tensor_tensor(
                    S_t[2 * h + 1][:], S_t[2 * h + 1][:], pS[:, 256:512],
                    ALU.add),
                    reads=[S_b[2 * h + 1], pS_b], writes=[S_b[2 * h + 1]],
                    accum=True)
                yield

        gidx = 0
        for gq in range(4):
            conv_group(gq, "q", 0, cprq, cqw, qg_t, qg_b, "a_Wq", None)
            conv_group(gq, "k", G, cprk, ckw, kg_t, kg_b, "a_Wk", None)
            for go in range(G):
                if qf_ap is not None:
                    kb.dma("pool", kb.tap_stream,
                           qf_ap[go * 128 : (go + 1) * 128,
                                 gq * TBW : (gq + 1) * TBW],
                           qg_t[go][:], reads=[qg_b[go]])
                if kf_ap is not None:
                    kb.dma("pool", kb.tap_stream,
                           kf_ap[go * 128 : (go + 1) * 128,
                                 gq * TBW : (gq + 1) * TBW],
                           kg_t[go][:], reads=[kg_b[go]])
            its = [(gq * 4 + cc, h) for cc in range(4) for h in range(H)]
            M = len(its)
            for step in range(M + 2):
                batch = []
                if step < M:
                    batch.append(pre_gen(gidx + step, *its[step]))
                if 0 <= step - 1 < M:
                    batch.append(chain_gen(gidx + step - 1, *its[step - 1]))
                if 0 <= step - 2 < M:
                    batch.append(post_gen(gidx + step - 2, *its[step - 2]))
                while batch:
                    for g2 in list(batch):
                        try:
                            next(g2)
                        except StopIteration:
                            batch.remove(g2)
            gidx += M
        kb.tap("delta", [D, T], [(dl_t[g][:], dl_b[g],
                                  (lambda t, g=g: t[g * 128:(g + 1) * 128]))
                                 for g in range(G)])
        kb.tap("Sfin", [8 * 128, DH],
               [(S_t[i][:], S_b[i],
                 (lambda t, i=i: t[i * 128:(i + 1) * 128])) for i in range(8)])
        kb.barrier()

    with contextlib.suppress(_Skip), contextlib.ExitStack() as sB:
        if PHASE < 3:
            raise _Skip()
        lsb = SB(sB)
        ws = kb.stream("wstr")
        lo_t, lo_b, mi_t, mi_b = [], [], [], []
        for g in range(G):
            t, b = lsb(f"lo{g}", [128, T], BF16)
            lo_t.append(t)
            lo_b.append(b)
            t, b = lsb(f"mi{g}", [128, T], BF16)
            mi_t.append(t)
            mi_b.append(b)
        selb_t, selb_b = lsb("selb", [128, 14 * 132], BF16)
        kb.dma("pool", ws, selb_t[:], dr["a_selbank"], writes=[selb_b])
        psel_t, psel_b = [], []
        for s in range(16):
            t, b = lsb(f"psel{s}", [16, 128], BF16)
            kb.dma("pool", ws, t[:], dr["a_psel"][s], writes=[b])
            psel_t.append(t)
            psel_b.append(b)
        mixsel_t, mixsel_b, onsel_t, onsel_b = [], [], [], []
        hsel_t, hsel_b = [], []
        for g in range(G):
            t, b = lsb(f"mxs{g}", [4, 128], BF16)
            kb.dma("pool", ws, t[:], dr["a_mixsel"][g], writes=[b])
            mixsel_t.append(t)
            mixsel_b.append(b)
            t, b = lsb(f"ons{g}", [4, 128], BF16)
            kb.dma("pool", ws, t[:], dr["a_onsel"][g], writes=[b])
            onsel_t.append(t)
            onsel_b.append(b)
            t, b = lsb(f"hsl{g}", [128, 4], BF16)
            kb.dma("pool", ws, t[:], dr["a_hsel"][g], writes=[b])
            hsel_t.append(t)
            hsel_b.append(b)
        rb1_t, rb1_b = lsb("rb1", [128, 17], F32)
        kb.dma("sp", ws, rb1_t[:], dr["a_rb1"], writes=[rb1_b])
        rb2_t, rb2_b = lsb("rb2", [16, 1], F32)
        kb.dma("sp", ws, rb2_t[:], dr["a_rb2"], writes=[rb2_b])
        rw2_t, rw2_b = lsb("rw2", [128, 17 * 16], BF16)
        kb.dma("pool", ws, rw2_t[:].rearrange("p (m n) -> p m n", n=16),
               dr["a_rw2"].rearrange("m p n -> p m n"), writes=[rw2_b])
        clw, cmw = [], []
        for g in range(G):
            t, b = lsb(f"clw{g}", [128, 7], F32)
            kb.dma("sp", ws, t[:], dr["a_local_w"][g], writes=[b])
            clw.append((t, b))
            t, b = lsb(f"cmw{g}", [128, 25], F32)
            kb.dma("sp", ws, t[:], dr["a_mid_w"][g], writes=[b])
            cmw.append((t, b))
        feats_t, feats_b = lsb("feats", [128, T], BF16)
        sh32_t, sh32_b = lsb("sh32", [64, 64], BF16)
        kb.dma("pool", ws, sh32_t[:], dr["c_shift32"], writes=[sh32_b])
        lgbf_t, lgbf_b = lsb("lgbf", [16, T], BF16)
        pfm_t, pfm_b = lsb("pfm", [16, T], BF16)

        for g in range(G):
            for (out_t2, out_b2, cw2, K2) in ((lo_t[g], lo_b[g], clw[g], 7),
                                              (mi_t[g], mi_b[g], cmw[g], 25)):
                ct2, cb2 = cw2
                kb.op("dve", lambda e, g=g, ct2=ct2, o=out_t2, K2=K2:
                      e.tensor_scalar(o[:], v_t[g][:],
                                      ct2[:, K2 - 1 : K2], None, ALU.mult),
                      reads=[v_b[g], cb2], writes=[out_b2])
                for tau in range(1, K2):
                    kb.op("dve", lambda e, g=g, tau=tau, ct2=ct2, o=out_t2,
                          K2=K2: e.scalar_tensor_tensor(
                              o[:, tau:T], v_t[g][:, 0 : T - tau],
                              ct2[:, K2 - 1 - tau : K2 - tau], o[:, tau:T],
                              ALU.mult, ALU.add),
                          reads=[v_b[g], cb2, out_b2], writes=[out_b2],
                          accum=True)
        kb.tap("local", [D, T], [(lo_t[g][:], lo_b[g],
                                  (lambda t, g=g: t[g * 128:(g + 1) * 128]))
                                 for g in range(G)])
        kb.tap("mid", [D, T], [(mi_t[g][:], mi_b[g],
                                (lambda t, g=g: t[g * 128:(g + 1) * 128]))
                               for g in range(G)])

        branches = [(lo_t, lo_b), (mi_t, mi_b), (dl_t, dl_b), (v_t, v_b)]

        with contextlib.suppress(_Skip), contextlib.ExitStack() as sBs:
            if BSTAGE < 1:
                raise _Skip()
            lps1, lsb1 = PS(sBs), SB(sBs)
            pF, pF_b = lps1("pF", [128, TBW], F32)
            psh, psh_b = lps1("psh", [64, TBW], F32)
            prods = [lsb1(f"prd{i}", [128, TBW], BF16) for i in range(3)]
            msq_t, msq_b = lsb1("msq", [64, TBW], BF16)
            fcp_t, fcp_b = lsb1("fcp", [64, TBW], F32)
            kb.op("dve", lambda e: e.memset(feats_t[64:128, :], 0.0),
                  writes=[feats_b], accum=True)
            srows = _stat_rows()
            for tb in range(NTB):
                sl = slice(tb * TBW, (tb + 1) * TBW)
                nst = 0
                pi = 0
                for si, (kind, ops, r0, coef) in enumerate(srows):
                    for g in range(G):
                        h = g // 2
                        selap = selb_t[:, 132 * si + 3 - h :
                                       132 * si + 131 - h]
                        if kind == "mean":
                            bt, bb = branches[ops[0]]
                            rt_ap = bt[g][:, sl]
                            rdeps = [bb[g]]
                        else:
                            pt2, pb2 = prods[pi % 3]
                            pi += 1
                            if kind == "sq":
                                bt, bb = branches[ops[0]]
                                kb.op("act", lambda e, bt=bt, g=g, sl=sl,
                                      pt2=pt2, bb=bb: e.activation(
                                          pt2[:], bt[g][:, sl], AF.Square),
                                      reads=[bb[g]], writes=[pb2])
                            else:
                                at2, ab2 = branches[ops[0]]
                                bt2, bb2 = branches[ops[1]]
                                kb.op("dve", lambda e, at2=at2, bt2=bt2,
                                      g=g, sl=sl, pt2=pt2, ab2=ab2, bb2=bb2:
                                      e.tensor_tensor(
                                          pt2[:], at2[g][:, sl],
                                          bt2[g][:, sl], ALU.mult),
                                      reads=[ab2[g], bb2[g]], writes=[pb2])
                            rt_ap = pt2[:]
                            rdeps = [pb2]
                        kb.op("pe", lambda e, selap=selap, rt_ap=rt_ap,
                              nst=nst: e.matmul(
                                  pF[:], selap, rt_ap,
                                  start=(nst == 0), stop=(nst == 111)),
                              reads=[selb_b] + rdeps, writes=[pF_b],
                              accum=(nst > 0))
                        nst += 1
                kb.op("dve", lambda e: e.tensor_copy(fcp_t[:], pF[0:64, :]),
                      reads=[pF_b], writes=[fcp_b])
                kb.op("act", lambda e: e.activation(
                    msq_t[:], fcp_t[:], AF.Square,
                    scale=float(np.sqrt(DH / (DH - 1.0)))),
                    reads=[fcp_b], writes=[msq_b])
                kb.op("pe", lambda e: e.matmul(
                    psh[:], sh32_t[:], msq_t[:], start=True, stop=True),
                    reads=[sh32_b, msq_b], writes=[psh_b])
                kb.op("dve", lambda e, sl=sl: e.tensor_tensor(
                    feats_t[0:64, sl], fcp_t[:], psh[:], ALU.subtract),
                    reads=[fcp_b, psh_b], writes=[feats_b], accum=True)
            kb.tap("feats", [128, T], [(feats_t[:], feats_b, lambda t: t)])
            kb.barrier()

        with contextlib.suppress(_Skip), contextlib.ExitStack() as sBr:
            if BSTAGE < 2:
                raise _Skip()
            lps2, lsb2 = PS(sBr), SB(sBr)
            rw1s_t, rw1s_b = lsb2("rw1s", [128, 9 * 512], BF16)
            h1bf_t, h1bf_b = lsb2("h1bf", [128, TBW], BF16)
            pH = [lps2(f"pH{i}", [128, TBW], F32) for i in range(4)]
            pL = [lps2(f"pL{i}", [16, TBW], F32) for i in range(NTB)]
            rhs_all = xb_t + [feats_t]
            rhsb_all = xb_b + [feats_b]
            for mg in range(5):
                kb.dma("pool", kb.stream("rw1"),
                       rw1s_t[:].rearrange("p (kg n) -> p kg n", n=512),
                       dr["a_rw1"][mg].rearrange("kg p n -> p kg n"),
                       writes=[rw1s_b])
                nm_ch = 4 if mg < 4 else 1
                for tb in range(NTB):
                    sl = slice(tb * TBW, (tb + 1) * TBW)
                    for mi in range(nm_ch):
                        m = 4 * mg + mi
                        pt, pb = pH[mi]
                        for kg in range(9):
                            kb.op("pe", lambda e, kg=kg, mi=mi, pt=pt, sl=sl:
                                  e.matmul(
                                      pt[:],
                                      rw1s_t[:, kg * 512 + mi * 128 :
                                             kg * 512 + (mi + 1) * 128],
                                      rhs_all[kg][:, sl],
                                      start=(kg == 0), stop=(kg == 8)),
                                  reads=[rw1s_b, rhsb_all[kg]], writes=[pb],
                                  accum=(kg > 0))
                        kb.op("act", lambda e, m=m, pt=pt: e.activation(
                            h1bf_t[:], pt[:], AF.Silu,
                            bias=rb1_t[:, m : m + 1]),
                            reads=[pb, rb1_b], writes=[h1bf_b])
                        kb.op("pe", lambda e, m=m, tb=tb: e.matmul(
                            pL[tb][0][:], rw2_t[:, m * 16 : (m + 1) * 16],
                            h1bf_t[:], start=(m == 0), stop=(m == 16)),
                            reads=[rw2_b, h1bf_b], writes=[pL[tb][1]],
                            accum=(m > 0))
            for tb in range(NTB):
                sl = slice(tb * TBW, (tb + 1) * TBW)
                kb.op("dve", lambda e, tb=tb, sl=sl: e.tensor_scalar(
                    lgbf_t[:, sl], pL[tb][0][:], rb2_t[:], None, ALU.add),
                    reads=[pL[tb][1], rb2_b], writes=[lgbf_b], accum=True)
            kb.tap("logits", [16, T], [(lgbf_t[:], lgbf_b, lambda t: t)])
            kb.barrier()

        with contextlib.suppress(_Skip), contextlib.ExitStack() as sBm:
            if BSTAGE < 3:
                raise _Skip()
            lps3, lsb3 = PS(sBm), SB(sBm)
            ptt, ptt_b = lps3("ptt", [128, 512], BF16)
            pbc, pbc_b = lps3("pbc", [128, TBW], F32)
            prms, prms_b = lps3("prms", [4, 128], F32)
            pgn, pgn_b = lps3("pgn", [128, 128], F32)
            pwo, pwo_b = lps3("pwo", [128, TBW], F32)
            pwo2, pwo2_b = lps3("pwo2", [128, TBW], F32)
            ptm_sm, ptm_sm_b = lsb3("ptm_sm", [128, 16], F32)
            exp_sm, exp_sm_b = lsb3("exp_sm", [128, 16], F32)
            col_sm, col_sm_b = lsb3("col_sm", [128, 4], F32)
            pback_t, pback_b = lsb3("pback", [128, 16], BF16)
            for tc in range(NCH):
                kb.op("pe", lambda e, tc=tc: e.transpose(
                    ptt[:, 0:16],
                    lgbf_t[:, tc * 128 : (tc + 1) * 128],
                    eye_bf[0:16, 0:16]),
                    reads=[lgbf_b, eye_bf_b], writes=[ptt_b])
                kb.op("dve", lambda e: e.tensor_copy(
                    ptm_sm[:], ptt[:, 0:16]),
                    reads=[ptt_b], writes=[ptm_sm_b])
                for h in range(H):
                    hs = slice(4 * h, 4 * h + 4)
                    kb.op("dve", lambda e, hs=hs, h=h: e.tensor_reduce(
                        col_sm[:, h : h + 1], ptm_sm[:, hs], AX.X, ALU.max,
                        negate=True),
                        reads=[ptm_sm_b], writes=[col_sm_b], accum=True)
                for h in range(H):
                    hs = slice(4 * h, 4 * h + 4)
                    kb.op("act", lambda e, hs=hs, h=h: e.activation(
                        exp_sm[:, hs], ptm_sm[:, hs], AF.Exp,
                        bias=col_sm[:, h : h + 1]),
                        reads=[ptm_sm_b, col_sm_b], writes=[exp_sm_b],
                        accum=True)
                for h in range(H):
                    hs = slice(4 * h, 4 * h + 4)
                    kb.op("dve", lambda e, hs=hs, h=h: e.tensor_reduce(
                        col_sm[:, h : h + 1], exp_sm[:, hs], AX.X, ALU.add),
                        reads=[exp_sm_b], writes=[col_sm_b], accum=True)
                kb.op("dve", lambda e: e.reciprocal(col_sm[:], col_sm[:]),
                      reads=[col_sm_b], writes=[col_sm_b], accum=True)
                for h in range(H):
                    hs = slice(4 * h, 4 * h + 4)
                    kb.op("dve", lambda e, hs=hs, h=h: e.tensor_scalar(
                        exp_sm[:, hs], exp_sm[:, hs], col_sm[:, h : h + 1],
                        1.0 - 4 * FLOOR, ALU.mult, ALU.mult),
                        reads=[exp_sm_b], sreads=[col_sm_b],
                        writes=[exp_sm_b], accum=True)
                kb.op("dve", lambda e: e.tensor_scalar(
                    pback_t[:], exp_sm[:], FLOOR, None, ALU.add),
                    reads=[exp_sm_b], writes=[pback_b])
                if tc == 0 and "smdbg" in DBG:
                    smd = nc.dram_tensor("dbg_smdbg", [128, 52], F32,
                                         kind="ExternalOutput").ap()
                    kb.taps["smdbg"] = smd
                    kb.dma("pool", kb.tap_stream, smd[:, 0:16], ptm_sm[:],
                           reads=[ptm_sm_b])
                    kb.dma("pool", kb.tap_stream, smd[:, 16:32], exp_sm[:],
                           reads=[exp_sm_b])
                    kb.dma("pool", kb.tap_stream, smd[:, 32:36], col_sm[:],
                           reads=[col_sm_b])
                    kb.dma("pool", kb.tap_stream, smd[:, 36:52], pback_t[:],
                           reads=[pback_b])
                kb.op("pe", lambda e: e.transpose(
                    ptt[0:16, 128:256], pback_t[:], eye_bf[:]),
                    reads=[pback_b, eye_bf_b], writes=[ptt_b], accum=True)
                kb.op("dve", lambda e, tc=tc: e.tensor_copy(
                    pfm_t[:, tc * 128 : (tc + 1) * 128],
                    ptt[0:16, 128:256]),
                    reads=[ptt_b], writes=[pfm_b], accum=True)
            kb.tap("p", [16, T], [(pfm_t[:], pfm_b, lambda t: t)])

            pbcsb_t, pbcsb_b = lsb3("pbcsb", [128, TBW], BF16)
            mxt, mxt_b = lsb3("mxt", [128, TBW], BF16)
            sqm_t, sqm_b = lsb3("sqm", [128, 128], BF16)
            rmsbf_t, rmsbf_b = lsb3("rmsbf", [4, 128], BF16)
            rmsrow_t, rmsrow_b = lsb3("rmsrow", [4, 128], F32)
            mix_t = xb_t
            mix_b = xb_b
            for tb in range(NTB):
                sl = slice(tb * TBW, (tb + 1) * TBW)
                for h in range(H):
                    for i in range(4):
                        slot = 4 * h + i
                        kb.op("pe", lambda e, slot=slot, sl=sl: e.matmul(
                            pbc[:], psel_t[slot][:], pfm_t[:, sl],
                            start=True, stop=True),
                            reads=[psel_b[slot], pfm_b], writes=[pbc_b])
                        kb.op("dve", lambda e: e.tensor_copy(
                            pbcsb_t[:], pbc[:]),
                            reads=[pbc_b], writes=[pbcsb_b])
                        for g in (2 * h, 2 * h + 1):
                            bt, bb = branches[i]
                            if i == 0:
                                kb.op("dve", lambda e, g=g, bt=bt, sl=sl,
                                      bb=bb: e.tensor_tensor(
                                          mix_t[g][:, 0:TBW], bt[g][:, sl],
                                          pbcsb_t[:], ALU.mult),
                                      reads=[bb[g], pbcsb_b],
                                      writes=[mix_b[g]])
                            else:
                                kb.op("dve", lambda e, g=g, bt=bt, sl=sl,
                                      bb=bb: e.tensor_tensor(
                                          mxt[:], bt[g][:, sl], pbcsb_t[:],
                                          ALU.mult),
                                      reads=[bb[g], pbcsb_b], writes=[mxt_b])
                                kb.op("dve", lambda e, g=g: e.tensor_tensor(
                                    mix_t[g][:, 0:TBW], mix_t[g][:, 0:TBW],
                                    mxt[:], ALU.add),
                                    reads=[mix_b[g], mxt_b],
                                    writes=[mix_b[g]], accum=True)
                for stage in range(2):
                    selset = mixsel_t if stage == 0 else onsel_t
                    selsetb = mixsel_b if stage == 0 else onsel_b
                    for sc in range(4):
                        scl = slice(sc * 128, (sc + 1) * 128)
                        for g in range(G):
                            kb.op("act", lambda e, g=g, scl=scl:
                                  e.activation(sqm_t[:], mix_t[g][:, scl],
                                               AF.Square),
                                  reads=[mix_b[g]], writes=[sqm_b])
                            kb.op("pe", lambda e, g=g: e.matmul(
                                prms[:], hsel_t[g][:], sqm_t[:],
                                start=(g == 0), stop=(g == G - 1)),
                                reads=[hsel_b[g], sqm_b], writes=[prms_b],
                                accum=(g > 0))
                        kb.op("dve", lambda e: e.tensor_scalar(
                            rmsrow_t[:], prms[:], 1.0 / DH, None, ALU.mult),
                            reads=[prms_b], writes=[rmsrow_b])
                        kb.op("act", lambda e: e.activation(
                            rmsrow_t[:], rmsrow_t[:], AF.Sqrt, bias=1e-5),
                            reads=[rmsrow_b, constf(1e-5)],
                            writes=[rmsrow_b], accum=True)
                        kb.op("dve", lambda e: e.reciprocal(
                            rmsrow_t[:], rmsrow_t[:]),
                            reads=[rmsrow_b], writes=[rmsrow_b], accum=True)
                        kb.op("dve", lambda e: e.tensor_copy(
                            rmsbf_t[:], rmsrow_t[:]),
                            reads=[rmsrow_b], writes=[rmsbf_b])
                        for g in range(G):
                            kb.op("pe", lambda e, g=g, selset=selset:
                                  e.matmul(pgn[:], selset[g][:], rmsbf_t[:],
                                           start=True, stop=True),
                                  reads=[selsetb[g], rmsbf_b],
                                  writes=[pgn_b])
                            if stage == 0:
                                kb.op("dve", lambda e, g=g, scl=scl:
                                      e.tensor_tensor(
                                          mix_t[g][:, scl],
                                          mix_t[g][:, scl], pgn[:],
                                          ALU.mult),
                                      reads=[mix_b[g], pgn_b],
                                      writes=[mix_b[g]], accum=True)
                            else:
                                osl = slice(tb * TBW + sc * 128,
                                            tb * TBW + (sc + 1) * 128)
                                kb.op("dve", lambda e, g=g, scl=scl, osl=osl:
                                      e.tensor_tensor(
                                          dl_t[g][:, osl],
                                          mix_t[g][:, scl], pgn[:],
                                          ALU.mult),
                                      reads=[mix_b[g], pgn_b],
                                      writes=[dl_b[g]], accum=True)
            kb.tap("o_n", [D, T], [(dl_t[g][:], dl_b[g],
                                    (lambda t, g=g:
                                     t[g * 128:(g + 1) * 128]))
                                   for g in range(G)])

            with contextlib.suppress(_Skip), contextlib.ExitStack() as sWo:
                if BSTAGE < 4:
                    raise _Skip()
                lsbW = SB(sWo)
                wos = [lsbW("wo0", [128, D], BF16),
                       lsbW("wo1", [128, D], BF16)]
                for go in range(G):
                    wtile, wbuf = wos[go % 2]
                    kb.dma("pool", kb.stream(f"wq{go % 2}"),
                           wtile[:].rearrange("p (gi m) -> p gi m", m=128),
                           dr["a_Wo"][go].rearrange("gi p m -> p gi m"),
                           writes=[wbuf])
                    for tb in range(NTB):
                        pt, pb = (pwo, pwo_b) if tb % 2 == 0 else \
                            (pwo2, pwo2_b)
                        sl = slice(tb * TBW, (tb + 1) * TBW)
                        proj_mms(pt[:], pb, wtile, wbuf, dl_t, dl_b, sl)
                        kb.op("act", lambda e, go=go, sl=sl, pt=pt:
                              e.activation(xb_t[go][:, sl], pt[:],
                                           AF.Identity),
                              reads=[pb], writes=[xb_b[go]], accum=True)
                for g in range(G):
                    kb.dma("sp", kb.stream("x_io"), aout_sp[g], xb_t[g][:],
                           reads=[xb_b[g]])
                kb.tap("aout", [D, T], [(xb_t[g][:], xb_b[g],
                                         (lambda t, g=g:
                                          t[g * 128:(g + 1) * 128]))
                                        for g in range(G)])
            kb.barrier()
        kb.barrier()

    s1.close()

    if PHASE < 4:
        with contextlib.ExitStack() as sO:
            lsb4 = SB(sO)
            dum_t, dum_b = lsb4("dum", [128, D], F32)
            kb.op("dve", lambda e: e.memset(dum_t[:], 0.0), writes=[dum_b])
            outs = kb.stream("out_store")
            for tc in range(NCH):
                kb.dma("sp", outs, out_ap[tc * 128 : (tc + 1) * 128, :],
                       dum_t[:], reads=[dum_b])
        kb.finish()
        st.close()
        return kb

    sR = st.enter_context(contextlib.ExitStack())
    sbR = SB(sR)
    x2_t, x2_b, xb2_t, xb2_b = [], [], [], []
    for g in range(G):
        t, b = sbR(f"x2_{g}", [128, T], F32)
        x2_t.append(t)
        x2_b.append(b)
    for g in range(G):
        t, b = sbR(f"xb2_{g}", [128, T], BF16)
        xb2_t.append(t)
        xb2_b.append(b)
    xio = kb.stream("x_io")
    for g in range(G):
        kb.dma("sp", xio, x2_t[g][:], xsp[g], writes=[x2_b[g]])
        kb.dma("sp", xio, xb2_t[g][:], aout_sp[g], writes=[xb2_b[g]])
    for g in range(G):
        kb.op("dve", lambda e, g=g: e.tensor_tensor(
            x2_t[g][:], x2_t[g][:], xb2_t[g][:], ALU.add),
            reads=[x2_b[g], xb2_b[g]], writes=[x2_b[g]], accum=True)
    kb.tap("x1", [D, T], [(x2_t[g][:], x2_b[g],
                           (lambda t, g=g: t[g * 128:(g + 1) * 128]))
                          for g in range(G)])

    def ffn_phase(tag):
        with contextlib.ExitStack() as sF:
            lsb, lps = SB(sF), PS(sF)
            layernorm(tag, x2_t, x2_b, xb2_t, xb2_b, sF)
            ws = kb.stream("wstr")
            a1r1 = []
            for g in range(G):
                t, b = lsb(f"f{tag}a1r1_{g}", [128, 27], BF16)
                kb.dma("pool", ws, t[:], dr["f_a1r1"][g], writes=[b])
                a1r1.append((t, b))
            b2c_t, b2c_b = lsb(f"f{tag}b2c", [128, D], BF16)
            kb.dma("pool", ws, b2c_t[:], dr["f_b2cat"], writes=[b2c_b])
            b2_t, b2_b = lsb(f"f{tag}b2", [128, G], F32)
            kb.dma("sp", ws, b2_t[:], dr["f_b2"], writes=[b2_b])
            w1h_t, w1h_b = lsb(f"f{tag}w1h", [128, 9 * 2048], BF16)
            w2h_t, w2h_b = lsb(f"f{tag}w2h", [128, 16 * 1024], BF16)
            a2r2_t, a2r2_b = lsb(f"f{tag}a2r2", [128, 2 * 16 * 27], BF16)
            b1_t, b1_b = lsb(f"f{tag}b1", [128, 32], F32)
            kb.dma("sp", ws, b1_t[:], dr["f_b1"], writes=[b1_b])
            kb.dma("pool", ws, a2r2_t[:].rearrange("p (h m n) -> p h m n", m=16, n=27),
                   dr["f_a2r2"].rearrange("h m p n -> p h m n"),
                   writes=[a2r2_b])
            zgfm_t, zgfm_b = lsb(f"f{tag}zgfm", [128, T], BF16)
            zg2fm_t, zg2fm_b = lsb(f"f{tag}zg2fm", [128, 512], BF16)
            z1sb_t, z1sb_b = lsb(f"f{tag}z1sb", [128, 27], F32)
            gt_t, gt_b = lsb(f"f{tag}gt", [128, 4], F32)
            zgtm_t, zgtm_b = lsb(f"f{tag}zgtm", [128, 24], BF16)
            z2sb = [lsb(f"f{tag}z2_{i}", [128, 108], F32) for i in range(4)]
            hsb = [lsb(f"f{tag}hsb{m}", [128, 512], BF16) for m in range(16)]
            pZ, pZ_b = lps(f"f{tag}pZ", [128, 108], F32)
            pH1, pH1_b = lps(f"f{tag}pH1", [128, 512], F32)
            pH2, pH2_b = lps(f"f{tag}pH2", [128, 512], F32)
            pTZ, pTZ_b = lps(f"f{tag}pTZ", [128, 128], BF16)
            pOg = [lps(f"f{tag}pO{i}", [128, 512], F32) for i in range(2)]

            kb.op("dve", lambda e: e.memset(zgfm_t[:], 0.0), writes=[zgfm_b])
            kb.op("dve", lambda e: e.memset(zg2fm_t[:], 0.0),
                  writes=[zg2fm_b])

            def gates_zg(z_ap, z_deps, dst_ap, dst_b):
                kb.op("dve", lambda e: e.tensor_copy(z1sb_t[:], z_ap),
                      reads=z_deps, writes=[z1sb_b])
                kb.op("dve", lambda e: e.tensor_reduce(
                    gt_t[:, 3:4], z1sb_t[:, 24:27], AX.X, ALU.max,
                    negate=True),
                    reads=[z1sb_b], writes=[gt_b])
                kb.op("act", lambda e: e.activation(
                    gt_t[:, 0:3], z1sb_t[:, 24:27], AF.Exp,
                    bias=gt_t[:, 3:4]),
                    reads=[z1sb_b, gt_b], writes=[gt_b], accum=True)
                kb.op("dve", lambda e: e.tensor_reduce(
                    gt_t[:, 3:4], gt_t[:, 0:3], AX.X, ALU.add),
                    reads=[gt_b], writes=[gt_b], accum=True)
                kb.op("dve", lambda e: e.reciprocal(
                    gt_t[:, 3:4], gt_t[:, 3:4]),
                    reads=[gt_b], writes=[gt_b], accum=True)
                for ei in range(3):
                    kb.op("dve", lambda e, ei=ei: e.tensor_scalar(
                        zgtm_t[:, 8 * ei : 8 * ei + 8],
                        z1sb_t[:, 8 * ei : 8 * ei + 8],
                        gt_t[:, ei : ei + 1], gt_t[:, 3:4],
                        ALU.mult, ALU.mult),
                        reads=[z1sb_b], sreads=[gt_b], writes=[zgtm_b],
                        accum=True)
                kb.op("dve", lambda e: e.tensor_scalar(
                    zgtm_t[:], zgtm_t[:], 0.125, None, ALU.mult),
                    reads=[zgtm_b], writes=[zgtm_b], accum=True)
                kb.op("pe", lambda e: e.transpose(
                    pTZ[0:24, 0:128], zgtm_t[:], eye_bf[:]),
                    reads=[zgtm_b, eye_bf_b], writes=[pTZ_b])
                kb.op("act", lambda e: e.activation(
                    dst_ap, pTZ[0:24, 0:128], AF.Identity),
                    reads=[pTZ_b], writes=[dst_b], accum=True)

            for tc in range(NCH):
                sl1 = slice(tc * 128, (tc + 1) * 128)
                for gi in range(G):
                    t2, b2cc = a1r1[gi]
                    kb.op("pe", lambda e, gi=gi, sl1=sl1, t2=t2: e.matmul(
                        pZ[:, 0:27], xb2_t[gi][:, sl1], t2[:],
                        start=(gi == 0), stop=(gi == G - 1)),
                        reads=[b2cc, xb2_b[gi]], writes=[pZ_b],
                        accum=(gi > 0))
                gates_zg(pZ[:, 0:27], [pZ_b],
                         zgfm_t[0:24, tc * 128 : (tc + 1) * 128], zgfm_b)

            kb.tap(f"zgfm{tag}", [128, T],
                   [(zgfm_t[:], zgfm_b, lambda t: t)])
            rhs9 = xb2_t + [zgfm_t]
            rhs9b = xb2_b + [zgfm_b]
            for half in range(2):
                kb.dma("pool", kb.stream("fw1"),
                       w1h_t[:].rearrange("p (kg n) -> p kg n", n=2048),
                       dr["f_w1"][half].rearrange("kg p n -> p kg n"),
                       writes=[w1h_b])
                kb.dma("pool", kb.stream("fw2"),
                       w2h_t[:].rearrange("p (m n) -> p m n", n=1024),
                       dr["f_w2"][half].rearrange("m p n -> p m n"),
                       writes=[w2h_b])
                for tb in range(4):
                    sl = slice(tb * 512, (tb + 1) * 512)
                    z2t, z2b = z2sb[tb]
                    def z2_mms(m, ht, hb, half=half):
                        for ts2 in range(4):
                            kb.op("pe", lambda e, ts2=ts2, half=half, m=m,
                                  ht=ht: e.matmul(
                                      pZ[:, 27 * ts2 : 27 * ts2 + 27],
                                      ht[:, 128 * ts2 : 128 * ts2 + 128],
                                      a2r2_t[:, (16 * half + m) * 27 :
                                             (16 * half + m + 1) * 27],
                                      start=(m == 0 and ts2 == 0),
                                      stop=(m == 15 and ts2 == 3)),
                                  reads=[hb, a2r2_b], writes=[pZ_b],
                                  accum=(not (m == 0 and ts2 == 0)))
                    pend = None
                    for m in range(16):
                        pt, pb = (pH1, pH1_b) if m % 2 == 0 else (pH2, pH2_b)
                        ht, hb = hsb[m]
                        for kg in range(9):
                            kb.op("pe", lambda e, kg=kg, m=m, sl=sl, pt=pt:
                                  e.matmul(
                                      pt[:],
                                      w1h_t[:, kg * 2048 + m * 128 :
                                            kg * 2048 + (m + 1) * 128],
                                      rhs9[kg][:, sl],
                                      start=(kg == 0), stop=(kg == 8)),
                                  reads=[w1h_b, rhs9b[kg]], writes=[pb],
                                  accum=(kg > 0))
                        if pend is not None:
                            z2_mms(*pend)
                        kb.op("act", lambda e, half=half, m=m, pt=pt, ht=ht:
                              e.activation(
                                  ht[:], pt[:], AF.Relu,
                                  bias=b1_t[:, 16 * half + m :
                                            16 * half + m + 1]),
                              reads=[pb, b1_b], writes=[hb])
                        pend = (m, ht, hb)
                    z2_mms(*pend)
                    if half == 0:
                        kb.op("dve", lambda e, z2t=z2t: e.tensor_copy(
                            z2t[:], pZ[:]), reads=[pZ_b], writes=[z2b])
                    else:
                        kb.op("dve", lambda e, z2t=z2t: e.tensor_tensor(
                            z2t[:], z2t[:], pZ[:], ALU.add),
                            reads=[z2b, pZ_b], writes=[z2b], accum=True)
                        for ts2 in range(4):
                            gates_zg(z2t[:, 27 * ts2 : 27 * ts2 + 27], [z2b],
                                     zg2fm_t[0:24, 128 * ts2 :
                                             128 * ts2 + 128], zg2fm_b)
                    for go in range(G):
                        pt2, pb2 = pOg[go % 2]
                        for m in range(16):
                            kb.op("pe", lambda e, go=go, m=m, pt2=pt2,
                                  half=half: e.matmul(
                                      pt2[:],
                                      w2h_t[:, m * 1024 + go * 128 :
                                            m * 1024 + (go + 1) * 128],
                                      hsb[m][0][:],
                                      start=(m == 0),
                                      stop=(half == 0 and m == 15)),
                                  reads=[w2h_b, hsb[m][1]],
                                  writes=[pb2], accum=(m > 0))
                        if half == 0:
                            kb.op("dve", lambda e, go=go, sl=sl, pt2=pt2:
                                  e.tensor_tensor(
                                      x2_t[go][:, sl], x2_t[go][:, sl],
                                      pt2[:], ALU.add),
                                  reads=[pb2, x2_b[go]], writes=[x2_b[go]],
                                  accum=True)
                        else:
                            kb.op("pe", lambda e, go=go, pt2=pt2: e.matmul(
                                      pt2[:],
                                      b2c_t[:, go * 128 : (go + 1) * 128],
                                      zg2fm_t[:],
                                      start=False, stop=True),
                                  reads=[b2c_b, zg2fm_b], writes=[pb2],
                                  accum=True)
                            kb.op("dve", lambda e, go=go, sl=sl, pt2=pt2:
                                  e.scalar_tensor_tensor(
                                      x2_t[go][:, sl], pt2[:],
                                      b2_t[:, go : go + 1],
                                      x2_t[go][:, sl], ALU.add, ALU.add),
                                  reads=[pb2, b2_b, x2_b[go]],
                                  writes=[x2_b[go]], accum=True)
            kb.tap(f"x_ffn{tag}", [D, T],
                   [(x2_t[g][:], x2_b[g],
                     (lambda t, g=g: t[g * 128:(g + 1) * 128]))
                    for g in range(G)])
            kb.barrier()

    if PHASE >= 4:
        ffn_phase("2")

    def attnb_phase():
        qb_t, qb_b, kbh_t, kbh_b = [], [], [], []
        with contextlib.ExitStack() as sD:
            lsb, lps = SB(sD), PS(sD)
            layernorm("3", x2_t, x2_b, xb2_t, xb2_b, sD)
            ws = kb.stream("wstr")
            for g in range(G):
                t, b = lsb(f"bq{g}", [128, T], BF16)
                qb_t.append(t)
                qb_b.append(b)
            for g in range(4):
                t, b = lsb(f"bk{g}", [128, T], BF16)
                kbh_t.append(t)
                kbh_b.append(b)
            vtm_t, vtm_b = [], []
            for kvh in range(KVH):
                t, b = lsb(f"bvt{kvh}", [128, NCH * 65], BF16)
                vtm_t.append(t)
                vtm_b.append(b)
                kb.op("dve", lambda e, t=t: e.memset(t[:], 1.0), writes=[b])
            wst = [lsb("bwst0", [128, D], BF16), lsb("bwst1", [128, D], BF16)]
            vfs_t, vfs_b = lsb("bvfs", [128, TBW], BF16)
            sqb_t, sqb_b = lsb("bsqb", [128, TBW], BF16)
            nrm_t, nrm_b = lsb("bnrm", [128, 32], F32)
            nbb_t, nbb_b = lsb("bnbb", [128, 16], F32)
            wcol_t, wcol_b = lsb("bwcol", [128, 2], F32)
            mw_t, mw_b = lsb("bmw", [1, 1], F32)
            mwb_t, mwb_b = lsb("bmwb", [1, 1], BF16)
            ones2_t, ones2_b = lsb("bones2", [128, 2], BF16)
            e8_t, e8_b = lsb("be8", [32, 16], BF16)
            kb.dma("sp", ws, mw_t[:], dr["b_mw"], writes=[mw_b])
            kb.dma("pool", ws, ones2_t[:], dr["c_ones2"], writes=[ones2_b])
            kb.dma("pool", ws, e8_t[:], dr["b_e8"], writes=[e8_b])

            sD1 = contextlib.ExitStack()
            lps1 = PS(sD1)
            pprj, pprj_b = lps1("bpprj", [128, TBW], F32)
            pprj2, pprj2_b = lps1("bpprj2", [128, TBW], F32)
            ptv, ptv_b = lps1("bptv", [128, 1024], BF16)
            pnrm, pnrm_b = lps1("bpnrm", [128, 32], F32)

            def stream_proj(wnm, ngo, outfn):
                for go in range(ngo):
                    wtile, wbuf = wst[go % 2]
                    kb.dma("pool", ws, wtile[:].rearrange("p (gi m) -> p gi m", m=128),
                           dr[wnm][go].rearrange("gi p m -> p gi m"),
                           writes=[wbuf])
                    for tb in range(NTB):
                        pt, pb = (pprj, pprj_b) if tb % 2 == 0 else \
                            (pprj2, pprj2_b)
                        sl = slice(tb * TBW, (tb + 1) * TBW)
                        proj_mms(pt[:], pb, wtile, wbuf, xb2_t, xb2_b, sl)
                        outfn(go, tb, sl, pt, pb)

            stream_proj("b_Wq", G, lambda go, tb, sl, pt, pb: kb.op(
                "act", lambda e, go=go, sl=sl, pt=pt: e.activation(
                    qb_t[go][:, sl], pt[:], AF.Identity),
                reads=[pb], writes=[qb_b[go]], accum=True))
            stream_proj("b_Wk", 4, lambda go, tb, sl, pt, pb: kb.op(
                "act", lambda e, go=go, sl=sl, pt=pt: e.activation(
                    kbh_t[go][:, sl], pt[:], AF.Identity),
                reads=[pb], writes=[kbh_b[go]], accum=True))

            def v_out(go, tb, sl, pt, pb):
                kb.op("act", lambda e, pt=pt: e.activation(
                    vfs_t[:], pt[:], AF.Identity),
                    reads=[pb], writes=[vfs_b])
                for j in range(4):
                    tc = tb * 4 + j
                    kb.op("pe", lambda e, j=j: e.transpose(
                        ptv[:, j * 256 : j * 256 + 128],
                        vfs_t[:, j * 128 : (j + 1) * 128], eye_bf[:]),
                        reads=[vfs_b, eye_bf_b], writes=[ptv_b], accum=True)
                    for half in range(2):
                        kvh = 2 * go + half
                        kb.op("dve", lambda e, j=j, half=half, kvh=kvh,
                              tc=tc: e.tensor_copy(
                                  vtm_t[kvh][:, tc * 65 : tc * 65 + 64],
                                  ptv[:, j * 256 + half * 64 :
                                      j * 256 + half * 64 + 64]),
                              reads=[ptv_b], writes=[vtm_b[kvh]],
                              accum=True)
            stream_proj("b_Wv", 4, v_out)

            kb.op("dve", lambda e: e.memset(nrm_t[:], 0.0), writes=[nrm_b])
            for src_t, src_b, ncols, base in ((qb_t, qb_b, G, 0),
                                              (kbh_t, kbh_b, 4, 16)):
                for g in range(ncols):
                    for tb in range(NTB):
                        sl = slice(tb * TBW, (tb + 1) * TBW)
                        kb.op("act", lambda e, g=g, sl=sl, src_t=src_t:
                              e.activation(sqb_t[:], src_t[g][:, sl],
                                           AF.Square),
                              reads=[src_b[g]], writes=[sqb_b])
                        for j in range(4):
                            kb.op("pe", lambda e, j=j, g=g, base=base:
                                  e.matmul(
                                      pnrm[:, base + 2 * g :
                                           base + 2 * g + 2],
                                      sqb_t[:, j * 128 : (j + 1) * 128],
                                      ones2_t[:],
                                      start=True, stop=True),
                                  reads=[sqb_b, ones2_b], writes=[pnrm_b])
                            kb.op("dve", lambda e, g=g, base=base:
                                  e.tensor_tensor(
                                      nrm_t[:, base + 2 * g :
                                            base + 2 * g + 2],
                                      nrm_t[:, base + 2 * g :
                                            base + 2 * g + 2],
                                      pnrm[:, base + 2 * g :
                                           base + 2 * g + 2], ALU.max),
                                  reads=[nrm_b, pnrm_b], writes=[nrm_b],
                                  accum=True)
            kb.op("dve", lambda e: e.tensor_copy(sqb_t[:, 0:32], nrm_t[:]),
                  reads=[nrm_b], writes=[sqb_b])
            kb.op("pe", lambda e: e.transpose(
                ptv[0:32, 0:128], sqb_t[:, 0:32], eye_bf[:]),
                reads=[sqb_b, eye_bf_b], writes=[ptv_b])
            kb.op("dve", lambda e: e.tensor_reduce(
                nrm_t[0:32, 0:1], ptv[0:32, 0:128], AX.X, ALU.max),
                reads=[ptv_b], writes=[nrm_b])
            kb.op("act", lambda e: e.activation(
                nrm_t[0:32, 0:1], nrm_t[0:32, 0:1], AF.Sqrt),
                reads=[nrm_b], writes=[nrm_b], accum=True)
            kb.op("dve", lambda e: e.tensor_copy(
                sqb_t[0:32, 0:1], nrm_t[0:32, 0:1]),
                reads=[nrm_b], writes=[sqb_b])
            kb.op("pe", lambda e: e.matmul(
                pnrm[0:16, 2:3], e8_t[:], sqb_t[0:32, 0:1],
                start=True, stop=True),
                reads=[e8_b, sqb_b], writes=[pnrm_b])
            kb.op("dve", lambda e: e.tensor_tensor(
                nrm_t[0:16, 1:2], nrm_t[0:16, 0:1], pnrm[0:16, 2:3],
                ALU.mult),
                reads=[nrm_b, pnrm_b], writes=[nrm_b], accum=True)
            kb.op("dve", lambda e: e.tensor_scalar(
                nrm_t[0:16, 2:3], nrm_t[0:16, 1:2], -0.125, -1.0,
                ALU.mult, ALU.add),
                reads=[nrm_b], writes=[nrm_b], accum=True)
            kb.op("dve", lambda e: e.tensor_copy(
                sqb_t[0:16, 0:1], nrm_t[0:16, 2:3]),
                reads=[nrm_b], writes=[sqb_b])
            kb.op("pe", lambda e: e.transpose(
                ptv[0:1, 0:16], sqb_t[0:16, 0:1], eye_bf[0:16, 0:16]),
                reads=[sqb_b, eye_bf_b], writes=[ptv_b], accum=True)
            kb.op("dve", lambda e: e.tensor_copy(
                sqb_t[0:1, 0:16], ptv[0:1, 0:16]),
                reads=[ptv_b], writes=[sqb_b])
            kb.op("pe", lambda e: e.matmul(
                pnrm[:, 16:32], ones_row[:], sqb_t[0:1, 0:16],
                start=True, stop=True),
                reads=[ones_row_b, sqb_b], writes=[pnrm_b])
            kb.op("dve", lambda e: e.tensor_copy(nbb_t[:], pnrm[:, 16:32]),
                  reads=[pnrm_b], writes=[nbb_b])
            kb.op("act", lambda e: e.activation(
                mwb_t[:], mw_t[:], AF.Sigmoid), reads=[mw_b], writes=[mwb_b])
            kb.op("pe", lambda e: e.matmul(
                pnrm[:, 0:1], ones_row[:], mwb_t[:], start=True, stop=True),
                reads=[ones_row_b, mwb_b], writes=[pnrm_b])
            kb.op("dve", lambda e: e.tensor_copy(
                wcol_t[:, 0:1], pnrm[:, 0:1]),
                reads=[pnrm_b], writes=[wcol_b])
            kb.op("dve", lambda e: e.tensor_scalar(
                wcol_t[:, 1:2], wcol_t[:, 0:1], -1.0, 1.0,
                ALU.mult, ALU.add),
                reads=[wcol_b], writes=[wcol_b], accum=True)
            kb.tap("bounds", [128, 16], [(nbb_t[:], nbb_b, lambda t: t)])
            kb.barrier()
            sD1.close()

            with contextlib.ExitStack() as sD2:
                lsb2, lps2 = SB(sD2), PS(sD2)
                mem_tiles = []
                for kc in range(4):
                    t, b = lsb2(f"bmem{kc}", [128, KV], BF16)
                    kb.dma("pool", ws, t[:], dr["b_mem"][kc], writes=[b])
                    mem_tiles.append((t, b))
                memn_tiles = []
                for kc in range(4):
                    t, b = lsb2(f"bmemn{kc}", [128, 1], BF16)
                    kb.dma("pool", ws, t[:], dr["b_memn"][kc], writes=[b])
                    memn_tiles.append((t, b))
                el1_t, el1_b = lsb2("bel1", [128, T], BF16)
                numt_t, numt_b = lsb2("bnum", [128, NCH * KV], BF16)
                rden_t, rden_b = lsb2("brden", [128, NCH], F32)
                etile2 = [lsb2(f"betile{i}", [128, T], BF16)
                          for i in range(2)]
                asb_t, asb_b = lsb2("basb", [128, NCH * 65], F32)
                rdn_t, rdn_b = lsb2("brdn", [128, NCH], F32)
                t1_t, t1_b = lsb2("bt1", [128, DHB], BF16)
                t2_t, t2_b = lsb2("bt2", [128, DHB], BF16)
                psc = [lps2("bpsc0", [128, TBW], F32),
                       lps2("bpsc1", [128, TBW], F32)]
                pad3 = [lps2(f"bpad{k}", [128, 512], F32) for k in range(3)]
                pnum, pnum_b = lps2("bpnum", [128, KV], F32)
                pden, pden_b = lps2("bpden", [128, NCH], F32)
                ptrx, ptrx_b = lps2("bptrx", [128, 128], BF16)

                qloc = {}
                for j in range(G):
                    qloc[QA[j]] = (j, 0)
                    qloc[QB[j]] = (j, 64)

                sigma_t = xb2_t[:4]
                sigma_b = xb2_b[:4]
                for nq in range(2):
                    for kc in range(4):
                        for half in range(2):
                            o = 8 * nq + 2 * kc + half
                            j, base = qloc[o]
                            qsl = qb_t[j][base : base + 64, :]
                            e1 = el1_t[base : base + 64, :]
                            dsl = sigma_t[kc][64 * half : 64 * half + 64, :]
                            kb.op("dve", lambda e, qsl=qsl, e1=e1:
                                  e.tensor_scalar(
                                      e1, qsl, 0.0, None, ALU.min),
                                  reads=[qb_b[j]], writes=[el1_b])
                            kb.op("act", lambda e, e1=e1: e.activation(
                                e1, e1, AF.Exp),
                                reads=[el1_b], writes=[el1_b], accum=True)
                            kb.op("dve", lambda e, qsl=qsl, e1=e1:
                                  e.scalar_tensor_tensor(
                                      e1, qsl, 0.0, e1, ALU.max, ALU.add),
                                  reads=[qb_b[j], el1_b], writes=[el1_b],
                                  accum=True)
                            kb.op("dve", lambda e, e1=e1, dsl=dsl:
                                  e.tensor_scalar(
                                      dsl, e1, -1.0, None, ALU.add),
                                  reads=[el1_b], writes=[sigma_b[kc]],
                                  accum=True)
                    for tc in range(NCH):
                        tsl = slice(tc * 128, (tc + 1) * 128)
                        for kc in range(4):
                            mt, mb = mem_tiles[kc]
                            kb.op("pe", lambda e, kc=kc, tsl=tsl, mt=mt:
                                  e.matmul(pnum[:], sigma_t[kc][:, tsl],
                                           mt[:], start=(kc == 0),
                                           stop=(kc == 3)),
                                  reads=[sigma_b[kc], mb], writes=[pnum_b],
                                  accum=(kc > 0))
                            nt, nb = memn_tiles[kc]
                            kb.op("pe", lambda e, kc=kc, tc=tc, tsl=tsl,
                                  nt=nt: e.matmul(
                                      pden[:, tc : tc + 1],
                                      sigma_t[kc][:, tsl], nt[:],
                                      start=(kc == 0), stop=(kc == 3)),
                                  reads=[sigma_b[kc], nb], writes=[pden_b],
                                  accum=(kc > 0 or tc > 0))
                        kb.op("dve", lambda e, tc=tc: e.tensor_copy(
                            numt_t[:, tc * KV : (tc + 1) * KV], pnum[:]),
                            reads=[pnum_b], writes=[numt_b], accum=True)
                    kb.op("dve", lambda e: e.reciprocal(
                        rden_t[:], pden[:]),
                        reads=[pden_b], writes=[rden_b])
                    for qh8 in range(8):
                        o = 8 * nq + qh8
                        kvh = o // 2
                        j, base = qloc[o]
                        slot = 2 * j + (0 if base == 0 else 1)
                        for tc in range(NCH):
                            tsl = slice(tc * 128, (tc + 1) * 128)
                            etile_t, etile_b = etile2[tc % 2]
                            for sb2 in range(4):
                                ssl = slice(sb2 * TBW, (sb2 + 1) * TBW)
                                pt, pb = psc[sb2 % 2]
                                kb.op("pe", lambda e, tsl=tsl, ssl=ssl,
                                      j=j, base=base, kvh=kvh, pt=pt:
                                      e.matmul(
                                          pt[:],
                                          kbh_t[kvh // 2][
                                              (kvh % 2) * 64 :
                                              (kvh % 2) * 64 + 64, tsl],
                                          qb_t[j][base : base + 64, ssl],
                                          start=True, stop=True),
                                      reads=[kbh_b[kvh // 2], qb_b[j]],
                                      writes=[pb])
                                kb.op("act", lambda e, ssl=ssl, pt=pt,
                                      slot=slot: e.activation(
                                          etile_t[:, ssl], pt[:], AF.Exp,
                                          bias=nbb_t[:, slot : slot + 1],
                                          scale=0.125),
                                      reads=[pb, nbb_b], writes=[etile_b],
                                      accum=True)
                            for sc in range(NCH):
                                pk_t, pk_b = pad3[sc // 7]
                                off = (sc % 7) * 65
                                first = (tc == 0 and sc % 7 == 0)
                                kb.op("pe", lambda e, sc=sc, tc=tc, kvh=kvh,
                                      first=first, pk_t=pk_t, off=off:
                                      e.matmul(
                                          pk_t[:, off : off + 65],
                                          etile_t[:, sc * 128 :
                                                  (sc + 1) * 128],
                                          vtm_t[kvh][:, tc * 65 :
                                                     (tc + 1) * 65],
                                          start=first,
                                          stop=(tc == NCH - 1)),
                                      reads=[etile_b, vtm_b[kvh]],
                                      writes=[pk_b], accum=(not first))
                        for k3 in range(3):
                            w3 = min(455, NCH * 65 - 455 * k3)
                            kb.op("dve", lambda e, k3=k3, w3=w3:
                                  e.tensor_copy(
                                      asb_t[:, 455 * k3 : 455 * k3 + w3],
                                      pad3[k3][0][:, 0:w3]),
                                  reads=[pad3[k3][1]], writes=[asb_b],
                                  accum=(k3 > 0))
                        kb.op("dve", lambda e: e.reciprocal(
                            rdn_t[:],
                            asb_t[:].rearrange("p (s c) -> p s c", c=65)
                            [:, :, 64]),
                            reads=[asb_b], writes=[rdn_b])
                        for sc in range(NCH):
                            kb.op("dve", lambda e, sc=sc: e.tensor_scalar(
                                t1_t[:], asb_t[:, sc * 65 : sc * 65 + 64],
                                rdn_t[:, sc : sc + 1], wcol_t[:, 1:2],
                                ALU.mult, ALU.mult),
                                reads=[asb_b], sreads=[rdn_b, wcol_b],
                                writes=[t1_b])
                            kb.op("dve", lambda e, sc=sc, o=o:
                                  e.tensor_scalar(
                                      t2_t[:],
                                      numt_t[:, sc * KV + (o % 8) * 64 :
                                             sc * KV + (o % 8) * 64 + 64],
                                      rden_t[:, sc : sc + 1],
                                      wcol_t[:, 0:1], ALU.mult, ALU.mult),
                                  reads=[numt_b],
                                  sreads=[rden_b, wcol_b],
                                  writes=[t2_b])
                            kb.op("dve", lambda e: e.tensor_tensor(
                                t1_t[:], t1_t[:], t2_t[:], ALU.add),
                                reads=[t1_b, t2_b], writes=[t1_b],
                                accum=True)
                            kb.op("pe", lambda e: e.transpose(
                                ptrx[0:64, 0:128], t1_t[:], eye_bf[:]),
                                reads=[t1_b, eye_bf_b], writes=[ptrx_b])
                            g = o // 2
                            rbase = (o % 2) * 64
                            kb.op("dve", lambda e, g=g, rbase=rbase, sc=sc:
                                  e.tensor_tensor(
                                      x2_t[g][rbase : rbase + 64,
                                              sc * 128 : (sc + 1) * 128],
                                      x2_t[g][rbase : rbase + 64,
                                              sc * 128 : (sc + 1) * 128],
                                      ptrx[0:64, 0:128], ALU.add),
                                  reads=[x2_b[g], ptrx_b],
                                  writes=[x2_b[g]], accum=True)
                kb.barrier()
            kb.barrier()

    if PHASE >= 5:
        attnb_phase()
    if PHASE >= 6:
        ffn_phase("4")

    kb.tap("x3", [D, T], [(x2_t[g][:], x2_b[g],
                           (lambda t, g=g: t[g * 128:(g + 1) * 128]))
                          for g in range(G)])

    with contextlib.ExitStack() as sO:
        lps4, lsb4 = PS(sO), SB(sO)
        pout = [lps4("pout0", [128, 512], F32), lps4("pout1", [128, 512], F32)]
        otile = [lsb4("otile0", [128, D], F32), lsb4("otile1", [128, D], F32)]
        outs = kb.stream("out_store")
        for tc in range(NCH):
            ot, ob = otile[tc % 2]
            for g in range(G):
                pt, pb = pout[g % 2]
                kb.op("pe", lambda e, tc=tc, g=g, pt=pt: e.transpose(
                    pt[:, 0:128], x2_t[g][:, tc * 128 : (tc + 1) * 128],
                    eye_f[:]),
                    reads=[x2_b[g], eye_f_b], writes=[pb])
                kb.op("dve", lambda e, g=g, pt=pt, ot=ot: e.tensor_copy(
                    ot[:, g * 128 : (g + 1) * 128], pt[:, 0:128]),
                    reads=[pb], writes=[ob], accum=True)
            kb.dma("sp", outs, out_ap[tc * 128 : (tc + 1) * 128, :],
                   ot[:], reads=[ob])
    kb.finish()
    st.close()
    return kb


def kernel(**inputs):
    per_core, shared = prep_inputs(inputs)
    kb = build_program()
    in_maps = [dict(pc, **shared) for pc in per_core]
    res = run_bass_kernel_spmd(kb.nc, in_maps, core_ids=list(range(8)))
    out = np.stack([r["out"] for r in res.results], axis=0)
    if DBG:
        kernel.dbg = [{k: r["dbg_" + k] for k in DBG} for r in res.results]
    return out


# version 09
# speedup vs baseline: 1.2432x; 1.0019x over previous
import sys
if "/opt/trn_rl_repo" not in sys.path:
    sys.path.insert(0, "/opt/trn_rl_repo")

import contextlib
import numpy as np
import concourse.bass as bass
import concourse.mybir as mybir
from concourse.bass_utils import run_bass_kernel_spmd

F32 = mybir.dt.float32
BF16 = mybir.dt.bfloat16
AF = mybir.ActivationFunctionType
ALU = mybir.AluOpType
AX = mybir.AxisListType

D = 1024
T = 2048
G = 8
NTB = 4
TBW = 512
H = 4
DH = 256
C = 128
NCH = T // C
FLOOR = 0.01
HID = 4096
QH = 16
KVH = 8
DHB = 64
KV = 512

DBG = []
PHASE = 6
SCANCH = NCH
SCAN_DBG = ""
BSTAGE = 4
STATS_LIM = NCH
STATS_NOFIX = False

QA = [0, 1, 4, 5, 8, 9, 12, 13]
QB = [2, 3, 6, 7, 10, 11, 14, 15]

_PAIRS = [(0, 1), (0, 2), (0, 3), (1, 2), (1, 3), (2, 3)]


def _stat_rows():
    out = []
    for b in range(4):
        out.append(("mean", (b,), 4 * b, 1.0 / DH))
    for pi, (a, b) in enumerate(_PAIRS[:4]):
        out.append(("prod", (a, b), 16 + 4 * pi, 1.0 / DH))
    for b in range(4):
        out.append(("sq", (b,), 32 + 4 * b, 1.0 / (DH - 1)))
    for pi, (a, b) in enumerate(_PAIRS[4:]):
        out.append(("prod", (a, b), 48 + 4 * pi, 1.0 / DH))
    return out


def _feat_perm():
    our = np.zeros(56, np.int64)
    for b in range(4):
        for h in range(4):
            our[4 * b + h] = (2 * b) * 4 + h
            our[32 + 4 * b + h] = (2 * b + 1) * 4 + h
    for pi in range(4):
        for h in range(4):
            our[16 + 4 * pi + h] = 32 + 4 * pi + h
    for pi in range(2):
        for h in range(4):
            our[48 + 4 * pi + h] = 32 + (4 + pi) * 4 + h
    return our


def _retile_km(w, kc=128):
    k, m = w.shape
    assert k % kc == 0
    return np.ascontiguousarray(w.reshape(k // kc, kc, m))


def _pad_rows(w, rows):
    out = np.zeros((rows, w.shape[1]), w.dtype)
    out[: w.shape[0]] = w
    return out


def prep_inputs(inp):
    f32 = np.float32
    x = np.asarray(inp["L0_attn_a_hidden_states"], f32)
    shared = {}

    for nm in ("Wq", "Wk", "Wv"):
        w = np.asarray(inp["L0_attn_a_" + nm], f32)
        shared["a_" + nm] = np.ascontiguousarray(
            w.reshape(G, 128, G, 128).transpose(2, 0, 1, 3))
    shared["a_Wb"] = _retile_km(np.asarray(inp["L0_attn_a_Wb"], f32))
    for nm in ("conv_q_w", "conv_k_w", "conv_v_w"):
        shared["a_" + nm] = np.ascontiguousarray(
            np.asarray(inp["L0_attn_a_" + nm], f32).reshape(G, 128, 4))
    shared["a_local_w"] = np.ascontiguousarray(
        np.asarray(inp["L0_attn_a_local_w"], f32).reshape(G, 128, 7))
    shared["a_mid_w"] = np.ascontiguousarray(
        np.asarray(inp["L0_attn_a_mid_w"], f32).reshape(G, 128, 25))
    rw1 = np.asarray(inp["L0_attn_a_r_w1"], f32)
    perm = _feat_perm()
    rw1p = np.zeros((1152, 2176), f32)
    rw1p[:1024, :2160] = rw1[:1024]
    rw1p[1024 : 1024 + 56, :2160] = rw1[1024 + perm]
    rw1g = np.zeros((5, 9, 128, 512), f32)
    for mg in range(5):
        cols = rw1p[:, 512 * mg : 512 * (mg + 1)]
        rw1g[mg, :, :, : cols.shape[1]] = cols.reshape(9, 128, -1)
    shared["a_rw1"] = np.ascontiguousarray(rw1g)
    shared["a_rb1"] = np.ascontiguousarray(
        _pad_rows(np.asarray(inp["L0_attn_a_r_b1"], f32)[:, None], 2176)
        .reshape(17, 128).T)
    rw2 = _pad_rows(np.asarray(inp["L0_attn_a_r_w2"], f32), 2176)
    shared["a_rw2"] = _retile_km(rw2)
    shared["a_rb2"] = np.ascontiguousarray(
        np.asarray(inp["L0_attn_a_r_b2"], f32).reshape(16, 1))
    shared["a_Wo"] = np.ascontiguousarray(
        np.asarray(inp["L0_attn_a_Wo"], f32)
        .reshape(G, 128, G, 128).transpose(2, 0, 1, 3))
    selbank = np.zeros((128, 14 * 132), f32)
    for si, (_, _, r0, coef) in enumerate(_stat_rows()):
        selbank[:, 132 * si + r0 + 3] = coef
    shared["a_selbank"] = selbank
    psel = np.zeros((16, 16, 128), f32)
    for s in range(16):
        psel[s, s, :] = 1.0
    shared["a_psel"] = psel
    mixw = np.asarray(inp["L0_attn_a_mix_w"], f32)
    onw = np.asarray(inp["L0_attn_a_onorm_w"], f32)
    mixsel = np.zeros((G, 4, 128), f32)
    onsel = np.zeros((G, 4, 128), f32)
    for g in range(G):
        h = g // 2
        mixsel[g, h, :] = mixw[h, (g % 2) * 128 : (g % 2) * 128 + 128]
        onsel[g, h, :] = onw[(g % 2) * 128 : (g % 2) * 128 + 128]
    shared["a_mixsel"] = mixsel
    shared["a_onsel"] = onsel
    hsel = np.zeros((G, 128, 4), f32)
    for g in range(G):
        hsel[g, :, g // 2] = 1.0
    shared["a_hsel"] = hsel

    w1 = np.asarray(inp["L0_ffn_W1"], f32)
    b1cat = np.asarray(inp["L0_ffn_B1"], f32).reshape(24, HID)
    w1aug = np.concatenate([w1, _pad_rows(b1cat, 128)], 0)
    shared["f_w1"] = np.ascontiguousarray(
        w1aug.reshape(9, 128, 2, 2048).transpose(2, 0, 1, 3))
    w2 = np.asarray(inp["L0_ffn_W2"], f32)
    shared["f_w2"] = np.ascontiguousarray(w2.reshape(2, 16, 128, D))
    b2cat = np.asarray(inp["L0_ffn_B2"], f32).reshape(24, D)
    shared["f_b2cat"] = np.ascontiguousarray(_pad_rows(b2cat, 128))
    a1 = np.asarray(inp["L0_ffn_A1"], f32).transpose(1, 0, 2).reshape(D, 24)
    r1 = np.asarray(inp["L0_ffn_R1"], f32)
    shared["f_a1r1"] = _retile_km(np.concatenate([a1, r1], 1))
    a2 = np.asarray(inp["L0_ffn_A2"], f32).transpose(1, 0, 2).reshape(HID, 24)
    r2 = np.asarray(inp["L0_ffn_R2"], f32)
    shared["f_a2r2"] = np.ascontiguousarray(
        np.concatenate([a2, r2], 1).reshape(2, 16, 128, 27))
    shared["f_b1"] = np.ascontiguousarray(
        np.asarray(inp["L0_ffn_b1"], f32).reshape(32, 128).T)
    shared["f_b2"] = np.ascontiguousarray(
        np.asarray(inp["L0_ffn_b2"], f32).reshape(G, 128).T)

    wqb = np.asarray(inp["L0_attn_b_Wq"], f32)
    qperm = np.zeros(D, np.int64)
    for j in range(8):
        qperm[j * 128 : j * 128 + 64] = QA[j] * 64 + np.arange(64)
        qperm[j * 128 + 64 : (j + 1) * 128] = QB[j] * 64 + np.arange(64)
    wqb = wqb[:, qperm]
    shared["b_Wq"] = np.ascontiguousarray(
        wqb.reshape(G, 128, G, 128).transpose(2, 0, 1, 3))
    for nm, go in (("Wk", 4), ("Wv", 4)):
        w = np.asarray(inp["L0_attn_b_" + nm], f32)
        shared["b_" + nm] = np.ascontiguousarray(
            w.reshape(G, 128, go, 128).transpose(2, 0, 1, 3))
    shared["b_mem"] = _retile_km(np.asarray(inp["L0_attn_b_memory"], f32))
    shared["b_memn"] = np.ascontiguousarray(
        np.asarray(inp["L0_attn_b_memory_norm"], f32).reshape(4, 128, 1))
    shared["b_mw"] = np.asarray(
        inp["L0_attn_b_memory_weight"], f32).reshape(1, 1)
    e8x = np.zeros((32, 16), f32)
    for j in range(8):
        e8x[16 + QA[j] // 2, 2 * j] = 1.0
        e8x[16 + QB[j] // 2, 2 * j + 1] = 1.0
    shared["b_e8"] = e8x
    ones2 = np.zeros((128, 2), f32)
    ones2[:64, 0] = 1.0
    ones2[64:, 1] = 1.0
    shared["c_ones2"] = ones2

    shared["c_eye"] = np.eye(128, dtype=f32)
    shared["c_ones_col"] = np.ones((128, 1), f32)
    shared["c_ones_row"] = np.ones((1, 128), f32)
    shift32 = np.zeros((64, 64), f32)
    for j in range(16):
        shift32[j, 32 + j] = 1.0
    shared["c_shift32"] = shift32
    shared["c_sl_mask"] = np.tril(np.ones((128, 128), f32), -1)
    shared["c_ui_mask"] = np.triu(np.ones((128, 128), f32), 0)

    per_core = [{"x_in": np.ascontiguousarray(x[b].T)} for b in range(8)]
    return per_core, shared



class Buf:
    __slots__ = ("name", "w", "r")

    def __init__(self, name):
        self.name = name
        self.w = set()
        self.r = set()


class DmaStream:
    __slots__ = ("name", "sem", "cum")

    def __init__(self, name, sem):
        self.name = name
        self.sem = sem
        self.cum = 0


class KB:
    ENGS = ("pe", "dve", "act", "pool", "sp")

    def __init__(self, nc, stack):
        self.nc = nc
        self.e = dict(pe=nc.tensor, dve=nc.vector, act=nc.scalar,
                      pool=nc.gpsimd, sp=nc.sync)
        self.rec = {k: [] for k in self.ENGS}
        self.assigned = {k: [] for k in self.ENGS}
        self.esem = {k: stack.enter_context(nc.semaphore("s_" + k))
                     for k in ("pe", "dve", "act", "pool")}
        self.wm = {}
        self.streams = {}
        self.stack = stack
        self.taps = {}
        self.tap_stream = self.stream("taps")
        self.n_waits = 0
        self.barrier_toks = []
        self.barrier_gen = 0
        self.barrier_seen = {}
        self.log = []

    def stream(self, name):
        if name not in self.streams:
            sem = self.stack.enter_context(self.nc.semaphore("d_" + name))
            self.streams[name] = DmaStream(name, sem)
        return self.streams[name]

    def _resolve_eng(self, eng, idx):
        a = self.assigned[eng]
        lo, hi = 0, len(a)
        while lo < hi:
            mid = (lo + hi) // 2
            if a[mid][0] >= idx:
                hi = mid
            else:
                lo = mid + 1
        if lo < len(a):
            return ("e" + eng, self.esem[eng], a[lo][1])
        cnt = (a[-1][1] if a else 0) + 1
        self.rec[eng][idx].then_inc(self.esem[eng], 1)
        self.log.append((eng, "inc@", idx, cnt))
        a.append((idx, cnt))
        return ("e" + eng, self.esem[eng], cnt)

    def _wait(self, eng, tok, force_same=False):
        if tok[0] == "d":
            _, stream, cum = tok
            key, sem, cnt = ("d" + stream.name, stream.sem, stream.cum)
        else:
            _, peng, idx = tok
            if peng == eng and not force_same:
                return
            key, sem, cnt = self._resolve_eng(peng, idx)
        if self.wm.get((eng, key), 0) >= cnt:
            self.log.append((eng, "skipwait", key, cnt))
            return
        self.e[eng].wait_ge(sem, cnt)
        self.log.append((eng, "wait", key, cnt))
        self.n_waits += 1
        self.wm[(eng, key)] = cnt

    def _deps(self, eng, reads, writes, sreads=()):
        if self.barrier_seen.get(eng, 0) < self.barrier_gen:
            self.barrier_seen[eng] = self.barrier_gen
            for tok in self.barrier_toks:
                self._wait(eng, tok)
        toks = set()
        for b in reads:
            toks |= b.w
        for b in writes:
            toks |= b.w | b.r
        stoks = set()
        for b in sreads:
            stoks |= b.w
        nrec = len(self.rec[eng])
        for tok in sorted(stoks - toks, key=repr):
            self._wait(eng, tok, force_same=True)
        for tok in sorted(toks, key=repr):
            force = (tok in stoks) or (
                tok[0] == "e" and tok[1] == eng and tok[2] >= nrec - 8)
            self._wait(eng, tok, force_same=force)

    def mark(self, *args):
        self.log.append(("MARK",) + args)

    def barrier(self):
        toks = []
        for eng in ("pe", "dve", "act", "pool"):
            n = len(self.rec[eng])
            if n:
                toks.append(("e", eng, n - 1))
        for s in self.streams.values():
            if s.cum:
                toks.append(("d", s, s.cum))
        self.barrier_toks = toks
        self.barrier_gen += 1

    def op(self, eng, emit, reads=(), writes=(), accum=False, sreads=()):
        self._deps(eng, reads, writes, sreads)
        inst = emit(self.e[eng])
        idx = len(self.rec[eng])
        self.rec[eng].append(inst)
        self.log.append((eng, "op", idx, getattr(inst, "name", "")))
        tok = ("e", eng, idx)
        for b in list(reads) + list(sreads):
            b.r.add(tok)
        for b in writes:
            if accum:
                b.w.add(tok)
            else:
                b.w = {tok}
                b.r = set()
        return tok

    def dma(self, qeng, stream, out_ap, in_ap, reads=(), writes=(),
            accum=False):
        self._deps(qeng, reads, writes)
        inst = self.e[qeng].dma_start(out=out_ap, in_=in_ap)
        stream.cum += 16
        inst.then_inc(stream.sem, 16)
        tok = ("d", stream, stream.cum)
        for b in reads:
            b.r.add(tok)
        for b in writes:
            if accum:
                b.w.add(tok)
            else:
                b.w = {tok}
                b.r = set()
        return tok

    def tap(self, name, shape, srcs):
        if name not in DBG:
            return
        t = self.nc.dram_tensor("dbg_" + name, list(shape), F32,
                                kind="ExternalOutput").ap()
        self.taps[name] = t
        for ap, buf, slc in srcs:
            self.dma("pool", self.tap_stream, slc(t), ap, reads=[buf])

    def finish(self):
        for s in self.streams.values():
            if s.cum:
                self.e["sp"].wait_ge(s.sem, s.cum)



class _Skip(Exception):
    pass


def build_program():
    nc = bass.Bass("TRN2", target_bir_lowering=False)
    st = contextlib.ExitStack()

    dr = {}
    dr["x_in"] = nc.dram_tensor("x_in", [D, T], F32, kind="ExternalInput").ap()
    shapes = {
        "a_Wq": [G, G, 128, 128], "a_Wk": [G, G, 128, 128],
        "a_Wv": [G, G, 128, 128], "a_Wb": [G, 128, 4],
        "a_conv_q_w": [G, 128, 4], "a_conv_k_w": [G, 128, 4],
        "a_conv_v_w": [G, 128, 4],
        "a_local_w": [G, 128, 7], "a_mid_w": [G, 128, 25],
        "a_rw1": [5, 9, 128, 512], "a_rb1": [128, 17],
        "a_rw2": [17, 128, 16], "a_rb2": [16, 1],
        "a_Wo": [G, G, 128, 128],
        "a_selbank": [128, 14 * 132], "a_psel": [16, 16, 128],
        "a_mixsel": [G, 4, 128], "a_onsel": [G, 4, 128],
        "f_w1": [2, 9, 128, 2048], "f_w2": [2, 16, 128, D],
        "f_b2cat": [128, D],
        "f_a1r1": [G, 128, 27], "f_a2r2": [2, 16, 128, 27],
        "f_b1": [128, 32], "f_b2": [128, G],
        "b_Wq": [G, G, 128, 128], "b_Wk": [4, G, 128, 128],
        "b_Wv": [4, G, 128, 128],
        "b_mem": [4, 128, KV], "b_memn": [4, 128, 1], "b_mw": [1, 1],
        "b_e8": [32, 16], "c_ones2": [128, 2],
        "a_hsel": [G, 128, 4],
        "c_eye": [128, 128], "c_ones_col": [128, 1], "c_ones_row": [1, 128],
        "c_sl_mask": [128, 128], "c_ui_mask": [128, 128],
        "c_shift32": [64, 64],
    }
    for nm, shp in shapes.items():
        dr[nm] = nc.dram_tensor(nm, shp, F32, kind="ExternalInput").ap()
    out_ap = nc.dram_tensor("out", [T, D], F32, kind="ExternalOutput").ap()
    xsp = nc.dram_tensor("x_spill", [G, 128, T], F32, kind="Internal").ap()
    aout_sp = nc.dram_tensor("aout_spill", [G, 128, T], BF16,
                             kind="Internal").ap()

    kb = KB(nc, st)

    def SB(stk):
        def f(name, shape, dtype):
            t = stk.enter_context(nc.sbuf_tensor(name, shape, dtype))
            return t, Buf(name)
        return f

    def PS(stk):
        def f(name, shape, dtype):
            t = stk.enter_context(nc.psum_tensor(name, shape, dtype))
            return t, Buf(name)
        return f

    sb = SB(st)

    cst = kb.stream("consts")
    eye_bf, eye_bf_b = sb("eye_bf", [128, 128], BF16)
    eye_f, eye_f_b = sb("eye_f", [128, 128], F32)
    ones_col, ones_col_b = sb("ones_col", [128, 1], BF16)
    ones_row, ones_row_b = sb("ones_row", [1, 128], BF16)
    sl_mask, sl_mask_b = sb("sl_mask", [128, 128], BF16)
    ui_mask, ui_mask_b = sb("ui_mask", [128, 128], BF16)
    kb.dma("pool", cst, eye_bf[:], dr["c_eye"], writes=[eye_bf_b])
    kb.dma("sp", cst, eye_f[:], dr["c_eye"], writes=[eye_f_b])
    kb.dma("pool", cst, ones_col[:], dr["c_ones_col"], writes=[ones_col_b])
    kb.dma("pool", cst, ones_row[:], dr["c_ones_row"], writes=[ones_row_b])
    kb.dma("pool", cst, sl_mask[:], dr["c_sl_mask"], writes=[sl_mask_b])
    kb.dma("pool", cst, ui_mask[:], dr["c_ui_mask"], writes=[ui_mask_b])

    constf_t, constf_b = sb("constf", [128, 16], F32)
    _constf = {}

    def constf(v):
        v = float(v)
        if v not in _constf:
            i = len(_constf)
            _constf[v] = i
            kb.op("pool", lambda e, i=i, v=v: e.memset(
                constf_t[:, i : i + 1], v), writes=[constf_b], accum=True)
            nc.const_aps.register(F32, v, constf_t[:, i : i + 1])
        return constf_b

    def layernorm(tag, x_t, x_b, xb_t, xb_b, scope=None):
        lnst = contextlib.ExitStack()
        lsb, lps = SB(lnst), PS(lnst)
        xbp_t, xbp_b = lsb(f"ln{tag}_xbp", [128, TBW], BF16)
        xsq_t, xsq_b = lsb(f"ln{tag}_xsq", [128, TBW], BF16)
        ps_s, ps_s_b = lps(f"ln{tag}_ps", [1, TBW], F32)
        ps_q, ps_q_b = lps(f"ln{tag}_pq", [1, TBW], F32)
        ps_bc, ps_bc_b = lps(f"ln{tag}_pbc", [128, TBW], F32)
        row_a, row_a_b = lsb(f"ln{tag}_ra", [1, TBW], F32)
        mrow, mrow_b = lsb(f"ln{tag}_m", [1, TBW], F32)
        rrow, rrow_b = lsb(f"ln{tag}_r", [1, TBW], F32)
        rowbf, rowbf_b = lsb(f"ln{tag}_bf", [1, TBW], BF16)
        for tb in range(NTB):
            sl = slice(tb * TBW, (tb + 1) * TBW)
            for g in range(G):
                kb.op("dve", lambda e, g=g, sl=sl: e.tensor_copy(
                    xbp_t[:], x_t[g][:, sl]), reads=[x_b[g]], writes=[xbp_b])
                kb.op("act", lambda e, g=g, sl=sl: e.activation(
                    xsq_t[:], x_t[g][:, sl], AF.Square),
                    reads=[x_b[g]], writes=[xsq_b])
                kb.op("pe", lambda e, g=g: e.matmul(
                    ps_s[:], ones_col[:], xbp_t[:],
                    start=(g == 0), stop=(g == G - 1)),
                    reads=[ones_col_b, xbp_b], writes=[ps_s_b], accum=(g > 0))
                kb.op("pe", lambda e, g=g: e.matmul(
                    ps_q[:], ones_col[:], xsq_t[:],
                    start=(g == 0), stop=(g == G - 1)),
                    reads=[ones_col_b, xsq_b], writes=[ps_q_b], accum=(g > 0))
            kb.op("dve", lambda e: e.tensor_scalar(
                mrow[:], ps_s[:], 1.0 / D, None, ALU.mult),
                reads=[ps_s_b], writes=[mrow_b])
            kb.op("act", lambda e: e.activation(row_a[:], mrow[:], AF.Square),
                  reads=[mrow_b], writes=[row_a_b])
            kb.op("dve", lambda e: e.scalar_tensor_tensor(
                rrow[:], ps_q[:], 1.0 / D, row_a[:], ALU.mult, ALU.subtract),
                reads=[ps_q_b, row_a_b], writes=[rrow_b])
            kb.op("act", lambda e: e.activation(
                rrow[:], rrow[:], AF.Sqrt, bias=1e-5),
                reads=[rrow_b, constf(1e-5)], writes=[rrow_b], accum=True)
            kb.op("dve", lambda e: e.reciprocal(rrow[:], rrow[:]),
                  reads=[rrow_b], writes=[rrow_b], accum=True)
            kb.op("dve", lambda e: e.tensor_scalar(
                rowbf[:], mrow[:], -1.0, None, ALU.mult),
                reads=[mrow_b], writes=[rowbf_b])
            kb.op("pe", lambda e: e.matmul(
                ps_bc[:], ones_row[:], rowbf[0:1, :], start=True, stop=True),
                reads=[ones_row_b, rowbf_b], writes=[ps_bc_b])
            for g in range(G):
                kb.op("dve", lambda e, g=g, sl=sl: e.tensor_tensor(
                    xb_t[g][:, sl], x_t[g][:, sl], ps_bc[:], ALU.add),
                    reads=[x_b[g], ps_bc_b], writes=[xb_b[g]], accum=True)
            kb.op("dve", lambda e: e.tensor_copy(rowbf[:], rrow[:]),
                  reads=[rrow_b], writes=[rowbf_b])
            kb.op("pe", lambda e: e.matmul(
                ps_bc[:], ones_row[:], rowbf[0:1, :], start=True, stop=True),
                reads=[ones_row_b, rowbf_b], writes=[ps_bc_b])
            for g in range(G):
                kb.op("dve", lambda e, g=g, sl=sl: e.tensor_tensor(
                    xb_t[g][:, sl], xb_t[g][:, sl], ps_bc[:], ALU.mult),
                    reads=[ps_bc_b] + [xb_b[g]], writes=[xb_b[g]], accum=True)
        kb.tap(f"ln{tag}", [D, T], [(xb_t[g][:], xb_b[g],
                                     (lambda t, g=g: t[g * 128:(g + 1) * 128]))
                                    for g in range(G)])
        kb.barrier()
        lnst.close()

    def proj_mms(pt, pb, wtile, wbuf, rhs_t, rhs_b, sl, nk=G):
        for gi in range(nk):
            kb.op("pe", lambda e, gi=gi: e.matmul(
                pt, wtile[:, gi * 128:(gi + 1) * 128], rhs_t[gi][:, sl],
                start=(gi == 0), stop=(gi == nk - 1)),
                reads=[wbuf, rhs_b[gi]], writes=[pb], accum=(gi > 0))

    s1 = st.enter_context(contextlib.ExitStack())
    sb1 = SB(s1)
    xb_t = []
    xb_b = []
    for g in range(G):
        t, b = sb1(f"xb{g}", [128, T], BF16)
        xb_t.append(t)
        xb_b.append(b)
    v_t = []
    v_b = []
    for g in range(G):
        t, b = sb1(f"v{g}", [128, T], BF16)
        v_t.append(t)
        v_b.append(b)
    dl_t = []
    dl_b = []
    for g in range(G):
        t, b = sb1(f"dl{g}", [128, T], BF16)
        dl_t.append(t)
        dl_b.append(b)
    beta_t, beta_b = sb1("beta", [128, 4 * NCH], F32)

    with contextlib.ExitStack() as sA:
        lsb, lps = SB(sA), PS(sA)
        x_t = []
        x_b = []
        for g in range(G):
            t, b = lsb(f"x{g}", [128, T], F32)
            x_t.append(t)
            x_b.append(b)
        xs = kb.stream("x_io")
        xin3 = dr["x_in"].rearrange("(g p) t -> g p t", p=128)
        for g in range(G):
            kb.dma("sp", xs, x_t[g][:], xin3[g], writes=[x_b[g]])

        layernorm("1", x_t, x_b, xb_t, xb_b, sA)

        ws = kb.stream("wstr")
        wst = [lsb("a1_wst0", [128, D], BF16), lsb("a1_wst1", [128, D], BF16)]
        wb_tiles = []
        for g in range(G):
            t, b = lsb(f"a_Wb_{g}", [128, 4], BF16)
            kb.dma("pool", ws, t[:], dr["a_Wb"][g], writes=[b])
            wb_tiles.append((t, b))
        cvw = []
        for g in range(G):
            t, b = lsb(f"cvw{g}", [128, 4], F32)
            kb.dma("sp", ws, t[:], dr["a_conv_v_w"][g], writes=[b])
            cvw.append((t, b))

        pp, pp_b = lps("a1_pp", [128, TBW], F32)
        pp2, pp2_b = lps("a1_pp2", [128, TBW], F32)
        pbeta, pbeta_b = lps("a1_pbeta", [128, 8], F32)
        cpre_t, cpre_b = lsb("a1_cpre", [128, T + 4], BF16)
        acc0_t, acc0_b = lsb("a1_acc0", [128, T], BF16)

        for go in range(G):
            wtile, wbuf = wst[go % 2]
            kb.dma("pool", kb.stream(f"wq{go % 2}"),
                   wtile[:].rearrange("p (gi m) -> p gi m", m=128),
                   dr["a_Wv"][go].rearrange("gi p m -> p gi m"),
                   writes=[wbuf])
            pbank = [(pp, pp_b), (pp2, pp2_b)]
            for tb in range(NTB):
                pt, pb = pbank[tb % 2]
                sl = slice(tb * TBW, (tb + 1) * TBW)
                proj_mms(pt[:], pb, wtile, wbuf, xb_t, xb_b, sl)
                kb.op("dve", lambda e, tb=tb, pt=pt: e.tensor_copy(
                    cpre_t[:, 4 + tb * TBW : 4 + (tb + 1) * TBW], pt[:]),
                    reads=[pb], writes=[cpre_b], accum=True)
            kb.op("dve", lambda e: e.memset(cpre_t[:, 0:4], 0.0),
                  writes=[cpre_b], accum=True)
            ctile, cbuf = cvw[go]
            kb.op("dve", lambda e, ctile=ctile: e.tensor_scalar(
                acc0_t[:], cpre_t[:, 4 : 4 + T], ctile[:, 3:4], None,
                ALU.mult), reads=[cpre_b, cbuf], writes=[acc0_b])
            accs = [(acc0_t, acc0_b), (v_t[go], v_b[go])]
            for tau in range(1, 4):
                src_t, src_b2 = accs[(tau + 1) % 2]
                dst_t, dst_b2 = accs[tau % 2]
                kb.op("dve", lambda e, tau=tau, ctile=ctile, src_t=src_t,
                      dst_t=dst_t: e.scalar_tensor_tensor(
                          dst_t[:], cpre_t[:, 4 - tau : 4 - tau + T],
                          ctile[:, 3 - tau : 4 - tau], src_t[:],
                          ALU.mult, ALU.add),
                      reads=[cpre_b, cbuf, src_b2], writes=[dst_b2])
            kb.op("act", lambda e, go=go: e.activation(
                v_t[go][:], v_t[go][:], AF.Silu),
                reads=[v_b[go]], writes=[v_b[go]], accum=True)

        for tc in range(NCH):
            for gi in range(G):
                wtile, wbuf = wb_tiles[gi]
                kb.op("pe", lambda e, gi=gi, tc=tc, wtile=wtile: e.matmul(
                    pbeta[:, 0:4], xb_t[gi][:, tc * 128 : (tc + 1) * 128],
                    wtile[:], start=(gi == 0), stop=(gi == G - 1)),
                    reads=[wbuf, xb_b[gi]], writes=[pbeta_b], accum=(gi > 0))
            kb.op("act", lambda e, tc=tc: e.activation(
                beta_t[:, 4 * tc : 4 * tc + 4], pbeta[:, 0:4], AF.Sigmoid),
                reads=[pbeta_b], writes=[beta_b], accum=True)

        for g in range(G):
            kb.dma("sp", xs, xsp[g], x_t[g][:], reads=[x_b[g]])
        kb.tap("v", [D, T], [(v_t[g][:], v_b[g],
                              (lambda t, g=g: t[g * 128:(g + 1) * 128]))
                             for g in range(G)])
        kb.tap("beta", [128, 4 * NCH], [(beta_t[:], beta_b, lambda t: t)])
        kb.barrier()

    with contextlib.suppress(_Skip), contextlib.ExitStack() as sS:
        if PHASE < 2:
            raise _Skip()
        lsb, lps = SB(sS), PS(sS)
        ws = kb.stream("wstr")
        wq2 = [lsb("s_wq0", [128, D], BF16), lsb("s_wq1", [128, D], BF16)]
        cqw = []
        ckw = []
        for g in range(G):
            t, b = lsb(f"cqw{g}", [128, 4], F32)
            kb.dma("sp", ws, t[:], dr["a_conv_q_w"][g], writes=[b])
            cqw.append((t, b))
            t, b = lsb(f"ckw{g}", [128, 4], F32)
            kb.dma("sp", ws, t[:], dr["a_conv_k_w"][g], writes=[b])
            ckw.append((t, b))
        qg_t = []
        qg_b = []
        kg_t = []
        kg_b = []
        cprq = []
        cprk = []
        for g in range(G):
            t, b = lsb(f"qg{g}", [128, TBW], BF16)
            qg_t.append(t)
            qg_b.append(b)
            t, b = lsb(f"kg{g}", [128, TBW], BF16)
            kg_t.append(t)
            kg_b.append(b)
            cprq.append(lsb(f"cprq{g}", [128, TBW + 4], BF16))
            cprk.append(lsb(f"cprk{g}", [128, TBW + 4], BF16))
        S_t = []
        S_b = []
        for h in range(H):
            for c2 in range(2):
                t, b = lsb(f"S{h}_{c2}", [128, DH], F32)
                S_t.append(t)
                S_b.append(b)
        Sbf_t, Sbf_b = lsb("Sbf", [128, 2 * DH], BF16)
        ktm3 = [lsb(f"ktm{i}", [128, DH], BF16) for i in range(3)]
        kbm3 = [lsb(f"kbm{i}", [128, DH], BF16) for i in range(3)]
        vcm3 = [lsb(f"vcm{i}", [128, DH], BF16) for i in range(3)]
        rq3 = [lsb(f"rqc{i}", [128, 2], F32) for i in range(3)]
        at3 = [lsb(f"at{i}", [128, 128], BF16) for i in range(3)]
        vtm_t, vtm_b = lsb("vtm", [128, DH], BF16)
        uib_t, uib_b = lsb("uib", [128, DH], BF16)
        nw_t, nw_b = lsb("nw", [128, DH], BF16)
        otmp_t, otmp_b = lsb("otmp", [128, DH], BF16)
        y02 = [lsb(f"y0{i}", [128, 128], F32) for i in range(2)]
        cn2 = [[lsb(f"cn{i}_{j}", [128, 128], F32) for j in range(2)]
               for i in range(2)]
        cp2 = [[lsb(f"cp{i}_{j}", [128, 128], F32) for j in range(2)]
               for i in range(2)]
        rsb_t, rsb_b = lsb("rsb", [128, 128], F32)
        rsbbf2 = [lsb(f"rsbbf{i}", [128, 128], BF16) for i in range(2)]
        sq_t, sq_b = lsb("sqs", [128, 256], BF16)
        bkr_t, bkr_b = lsb("bkr", [128, 1], F32)
        pAG, pAG_b = lps("pAG", [128, 512], F32)
        ptm, ptm_b = lps("ptm", [128, 512], BF16)
        psq, psq_b = lps("psq", [128, 128], F32)
        pR, pR_b = lps("pR", [128, 128], F32)
        pUO, pUO_b = lps("pUO", [128, DH], F32)
        pW, pW_b = lps("pW", [128, DH], F32)
        pOT, pOT_b = lps("pOT", [128, 256], BF16)
        pS, pS_b = lps("pS", [128, 512], F32)
        pproj, pproj_b = pS, pS_b

        for h in range(H):
            for c2 in range(2):
                kb.op("dve", lambda e, i=2 * h + c2: e.memset(S_t[i][:], 0.0),
                      writes=[S_b[2 * h + c2]])

        def conv_group(gq, which, wst_idx, cpr, cw, out_t, out_b, wnm, cnmb):
            sl = slice(gq * TBW, (gq + 1) * TBW)
            for go in range(G):
                wtile, wbuf = wq2[(wst_idx + go) % 2]
                kb.dma("pool", kb.stream(f"wq{(wst_idx + go) % 2}"),
                       wtile[:].rearrange("p (gi m) -> p gi m", m=128),
                       dr[wnm][go].rearrange("gi p m -> p gi m"),
                       writes=[wbuf])
                proj_mms(pproj[:], pproj_b, wtile, wbuf, xb_t, xb_b, sl)
                ct, cb = cpr[go]
                if gq == 0:
                    kb.op("dve", lambda e, ct=ct: e.memset(ct[:, 0:4], 0.0),
                          writes=[cb], accum=True)
                kb.op("dve", lambda e, ct=ct: e.tensor_copy(
                    ct[:, 4 : 4 + TBW], pproj[:]),
                    reads=[pproj_b], writes=[cb], accum=True)
                wtile2, cbuf2 = cw[go]
                kb.op("dve", lambda e, ct=ct, w=wtile2: e.tensor_scalar(
                    acc2_t[:], ct[:, 4 : 4 + TBW], w[:, 3:4], None, ALU.mult),
                    reads=[cb, cbuf2], writes=[acc2_b])
                accs = [(acc2_t, acc2_b), (out_t[go], out_b[go])]
                for tau in range(1, 4):
                    src_t, src_b2 = accs[(tau + 1) % 2]
                    dst_t, dst_b2 = accs[tau % 2]
                    kb.op("dve", lambda e, tau=tau, ct=ct, w=wtile2,
                          src_t=src_t, dst_t=dst_t: e.scalar_tensor_tensor(
                              dst_t[:], ct[:, 4 - tau : 4 - tau + TBW],
                              w[:, 3 - tau : 4 - tau], src_t[:],
                              ALU.mult, ALU.add),
                          reads=[cb, cbuf2, src_b2], writes=[dst_b2])
                kb.op("act", lambda e, o=out_t[go]: e.activation(
                    o[:], o[:], AF.Silu),
                    reads=[out_b[go]], writes=[out_b[go]], accum=True)
                kb.op("dve", lambda e, ct=ct: e.tensor_copy(
                    ct[:, 0:4], ct[:, TBW : TBW + 4]),
                    reads=[cb], writes=[cb], accum=True)

        acc2_t, acc2_b = lsb("s_acc2", [128, TBW], BF16)

        qf_ap = kf_ap = None
        if "qf" in DBG:
            qf_ap = nc.dram_tensor("dbg_qf", [D, T], F32,
                                   kind="ExternalOutput").ap()
            kb.taps["qf"] = qf_ap
        if "kf" in DBG:
            kf_ap = nc.dram_tensor("dbg_kf", [D, T], F32,
                                   kind="ExternalOutput").ap()
            kb.taps["kf"] = kf_ap

        def pre_gen(i, c, h):
            s3, s2 = i % 3, i % 2
            co = (c % 4) * 128
            csl = slice(co, co + 128)
            ga, gb = 2 * h, 2 * h + 1
            bcol = beta_t[:, 4 * c + h : 4 * c + h + 1]
            ktm_t, ktm_b = ktm3[s3]
            kbm_t, kbm_b = kbm3[s3]
            vcm_t, vcm_b = vcm3[s3]
            rq_t, rq_b = rq3[s3]
            at_t, at_b = at3[s3]
            y0_t, y0_b = y02[s2]
            cn, cp = cn2[s2], cp2[s2]
            for idx, (src_t, src_b) in enumerate(
                    [(qg_t, qg_b), (kg_t, kg_b)]):
                kb.op("act", lambda e, s=src_t[ga]: e.activation(
                    sq_t[:, 0:128], s[:, csl], AF.Square),
                    reads=[src_b[ga]], writes=[sq_b])
                yield
                kb.op("act", lambda e, s=src_t[gb]: e.activation(
                    sq_t[:, 128:256], s[:, csl], AF.Square),
                    reads=[src_b[gb]], writes=[sq_b], accum=True)
                yield
                kb.op("pe", lambda e, idx=idx: e.matmul(
                    pAG[:, 256 + idx : 257 + idx], sq_t[:, 0:128],
                    ones_col[:], start=True, stop=False),
                    reads=[sq_b, ones_col_b], writes=[pAG_b], accum=True)
                yield
                kb.op("pe", lambda e, idx=idx: e.matmul(
                    pAG[:, 256 + idx : 257 + idx], sq_t[:, 128:256],
                    ones_col[:], start=False, stop=True),
                    reads=[sq_b, ones_col_b], writes=[pAG_b], accum=True)
                yield
            kb.op("dve", lambda e, rq_t=rq_t: e.tensor_scalar(
                rq_t[:], pAG[:, 256:258], 1e-12, None, ALU.add),
                reads=[pAG_b], writes=[rq_b])
            yield
            kb.op("act", lambda e, rq_t=rq_t: e.activation(
                rq_t[:], rq_t[:], AF.Sqrt),
                reads=[rq_b], writes=[rq_b], accum=True)
            yield
            kb.op("dve", lambda e, rq_t=rq_t: e.reciprocal(
                rq_t[:], rq_t[:]),
                reads=[rq_b], writes=[rq_b], accum=True)
            yield
            kb.op("pe", lambda e: e.transpose(
                ptm[:, 0:128], kg_t[ga][:, csl], eye_bf[:]),
                reads=[kg_b[ga], eye_bf_b], writes=[ptm_b])
            yield
            kb.op("pe", lambda e: e.transpose(
                ptm[:, 128:256], kg_t[gb][:, csl], eye_bf[:]),
                reads=[kg_b[gb], eye_bf_b], writes=[ptm_b], accum=True)
            yield
            kb.op("pe", lambda e: e.transpose(
                ptm[:, 256:384], v_t[ga][:, c * 128 : c * 128 + 128],
                eye_bf[:]),
                reads=[v_b[ga], eye_bf_b], writes=[ptm_b], accum=True)
            yield
            kb.op("pe", lambda e: e.transpose(
                ptm[:, 384:512], v_t[gb][:, c * 128 : c * 128 + 128],
                eye_bf[:]),
                reads=[v_b[gb], eye_bf_b], writes=[ptm_b], accum=True)
            yield
            kb.op("dve", lambda e, ktm_t=ktm_t: e.tensor_copy(
                ktm_t[:], ptm[:, 0:256]),
                reads=[ptm_b], writes=[ktm_b])
            yield
            kb.op("dve", lambda e: e.tensor_copy(
                vtm_t[:], ptm[:, 256:512]),
                reads=[ptm_b], writes=[vtm_b])
            yield
            kb.op("dve", lambda e, bcol=bcol, rq_t=rq_t:
                  e.scalar_tensor_tensor(
                      bkr_t[:], rq_t[:, 1:2], -1.0, bcol, ALU.mult,
                      ALU.mult),
                  reads=[rq_b, beta_b], writes=[bkr_b])
            yield
            kb.op("dve", lambda e, kbm_t=kbm_t, ktm_t=ktm_t: e.tensor_scalar(
                kbm_t[:], ktm_t[:], bkr_t[:], None, ALU.mult),
                reads=[ktm_b], sreads=[bkr_b], writes=[kbm_b])
            yield
            kb.op("dve", lambda e, bcol=bcol, vcm_t=vcm_t: e.tensor_scalar(
                vcm_t[:], vtm_t[:], bcol, None, ALU.mult),
                reads=[vtm_b], sreads=[beta_b], writes=[vcm_b])
            yield
            kb.op("pe", lambda e: e.matmul(
                pAG[:, 0:128], kg_t[ga][:, csl], kg_t[ga][:, csl],
                start=True, stop=False),
                reads=[kg_b[ga]], writes=[pAG_b], accum=True)
            yield
            kb.op("pe", lambda e: e.matmul(
                pAG[:, 0:128], kg_t[gb][:, csl], kg_t[gb][:, csl],
                start=False, stop=True),
                reads=[kg_b[gb]], writes=[pAG_b], accum=True)
            yield
            kb.op("dve", lambda e, y0_t=y0_t: e.tensor_scalar(
                y0_t[:], pAG[:, 0:128], bkr_t[:], None, ALU.mult),
                reads=[pAG_b], sreads=[bkr_b], writes=[y0_b])
            yield
            kb.op("dve", lambda e, y0_t=y0_t: e.tensor_tensor(
                y0_t[:], y0_t[:], sl_mask[:], ALU.mult),
                reads=[y0_b, sl_mask_b], writes=[y0_b], accum=True)
            yield
            kb.op("pe", lambda e, y0_t=y0_t: e.transpose(
                pAG[:, 320:448], y0_t[:], eye_f[:]),
                reads=[y0_b, eye_f_b], writes=[pAG_b], accum=True)
            yield
            kb.op("dve", lambda e, cp=cp, rq_t=rq_t: e.tensor_scalar(
                cp[0][0][:], pAG[:, 320:448], rq_t[:, 1:2], None,
                ALU.mult),
                reads=[pAG_b], sreads=[rq_b], writes=[cp[0][1]])
            yield
            kb.op("pe", lambda e, cp=cp: e.transpose(
                pAG[:, 320:448], cp[0][0][:], eye_f[:]),
                reads=[cp[0][1], eye_f_b], writes=[pAG_b], accum=True)
            yield
            kb.op("dve", lambda e, cn=cn: e.tensor_copy(
                cn[0][0][:], pAG[:, 320:448]),
                reads=[pAG_b], writes=[cn[0][1]])
            yield
            kb.op("pe", lambda e: e.matmul(
                pAG[:, 128:256], kg_t[ga][:, csl], qg_t[ga][:, csl],
                start=True, stop=False),
                reads=[kg_b[ga], qg_b[ga]], writes=[pAG_b], accum=True)
            yield
            kb.op("pe", lambda e: e.matmul(
                pAG[:, 128:256], kg_t[gb][:, csl], qg_t[gb][:, csl],
                start=False, stop=True),
                reads=[kg_b[gb], qg_b[gb]], writes=[pAG_b], accum=True)
            yield
            kb.op("dve", lambda e, at_t=at_t: e.tensor_tensor(
                at_t[:], pAG[:, 128:256], ui_mask[:], ALU.mult),
                reads=[pAG_b, ui_mask_b], writes=[at_b])
            yield

        def chain_gen(i, c, h):
            s2 = i % 2
            cn, cp = cn2[s2], cp2[s2]
            rsbbf_t, rsbbf_b = rsbbf2[s2]
            kb.op("dve", lambda e, cp=cp: e.tensor_tensor(
                rsb_t[:], cp[0][0][:], eye_f[:], ALU.add),
                reads=[cp[0][1], eye_f_b], writes=[rsb_b])
            yield
            kb.op("pe", lambda e: e.matmul(
                pR[:], eye_f[:], rsb_t[:], start=True, stop=False),
                reads=[eye_f_b, rsb_b], writes=[pR_b])
            yield
            cur = 0
            for lev in range(1, 7):
                nxt = 1 - cur
                kb.op("pe", lambda e, cur=cur, cp=cp, cn=cn: e.matmul(
                    psq[:], cp[cur][0][:], cn[cur][0][:],
                    start=True, stop=True),
                    reads=[cp[cur][1], cn[cur][1]], writes=[psq_b])
                yield
                kb.op("dve", lambda e, nxt=nxt, cn=cn: e.tensor_copy(
                    cn[nxt][0][:], psq[:]),
                    reads=[psq_b], writes=[cn[nxt][1]])
                yield
                if lev < 6:
                    kb.op("pe", lambda e, cur=cur, cp=cp, cn=cn: e.matmul(
                        psq[:], cn[cur][0][:], cp[cur][0][:],
                        start=True, stop=True),
                        reads=[cn[cur][1], cp[cur][1]], writes=[psq_b])
                    yield
                    kb.op("dve", lambda e, nxt=nxt, cp=cp: e.tensor_copy(
                        cp[nxt][0][:], psq[:]),
                        reads=[psq_b], writes=[cp[nxt][1]])
                    yield
                cur = nxt
                kb.op("pe", lambda e, cur=cur, cn=cn, last=(lev == 6):
                      e.matmul(pR[:], cn[cur][0][:], rsb_t[:],
                               start=False, stop=last),
                      reads=[cn[cur][1], rsb_b], writes=[pR_b], accum=True)
                yield
                kb.op("dve", lambda e: e.tensor_copy(rsb_t[:], pR[:]),
                      reads=[pR_b], writes=[rsb_b])
                yield
            kb.op("dve", lambda e, rsbbf_t=rsbbf_t: e.tensor_copy(
                rsbbf_t[:], rsb_t[:]),
                reads=[rsb_b], writes=[rsbbf_b])
            yield

        def post_gen(i, c, h):
            s3, s2 = i % 3, i % 2
            ga, gb = 2 * h, 2 * h + 1
            co = (c % 4) * 128
            csl = slice(co, co + 128)
            ktm_t, ktm_b = ktm3[s3]
            kbm_t, kbm_b = kbm3[s3]
            vcm_t, vcm_b = vcm3[s3]
            rq_t, rq_b = rq3[s3]
            at_t, at_b = at3[s3]
            rsbbf_t, rsbbf_b = rsbbf2[s2]
            first = (c == 0)
            if not first:
                kb.op("dve", lambda e, h=h: e.tensor_copy(
                    Sbf_t[:, 0:DH], S_t[2 * h][:]),
                    reads=[S_b[2 * h]], writes=[Sbf_b])
                yield
                kb.op("dve", lambda e, h=h: e.tensor_copy(
                    Sbf_t[:, DH : 2 * DH], S_t[2 * h + 1][:]),
                    reads=[S_b[2 * h + 1]], writes=[Sbf_b], accum=True)
                yield
            kb.op("pe", lambda e, first=first, rsbbf_t=rsbbf_t,
                  vcm_t=vcm_t: e.matmul(
                      pUO[:], rsbbf_t[:], vcm_t[:], start=True, stop=first),
                  reads=[rsbbf_b, vcm_b], writes=[pUO_b])
            yield
            if not first:
                kb.op("pe", lambda e, kbm_t=kbm_t, rsbbf_t=rsbbf_t:
                      e.matmul(pW[:, 0:128], kbm_t[:, 0:128], rsbbf_t[:],
                               start=True, stop=True),
                      reads=[kbm_b, rsbbf_b], writes=[pW_b])
                yield
                kb.op("pe", lambda e, kbm_t=kbm_t, rsbbf_t=rsbbf_t:
                      e.matmul(pW[:, 128:256], kbm_t[:, 128:256],
                               rsbbf_t[:], start=True, stop=True),
                      reads=[kbm_b, rsbbf_b], writes=[pW_b], accum=True)
                yield
                kb.op("dve", lambda e: e.tensor_copy(nw_t[:], pW[:]),
                      reads=[pW_b], writes=[nw_b])
                yield
                kb.op("pe", lambda e: e.matmul(
                    pUO[:], nw_t[:, 0:128], Sbf_t[:, 0:DH],
                    start=False, stop=False),
                    reads=[nw_b, Sbf_b], writes=[pUO_b], accum=True)
                yield
                kb.op("pe", lambda e: e.matmul(
                    pUO[:], nw_t[:, 128:256], Sbf_t[:, DH : 2 * DH],
                    start=False, stop=True),
                    reads=[nw_b, Sbf_b], writes=[pUO_b], accum=True)
                yield
            kb.op("dve", lambda e, rq_t=rq_t: e.tensor_scalar(
                uib_t[:], pUO[:], rq_t[:, 1:2], None, ALU.mult),
                reads=[pUO_b], sreads=[rq_b], writes=[uib_b])
            yield
            if not first:
                kb.op("pe", lambda e: e.matmul(
                    pUO[:], qg_t[ga][:, csl], Sbf_t[:, 0:DH],
                    start=True, stop=False),
                    reads=[qg_b[ga], Sbf_b], writes=[pUO_b])
                yield
                kb.op("pe", lambda e: e.matmul(
                    pUO[:], qg_t[gb][:, csl], Sbf_t[:, DH : 2 * DH],
                    start=False, stop=False),
                    reads=[qg_b[gb], Sbf_b], writes=[pUO_b], accum=True)
                yield
            kb.op("pe", lambda e, first=first, at_t=at_t: e.matmul(
                pUO[:], at_t[:], uib_t[:], start=first, stop=True),
                reads=[at_b, uib_b], writes=[pUO_b], accum=(not first))
            yield
            kb.op("dve", lambda e, rq_t=rq_t: e.tensor_scalar(
                otmp_t[:], pUO[:], rq_t[:, 0:1], None, ALU.mult),
                reads=[pUO_b], sreads=[rq_b], writes=[otmp_b])
            yield
            kb.op("pe", lambda e: e.transpose(
                pOT[:, 0:128], otmp_t[:, 0:128], eye_bf[:]),
                reads=[otmp_b, eye_bf_b], writes=[pOT_b])
            yield
            kb.op("pe", lambda e: e.transpose(
                pOT[:, 128:256], otmp_t[:, 128:256], eye_bf[:]),
                reads=[otmp_b, eye_bf_b], writes=[pOT_b], accum=True)
            yield
            kb.op("dve", lambda e, c=c: e.tensor_copy(
                dl_t[ga][:, c * 128 : c * 128 + 128], pOT[:, 0:128]),
                reads=[pOT_b], writes=[dl_b[ga]], accum=True)
            yield
            kb.op("dve", lambda e, c=c: e.tensor_copy(
                dl_t[gb][:, c * 128 : c * 128 + 128], pOT[:, 128:256]),
                reads=[pOT_b], writes=[dl_b[gb]], accum=True)
            yield
            if c < NCH - 1:
                kb.op("pe", lambda e, ktm_t=ktm_t: e.matmul(
                    pS[:, 0:256], ktm_t[:, 0:128], uib_t[:],
                    start=True, stop=True),
                    reads=[ktm_b, uib_b], writes=[pS_b])
                yield
                kb.op("pe", lambda e, ktm_t=ktm_t: e.matmul(
                    pS[:, 256:512], ktm_t[:, 128:256], uib_t[:],
                    start=False, stop=True),
                    reads=[ktm_b, uib_b], writes=[pS_b], accum=True)
                yield
                kb.op("dve", lambda e, h=h: e.tensor_tensor(
                    S_t[2 * h][:], S_t[2 * h][:], pS[:, 0:256], ALU.add),
                    reads=[S_b[2 * h], pS_b], writes=[S_b[2 * h]],
                    accum=True)
                yield
                kb.op("dve", lambda e, h=h: e.tensor_tensor(
                    S_t[2 * h + 1][:], S_t[2 * h + 1][:], pS[:, 256:512],
                    ALU.add),
                    reads=[S_b[2 * h + 1], pS_b], writes=[S_b[2 * h + 1]],
                    accum=True)
                yield

        gidx = 0
        for gq in range(4):
            conv_group(gq, "q", 0, cprq, cqw, qg_t, qg_b, "a_Wq", None)
            conv_group(gq, "k", G, cprk, ckw, kg_t, kg_b, "a_Wk", None)
            for go in range(G):
                if qf_ap is not None:
                    kb.dma("pool", kb.tap_stream,
                           qf_ap[go * 128 : (go + 1) * 128,
                                 gq * TBW : (gq + 1) * TBW],
                           qg_t[go][:], reads=[qg_b[go]])
                if kf_ap is not None:
                    kb.dma("pool", kb.tap_stream,
                           kf_ap[go * 128 : (go + 1) * 128,
                                 gq * TBW : (gq + 1) * TBW],
                           kg_t[go][:], reads=[kg_b[go]])
            its = [(gq * 4 + cc, h) for cc in range(4) for h in range(H)]
            M = len(its)
            for step in range(M + 2):
                batch = []
                if step < M:
                    batch.append(pre_gen(gidx + step, *its[step]))
                if 0 <= step - 1 < M:
                    batch.append(chain_gen(gidx + step - 1, *its[step - 1]))
                if 0 <= step - 2 < M:
                    batch.append(post_gen(gidx + step - 2, *its[step - 2]))
                while batch:
                    for g2 in list(batch):
                        try:
                            next(g2)
                        except StopIteration:
                            batch.remove(g2)
            gidx += M
        kb.tap("delta", [D, T], [(dl_t[g][:], dl_b[g],
                                  (lambda t, g=g: t[g * 128:(g + 1) * 128]))
                                 for g in range(G)])
        kb.tap("Sfin", [8 * 128, DH],
               [(S_t[i][:], S_b[i],
                 (lambda t, i=i: t[i * 128:(i + 1) * 128])) for i in range(8)])
        kb.barrier()

    with contextlib.suppress(_Skip), contextlib.ExitStack() as sB:
        if PHASE < 3:
            raise _Skip()
        lsb = SB(sB)
        ws = kb.stream("wstr")
        lo_t, lo_b, mi_t, mi_b = [], [], [], []
        for g in range(G):
            t, b = lsb(f"lo{g}", [128, T], BF16)
            lo_t.append(t)
            lo_b.append(b)
            t, b = lsb(f"mi{g}", [128, T], BF16)
            mi_t.append(t)
            mi_b.append(b)
        selb_t, selb_b = lsb("selb", [128, 14 * 132], BF16)
        kb.dma("pool", ws, selb_t[:], dr["a_selbank"], writes=[selb_b])
        psel_t, psel_b = [], []
        for s in range(16):
            t, b = lsb(f"psel{s}", [16, 128], BF16)
            kb.dma("pool", ws, t[:], dr["a_psel"][s], writes=[b])
            psel_t.append(t)
            psel_b.append(b)
        mixsel_t, mixsel_b, onsel_t, onsel_b = [], [], [], []
        hsel_t, hsel_b = [], []
        for g in range(G):
            t, b = lsb(f"mxs{g}", [4, 128], BF16)
            kb.dma("pool", ws, t[:], dr["a_mixsel"][g], writes=[b])
            mixsel_t.append(t)
            mixsel_b.append(b)
            t, b = lsb(f"ons{g}", [4, 128], BF16)
            kb.dma("pool", ws, t[:], dr["a_onsel"][g], writes=[b])
            onsel_t.append(t)
            onsel_b.append(b)
            t, b = lsb(f"hsl{g}", [128, 4], BF16)
            kb.dma("pool", ws, t[:], dr["a_hsel"][g], writes=[b])
            hsel_t.append(t)
            hsel_b.append(b)
        rb1_t, rb1_b = lsb("rb1", [128, 17], F32)
        kb.dma("sp", ws, rb1_t[:], dr["a_rb1"], writes=[rb1_b])
        rb2_t, rb2_b = lsb("rb2", [16, 1], F32)
        kb.dma("sp", ws, rb2_t[:], dr["a_rb2"], writes=[rb2_b])
        rw2_t, rw2_b = lsb("rw2", [128, 17 * 16], BF16)
        kb.dma("pool", ws, rw2_t[:].rearrange("p (m n) -> p m n", n=16),
               dr["a_rw2"].rearrange("m p n -> p m n"), writes=[rw2_b])
        clw, cmw = [], []
        for g in range(G):
            t, b = lsb(f"clw{g}", [128, 7], F32)
            kb.dma("sp", ws, t[:], dr["a_local_w"][g], writes=[b])
            clw.append((t, b))
            t, b = lsb(f"cmw{g}", [128, 25], F32)
            kb.dma("sp", ws, t[:], dr["a_mid_w"][g], writes=[b])
            cmw.append((t, b))
        feats_t, feats_b = lsb("feats", [128, T], BF16)
        sh32_t, sh32_b = lsb("sh32", [64, 64], BF16)
        kb.dma("pool", ws, sh32_t[:], dr["c_shift32"], writes=[sh32_b])
        lgbf_t, lgbf_b = lsb("lgbf", [16, T], BF16)
        pfm_t, pfm_b = lsb("pfm", [16, T], BF16)

        for g in range(G):
            for (out_t2, out_b2, cw2, K2) in ((lo_t[g], lo_b[g], clw[g], 7),
                                              (mi_t[g], mi_b[g], cmw[g], 25)):
                ct2, cb2 = cw2
                kb.op("dve", lambda e, g=g, ct2=ct2, o=out_t2, K2=K2:
                      e.tensor_scalar(o[:], v_t[g][:],
                                      ct2[:, K2 - 1 : K2], None, ALU.mult),
                      reads=[v_b[g], cb2], writes=[out_b2])
                for tau in range(1, K2):
                    kb.op("dve", lambda e, g=g, tau=tau, ct2=ct2, o=out_t2,
                          K2=K2: e.scalar_tensor_tensor(
                              o[:, tau:T], v_t[g][:, 0 : T - tau],
                              ct2[:, K2 - 1 - tau : K2 - tau], o[:, tau:T],
                              ALU.mult, ALU.add),
                          reads=[v_b[g], cb2, out_b2], writes=[out_b2],
                          accum=True)
        kb.tap("local", [D, T], [(lo_t[g][:], lo_b[g],
                                  (lambda t, g=g: t[g * 128:(g + 1) * 128]))
                                 for g in range(G)])
        kb.tap("mid", [D, T], [(mi_t[g][:], mi_b[g],
                                (lambda t, g=g: t[g * 128:(g + 1) * 128]))
                               for g in range(G)])

        branches = [(lo_t, lo_b), (mi_t, mi_b), (dl_t, dl_b), (v_t, v_b)]

        with contextlib.suppress(_Skip), contextlib.ExitStack() as sBs:
            if BSTAGE < 1:
                raise _Skip()
            lps1, lsb1 = PS(sBs), SB(sBs)
            pF, pF_b = lps1("pF", [128, TBW], F32)
            psh, psh_b = lps1("psh", [64, TBW], F32)
            prods = [lsb1(f"prd{i}", [128, TBW], BF16) for i in range(3)]
            msq_t, msq_b = lsb1("msq", [64, TBW], BF16)
            fcp_t, fcp_b = lsb1("fcp", [64, TBW], F32)
            kb.op("dve", lambda e: e.memset(feats_t[64:128, :], 0.0),
                  writes=[feats_b], accum=True)
            srows = _stat_rows()
            for tb in range(NTB):
                sl = slice(tb * TBW, (tb + 1) * TBW)
                nst = 0
                pi = 0
                for si, (kind, ops, r0, coef) in enumerate(srows):
                    for g in range(G):
                        h = g // 2
                        selap = selb_t[:, 132 * si + 3 - h :
                                       132 * si + 131 - h]
                        if kind == "mean":
                            bt, bb = branches[ops[0]]
                            rt_ap = bt[g][:, sl]
                            rdeps = [bb[g]]
                        else:
                            pt2, pb2 = prods[pi % 3]
                            pi += 1
                            if kind == "sq":
                                bt, bb = branches[ops[0]]
                                kb.op("act", lambda e, bt=bt, g=g, sl=sl,
                                      pt2=pt2, bb=bb: e.activation(
                                          pt2[:], bt[g][:, sl], AF.Square),
                                      reads=[bb[g]], writes=[pb2])
                            else:
                                at2, ab2 = branches[ops[0]]
                                bt2, bb2 = branches[ops[1]]
                                kb.op("dve", lambda e, at2=at2, bt2=bt2,
                                      g=g, sl=sl, pt2=pt2, ab2=ab2, bb2=bb2:
                                      e.tensor_tensor(
                                          pt2[:], at2[g][:, sl],
                                          bt2[g][:, sl], ALU.mult),
                                      reads=[ab2[g], bb2[g]], writes=[pb2])
                            rt_ap = pt2[:]
                            rdeps = [pb2]
                        kb.op("pe", lambda e, selap=selap, rt_ap=rt_ap,
                              nst=nst: e.matmul(
                                  pF[:], selap, rt_ap,
                                  start=(nst == 0), stop=(nst == 111)),
                              reads=[selb_b] + rdeps, writes=[pF_b],
                              accum=(nst > 0))
                        nst += 1
                kb.op("dve", lambda e: e.tensor_copy(fcp_t[:], pF[0:64, :]),
                      reads=[pF_b], writes=[fcp_b])
                kb.op("act", lambda e: e.activation(
                    msq_t[:], fcp_t[:], AF.Square,
                    scale=float(np.sqrt(DH / (DH - 1.0)))),
                    reads=[fcp_b], writes=[msq_b])
                kb.op("pe", lambda e: e.matmul(
                    psh[:], sh32_t[:], msq_t[:], start=True, stop=True),
                    reads=[sh32_b, msq_b], writes=[psh_b])
                kb.op("dve", lambda e, sl=sl: e.tensor_tensor(
                    feats_t[0:64, sl], fcp_t[:], psh[:], ALU.subtract),
                    reads=[fcp_b, psh_b], writes=[feats_b], accum=True)
            kb.tap("feats", [128, T], [(feats_t[:], feats_b, lambda t: t)])
            kb.barrier()

        with contextlib.suppress(_Skip), contextlib.ExitStack() as sBr:
            if BSTAGE < 2:
                raise _Skip()
            lps2, lsb2 = PS(sBr), SB(sBr)
            rw1s_t, rw1s_b = lsb2("rw1s", [128, 9 * 512], BF16)
            h1bf_t, h1bf_b = lsb2("h1bf", [128, TBW], BF16)
            pH = [lps2(f"pH{i}", [128, TBW], F32) for i in range(4)]
            pL = [lps2(f"pL{i}", [16, TBW], F32) for i in range(NTB)]
            rhs_all = xb_t + [feats_t]
            rhsb_all = xb_b + [feats_b]
            for mg in range(5):
                kb.dma("pool", kb.stream("rw1"),
                       rw1s_t[:].rearrange("p (kg n) -> p kg n", n=512),
                       dr["a_rw1"][mg].rearrange("kg p n -> p kg n"),
                       writes=[rw1s_b])
                nm_ch = 4 if mg < 4 else 1
                for tb in range(NTB):
                    sl = slice(tb * TBW, (tb + 1) * TBW)
                    for mi in range(nm_ch):
                        m = 4 * mg + mi
                        pt, pb = pH[mi]
                        for kg in range(9):
                            kb.op("pe", lambda e, kg=kg, mi=mi, pt=pt, sl=sl:
                                  e.matmul(
                                      pt[:],
                                      rw1s_t[:, kg * 512 + mi * 128 :
                                             kg * 512 + (mi + 1) * 128],
                                      rhs_all[kg][:, sl],
                                      start=(kg == 0), stop=(kg == 8)),
                                  reads=[rw1s_b, rhsb_all[kg]], writes=[pb],
                                  accum=(kg > 0))
                        kb.op("act", lambda e, m=m, pt=pt: e.activation(
                            h1bf_t[:], pt[:], AF.Silu,
                            bias=rb1_t[:, m : m + 1]),
                            reads=[pb, rb1_b], writes=[h1bf_b])
                        kb.op("pe", lambda e, m=m, tb=tb: e.matmul(
                            pL[tb][0][:], rw2_t[:, m * 16 : (m + 1) * 16],
                            h1bf_t[:], start=(m == 0), stop=(m == 16)),
                            reads=[rw2_b, h1bf_b], writes=[pL[tb][1]],
                            accum=(m > 0))
            for tb in range(NTB):
                sl = slice(tb * TBW, (tb + 1) * TBW)
                kb.op("dve", lambda e, tb=tb, sl=sl: e.tensor_scalar(
                    lgbf_t[:, sl], pL[tb][0][:], rb2_t[:], None, ALU.add),
                    reads=[pL[tb][1], rb2_b], writes=[lgbf_b], accum=True)
            kb.tap("logits", [16, T], [(lgbf_t[:], lgbf_b, lambda t: t)])
            kb.barrier()

        with contextlib.suppress(_Skip), contextlib.ExitStack() as sBm:
            if BSTAGE < 3:
                raise _Skip()
            lps3, lsb3 = PS(sBm), SB(sBm)
            ptt, ptt_b = lps3("ptt", [128, 512], BF16)
            pbc, pbc_b = lps3("pbc", [128, TBW], F32)
            prms, prms_b = lps3("prms", [4, 128], F32)
            pgn, pgn_b = lps3("pgn", [128, 128], F32)
            pwo, pwo_b = lps3("pwo", [128, TBW], F32)
            pwo2, pwo2_b = lps3("pwo2", [128, TBW], F32)
            ptm_sm, ptm_sm_b = lsb3("ptm_sm", [128, 16], F32)
            exp_sm, exp_sm_b = lsb3("exp_sm", [128, 16], F32)
            col_sm, col_sm_b = lsb3("col_sm", [128, 4], F32)
            pback_t, pback_b = lsb3("pback", [128, 16], BF16)
            for tc in range(NCH):
                kb.op("pe", lambda e, tc=tc: e.transpose(
                    ptt[:, 0:16],
                    lgbf_t[:, tc * 128 : (tc + 1) * 128],
                    eye_bf[0:16, 0:16]),
                    reads=[lgbf_b, eye_bf_b], writes=[ptt_b])
                kb.op("dve", lambda e: e.tensor_copy(
                    ptm_sm[:], ptt[:, 0:16]),
                    reads=[ptt_b], writes=[ptm_sm_b])
                for h in range(H):
                    hs = slice(4 * h, 4 * h + 4)
                    kb.op("dve", lambda e, hs=hs, h=h: e.tensor_reduce(
                        col_sm[:, h : h + 1], ptm_sm[:, hs], AX.X, ALU.max,
                        negate=True),
                        reads=[ptm_sm_b], writes=[col_sm_b], accum=True)
                for h in range(H):
                    hs = slice(4 * h, 4 * h + 4)
                    kb.op("act", lambda e, hs=hs, h=h: e.activation(
                        exp_sm[:, hs], ptm_sm[:, hs], AF.Exp,
                        bias=col_sm[:, h : h + 1]),
                        reads=[ptm_sm_b, col_sm_b], writes=[exp_sm_b],
                        accum=True)
                for h in range(H):
                    hs = slice(4 * h, 4 * h + 4)
                    kb.op("dve", lambda e, hs=hs, h=h: e.tensor_reduce(
                        col_sm[:, h : h + 1], exp_sm[:, hs], AX.X, ALU.add),
                        reads=[exp_sm_b], writes=[col_sm_b], accum=True)
                kb.op("dve", lambda e: e.reciprocal(col_sm[:], col_sm[:]),
                      reads=[col_sm_b], writes=[col_sm_b], accum=True)
                for h in range(H):
                    hs = slice(4 * h, 4 * h + 4)
                    kb.op("dve", lambda e, hs=hs, h=h: e.tensor_scalar(
                        exp_sm[:, hs], exp_sm[:, hs], col_sm[:, h : h + 1],
                        1.0 - 4 * FLOOR, ALU.mult, ALU.mult),
                        reads=[exp_sm_b], sreads=[col_sm_b],
                        writes=[exp_sm_b], accum=True)
                kb.op("dve", lambda e: e.tensor_scalar(
                    pback_t[:], exp_sm[:], FLOOR, None, ALU.add),
                    reads=[exp_sm_b], writes=[pback_b])
                if tc == 0 and "smdbg" in DBG:
                    smd = nc.dram_tensor("dbg_smdbg", [128, 52], F32,
                                         kind="ExternalOutput").ap()
                    kb.taps["smdbg"] = smd
                    kb.dma("pool", kb.tap_stream, smd[:, 0:16], ptm_sm[:],
                           reads=[ptm_sm_b])
                    kb.dma("pool", kb.tap_stream, smd[:, 16:32], exp_sm[:],
                           reads=[exp_sm_b])
                    kb.dma("pool", kb.tap_stream, smd[:, 32:36], col_sm[:],
                           reads=[col_sm_b])
                    kb.dma("pool", kb.tap_stream, smd[:, 36:52], pback_t[:],
                           reads=[pback_b])
                kb.op("pe", lambda e: e.transpose(
                    ptt[0:16, 128:256], pback_t[:], eye_bf[:]),
                    reads=[pback_b, eye_bf_b], writes=[ptt_b], accum=True)
                kb.op("dve", lambda e, tc=tc: e.tensor_copy(
                    pfm_t[:, tc * 128 : (tc + 1) * 128],
                    ptt[0:16, 128:256]),
                    reads=[ptt_b], writes=[pfm_b], accum=True)
            kb.tap("p", [16, T], [(pfm_t[:], pfm_b, lambda t: t)])

            pbcsb_t, pbcsb_b = lsb3("pbcsb", [128, TBW], BF16)
            mxt, mxt_b = lsb3("mxt", [128, TBW], BF16)
            sqm_t, sqm_b = lsb3("sqm", [128, 128], BF16)
            rmsbf_t, rmsbf_b = lsb3("rmsbf", [4, 128], BF16)
            rmsrow_t, rmsrow_b = lsb3("rmsrow", [4, 128], F32)
            mix_t = xb_t
            mix_b = xb_b
            for tb in range(NTB):
                sl = slice(tb * TBW, (tb + 1) * TBW)
                for h in range(H):
                    for i in range(4):
                        slot = 4 * h + i
                        kb.op("pe", lambda e, slot=slot, sl=sl: e.matmul(
                            pbc[:], psel_t[slot][:], pfm_t[:, sl],
                            start=True, stop=True),
                            reads=[psel_b[slot], pfm_b], writes=[pbc_b])
                        kb.op("dve", lambda e: e.tensor_copy(
                            pbcsb_t[:], pbc[:]),
                            reads=[pbc_b], writes=[pbcsb_b])
                        for g in (2 * h, 2 * h + 1):
                            bt, bb = branches[i]
                            if i == 0:
                                kb.op("dve", lambda e, g=g, bt=bt, sl=sl,
                                      bb=bb: e.tensor_tensor(
                                          mix_t[g][:, 0:TBW], bt[g][:, sl],
                                          pbcsb_t[:], ALU.mult),
                                      reads=[bb[g], pbcsb_b],
                                      writes=[mix_b[g]])
                            else:
                                kb.op("dve", lambda e, g=g, bt=bt, sl=sl,
                                      bb=bb: e.tensor_tensor(
                                          mxt[:], bt[g][:, sl], pbcsb_t[:],
                                          ALU.mult),
                                      reads=[bb[g], pbcsb_b], writes=[mxt_b])
                                kb.op("dve", lambda e, g=g: e.tensor_tensor(
                                    mix_t[g][:, 0:TBW], mix_t[g][:, 0:TBW],
                                    mxt[:], ALU.add),
                                    reads=[mix_b[g], mxt_b],
                                    writes=[mix_b[g]], accum=True)
                for stage in range(2):
                    selset = mixsel_t if stage == 0 else onsel_t
                    selsetb = mixsel_b if stage == 0 else onsel_b
                    for sc in range(4):
                        scl = slice(sc * 128, (sc + 1) * 128)
                        for g in range(G):
                            kb.op("act", lambda e, g=g, scl=scl:
                                  e.activation(sqm_t[:], mix_t[g][:, scl],
                                               AF.Square),
                                  reads=[mix_b[g]], writes=[sqm_b])
                            kb.op("pe", lambda e, g=g: e.matmul(
                                prms[:], hsel_t[g][:], sqm_t[:],
                                start=(g == 0), stop=(g == G - 1)),
                                reads=[hsel_b[g], sqm_b], writes=[prms_b],
                                accum=(g > 0))
                        kb.op("dve", lambda e: e.tensor_scalar(
                            rmsrow_t[:], prms[:], 1.0 / DH, None, ALU.mult),
                            reads=[prms_b], writes=[rmsrow_b])
                        kb.op("act", lambda e: e.activation(
                            rmsrow_t[:], rmsrow_t[:], AF.Sqrt, bias=1e-5),
                            reads=[rmsrow_b, constf(1e-5)],
                            writes=[rmsrow_b], accum=True)
                        kb.op("dve", lambda e: e.reciprocal(
                            rmsrow_t[:], rmsrow_t[:]),
                            reads=[rmsrow_b], writes=[rmsrow_b], accum=True)
                        kb.op("dve", lambda e: e.tensor_copy(
                            rmsbf_t[:], rmsrow_t[:]),
                            reads=[rmsrow_b], writes=[rmsbf_b])
                        for g in range(G):
                            kb.op("pe", lambda e, g=g, selset=selset:
                                  e.matmul(pgn[:], selset[g][:], rmsbf_t[:],
                                           start=True, stop=True),
                                  reads=[selsetb[g], rmsbf_b],
                                  writes=[pgn_b])
                            if stage == 0:
                                kb.op("dve", lambda e, g=g, scl=scl:
                                      e.tensor_tensor(
                                          mix_t[g][:, scl],
                                          mix_t[g][:, scl], pgn[:],
                                          ALU.mult),
                                      reads=[mix_b[g], pgn_b],
                                      writes=[mix_b[g]], accum=True)
                            else:
                                osl = slice(tb * TBW + sc * 128,
                                            tb * TBW + (sc + 1) * 128)
                                kb.op("dve", lambda e, g=g, scl=scl, osl=osl:
                                      e.tensor_tensor(
                                          dl_t[g][:, osl],
                                          mix_t[g][:, scl], pgn[:],
                                          ALU.mult),
                                      reads=[mix_b[g], pgn_b],
                                      writes=[dl_b[g]], accum=True)
            kb.tap("o_n", [D, T], [(dl_t[g][:], dl_b[g],
                                    (lambda t, g=g:
                                     t[g * 128:(g + 1) * 128]))
                                   for g in range(G)])

            with contextlib.suppress(_Skip), contextlib.ExitStack() as sWo:
                if BSTAGE < 4:
                    raise _Skip()
                lsbW = SB(sWo)
                wos = [lsbW("wo0", [128, D], BF16),
                       lsbW("wo1", [128, D], BF16)]
                for go in range(G):
                    wtile, wbuf = wos[go % 2]
                    kb.dma("pool", kb.stream(f"wq{go % 2}"),
                           wtile[:].rearrange("p (gi m) -> p gi m", m=128),
                           dr["a_Wo"][go].rearrange("gi p m -> p gi m"),
                           writes=[wbuf])
                    for tb in range(NTB):
                        pt, pb = (pwo, pwo_b) if tb % 2 == 0 else \
                            (pwo2, pwo2_b)
                        sl = slice(tb * TBW, (tb + 1) * TBW)
                        proj_mms(pt[:], pb, wtile, wbuf, dl_t, dl_b, sl)
                        kb.op("act", lambda e, go=go, sl=sl, pt=pt:
                              e.activation(xb_t[go][:, sl], pt[:],
                                           AF.Identity),
                              reads=[pb], writes=[xb_b[go]], accum=True)
                for g in range(G):
                    kb.dma("sp", kb.stream("x_io"), aout_sp[g], xb_t[g][:],
                           reads=[xb_b[g]])
                kb.tap("aout", [D, T], [(xb_t[g][:], xb_b[g],
                                         (lambda t, g=g:
                                          t[g * 128:(g + 1) * 128]))
                                        for g in range(G)])
            kb.barrier()
        kb.barrier()

    s1.close()

    if PHASE < 4:
        with contextlib.ExitStack() as sO:
            lsb4 = SB(sO)
            dum_t, dum_b = lsb4("dum", [128, D], F32)
            kb.op("dve", lambda e: e.memset(dum_t[:], 0.0), writes=[dum_b])
            outs = kb.stream("out_store")
            for tc in range(NCH):
                kb.dma("sp", outs, out_ap[tc * 128 : (tc + 1) * 128, :],
                       dum_t[:], reads=[dum_b])
        kb.finish()
        st.close()
        return kb

    sR = st.enter_context(contextlib.ExitStack())
    sbR = SB(sR)
    x2_t, x2_b, xb2_t, xb2_b = [], [], [], []
    for g in range(G):
        t, b = sbR(f"x2_{g}", [128, T], F32)
        x2_t.append(t)
        x2_b.append(b)
    for g in range(G):
        t, b = sbR(f"xb2_{g}", [128, T], BF16)
        xb2_t.append(t)
        xb2_b.append(b)
    xio = kb.stream("x_io")
    for g in range(G):
        kb.dma("sp", xio, x2_t[g][:], xsp[g], writes=[x2_b[g]])
        kb.dma("sp", xio, xb2_t[g][:], aout_sp[g], writes=[xb2_b[g]])
    for g in range(G):
        kb.op("dve", lambda e, g=g: e.tensor_tensor(
            x2_t[g][:], x2_t[g][:], xb2_t[g][:], ALU.add),
            reads=[x2_b[g], xb2_b[g]], writes=[x2_b[g]], accum=True)
    kb.tap("x1", [D, T], [(x2_t[g][:], x2_b[g],
                           (lambda t, g=g: t[g * 128:(g + 1) * 128]))
                          for g in range(G)])

    def ffn_phase(tag):
        with contextlib.ExitStack() as sF:
            lsb, lps = SB(sF), PS(sF)
            layernorm(tag, x2_t, x2_b, xb2_t, xb2_b, sF)
            ws = kb.stream("wstr")
            a1r1 = []
            for g in range(G):
                t, b = lsb(f"f{tag}a1r1_{g}", [128, 27], BF16)
                kb.dma("pool", ws, t[:], dr["f_a1r1"][g], writes=[b])
                a1r1.append((t, b))
            b2c_t, b2c_b = lsb(f"f{tag}b2c", [128, D], BF16)
            kb.dma("pool", ws, b2c_t[:], dr["f_b2cat"], writes=[b2c_b])
            b2_t, b2_b = lsb(f"f{tag}b2", [128, G], F32)
            kb.dma("sp", ws, b2_t[:], dr["f_b2"], writes=[b2_b])
            w1h_t, w1h_b = lsb(f"f{tag}w1h", [128, 9 * 2048], BF16)
            w2h_t, w2h_b = lsb(f"f{tag}w2h", [128, 16 * 1024], BF16)
            a2r2_t, a2r2_b = lsb(f"f{tag}a2r2", [128, 2 * 16 * 27], BF16)
            b1_t, b1_b = lsb(f"f{tag}b1", [128, 32], F32)
            kb.dma("sp", ws, b1_t[:], dr["f_b1"], writes=[b1_b])
            kb.dma("pool", ws, a2r2_t[:].rearrange("p (h m n) -> p h m n", m=16, n=27),
                   dr["f_a2r2"].rearrange("h m p n -> p h m n"),
                   writes=[a2r2_b])
            zgfm_t, zgfm_b = lsb(f"f{tag}zgfm", [128, T], BF16)
            zg2fm_t, zg2fm_b = lsb(f"f{tag}zg2fm", [128, 512], BF16)
            z1sb_t, z1sb_b = lsb(f"f{tag}z1sb", [128, 27], F32)
            gt_t, gt_b = lsb(f"f{tag}gt", [128, 4], F32)
            zgtm_t, zgtm_b = lsb(f"f{tag}zgtm", [128, 24], BF16)
            z2sb = [lsb(f"f{tag}z2_{i}", [128, 108], F32) for i in range(4)]
            hsb = [lsb(f"f{tag}hsb{m}", [128, 512], BF16) for m in range(16)]
            pZ, pZ_b = lps(f"f{tag}pZ", [128, 108], F32)
            pH1, pH1_b = lps(f"f{tag}pH1", [128, 512], F32)
            pH2, pH2_b = lps(f"f{tag}pH2", [128, 512], F32)
            pTZ, pTZ_b = lps(f"f{tag}pTZ", [128, 128], BF16)
            pOg = [lps(f"f{tag}pO{i}", [128, 512], F32) for i in range(2)]

            kb.op("dve", lambda e: e.memset(zgfm_t[:], 0.0), writes=[zgfm_b])
            kb.op("dve", lambda e: e.memset(zg2fm_t[:], 0.0),
                  writes=[zg2fm_b])

            def gates_zg(z_ap, z_deps, dst_ap, dst_b):
                kb.op("dve", lambda e: e.tensor_copy(z1sb_t[:], z_ap),
                      reads=z_deps, writes=[z1sb_b])
                kb.op("dve", lambda e: e.tensor_reduce(
                    gt_t[:, 3:4], z1sb_t[:, 24:27], AX.X, ALU.max,
                    negate=True),
                    reads=[z1sb_b], writes=[gt_b])
                kb.op("act", lambda e: e.activation(
                    gt_t[:, 0:3], z1sb_t[:, 24:27], AF.Exp,
                    bias=gt_t[:, 3:4]),
                    reads=[z1sb_b, gt_b], writes=[gt_b], accum=True)
                kb.op("dve", lambda e: e.tensor_reduce(
                    gt_t[:, 3:4], gt_t[:, 0:3], AX.X, ALU.add),
                    reads=[gt_b], writes=[gt_b], accum=True)
                kb.op("dve", lambda e: e.reciprocal(
                    gt_t[:, 3:4], gt_t[:, 3:4]),
                    reads=[gt_b], writes=[gt_b], accum=True)
                for ei in range(3):
                    kb.op("dve", lambda e, ei=ei: e.tensor_scalar(
                        zgtm_t[:, 8 * ei : 8 * ei + 8],
                        z1sb_t[:, 8 * ei : 8 * ei + 8],
                        gt_t[:, ei : ei + 1], gt_t[:, 3:4],
                        ALU.mult, ALU.mult),
                        reads=[z1sb_b], sreads=[gt_b], writes=[zgtm_b],
                        accum=True)
                kb.op("dve", lambda e: e.tensor_scalar(
                    zgtm_t[:], zgtm_t[:], 0.125, None, ALU.mult),
                    reads=[zgtm_b], writes=[zgtm_b], accum=True)
                kb.op("pe", lambda e: e.transpose(
                    pTZ[0:24, 0:128], zgtm_t[:], eye_bf[:]),
                    reads=[zgtm_b, eye_bf_b], writes=[pTZ_b])
                kb.op("act", lambda e: e.activation(
                    dst_ap, pTZ[0:24, 0:128], AF.Identity),
                    reads=[pTZ_b], writes=[dst_b], accum=True)

            for tc in range(NCH):
                sl1 = slice(tc * 128, (tc + 1) * 128)
                for gi in range(G):
                    t2, b2cc = a1r1[gi]
                    kb.op("pe", lambda e, gi=gi, sl1=sl1, t2=t2: e.matmul(
                        pZ[:, 0:27], xb2_t[gi][:, sl1], t2[:],
                        start=(gi == 0), stop=(gi == G - 1)),
                        reads=[b2cc, xb2_b[gi]], writes=[pZ_b],
                        accum=(gi > 0))
                gates_zg(pZ[:, 0:27], [pZ_b],
                         zgfm_t[0:24, tc * 128 : (tc + 1) * 128], zgfm_b)

            kb.tap(f"zgfm{tag}", [128, T],
                   [(zgfm_t[:], zgfm_b, lambda t: t)])
            rhs9 = xb2_t + [zgfm_t]
            rhs9b = xb2_b + [zgfm_b]
            for half in range(2):
                kb.dma("pool", kb.stream("fw1"),
                       w1h_t[:].rearrange("p (kg n) -> p kg n", n=2048),
                       dr["f_w1"][half].rearrange("kg p n -> p kg n"),
                       writes=[w1h_b])
                kb.dma("pool", kb.stream("fw2"),
                       w2h_t[:].rearrange("p (m n) -> p m n", n=1024),
                       dr["f_w2"][half].rearrange("m p n -> p m n"),
                       writes=[w2h_b])
                for tb in range(4):
                    sl = slice(tb * 512, (tb + 1) * 512)
                    z2t, z2b = z2sb[tb]
                    def z2_mms(m, ht, hb, half=half):
                        for ts2 in range(4):
                            kb.op("pe", lambda e, ts2=ts2, half=half, m=m,
                                  ht=ht: e.matmul(
                                      pZ[:, 27 * ts2 : 27 * ts2 + 27],
                                      ht[:, 128 * ts2 : 128 * ts2 + 128],
                                      a2r2_t[:, (16 * half + m) * 27 :
                                             (16 * half + m + 1) * 27],
                                      start=(m == 0 and ts2 == 0),
                                      stop=(m == 15 and ts2 == 3)),
                                  reads=[hb, a2r2_b], writes=[pZ_b],
                                  accum=(not (m == 0 and ts2 == 0)))
                    pend = None
                    for m in range(16):
                        pt, pb = (pH1, pH1_b) if m % 2 == 0 else (pH2, pH2_b)
                        ht, hb = hsb[m]
                        for kg in range(9):
                            kb.op("pe", lambda e, kg=kg, m=m, sl=sl, pt=pt:
                                  e.matmul(
                                      pt[:],
                                      w1h_t[:, kg * 2048 + m * 128 :
                                            kg * 2048 + (m + 1) * 128],
                                      rhs9[kg][:, sl],
                                      start=(kg == 0), stop=(kg == 8)),
                                  reads=[w1h_b, rhs9b[kg]], writes=[pb],
                                  accum=(kg > 0))
                        if pend is not None:
                            z2_mms(*pend)
                        kb.op("act", lambda e, half=half, m=m, pt=pt, ht=ht:
                              e.activation(
                                  ht[:], pt[:], AF.Relu,
                                  bias=b1_t[:, 16 * half + m :
                                            16 * half + m + 1]),
                              reads=[pb, b1_b], writes=[hb])
                        pend = (m, ht, hb)
                    z2_mms(*pend)
                    if half == 0:
                        kb.op("dve", lambda e, z2t=z2t: e.tensor_copy(
                            z2t[:], pZ[:]), reads=[pZ_b], writes=[z2b])
                    else:
                        kb.op("dve", lambda e, z2t=z2t: e.tensor_tensor(
                            z2t[:], z2t[:], pZ[:], ALU.add),
                            reads=[z2b, pZ_b], writes=[z2b], accum=True)
                        for ts2 in range(4):
                            gates_zg(z2t[:, 27 * ts2 : 27 * ts2 + 27], [z2b],
                                     zg2fm_t[0:24, 128 * ts2 :
                                             128 * ts2 + 128], zg2fm_b)
                    for go in range(G):
                        pt2, pb2 = pOg[go % 2]
                        for m in range(16):
                            kb.op("pe", lambda e, go=go, m=m, pt2=pt2,
                                  half=half: e.matmul(
                                      pt2[:],
                                      w2h_t[:, m * 1024 + go * 128 :
                                            m * 1024 + (go + 1) * 128],
                                      hsb[m][0][:],
                                      start=(m == 0),
                                      stop=(half == 0 and m == 15)),
                                  reads=[w2h_b, hsb[m][1]],
                                  writes=[pb2], accum=(m > 0))
                        if half == 0:
                            kb.op("dve", lambda e, go=go, sl=sl, pt2=pt2:
                                  e.tensor_tensor(
                                      x2_t[go][:, sl], x2_t[go][:, sl],
                                      pt2[:], ALU.add),
                                  reads=[pb2, x2_b[go]], writes=[x2_b[go]],
                                  accum=True)
                        else:
                            kb.op("pe", lambda e, go=go, pt2=pt2: e.matmul(
                                      pt2[:],
                                      b2c_t[:, go * 128 : (go + 1) * 128],
                                      zg2fm_t[:],
                                      start=False, stop=True),
                                  reads=[b2c_b, zg2fm_b], writes=[pb2],
                                  accum=True)
                            kb.op("dve", lambda e, go=go, sl=sl, pt2=pt2:
                                  e.scalar_tensor_tensor(
                                      x2_t[go][:, sl], pt2[:],
                                      b2_t[:, go : go + 1],
                                      x2_t[go][:, sl], ALU.add, ALU.add),
                                  reads=[pb2, b2_b, x2_b[go]],
                                  writes=[x2_b[go]], accum=True)
            kb.tap(f"x_ffn{tag}", [D, T],
                   [(x2_t[g][:], x2_b[g],
                     (lambda t, g=g: t[g * 128:(g + 1) * 128]))
                    for g in range(G)])
            kb.barrier()

    if PHASE >= 4:
        ffn_phase("2")

    def attnb_phase():
        qb_t, qb_b, kbh_t, kbh_b = [], [], [], []
        with contextlib.ExitStack() as sD:
            lsb, lps = SB(sD), PS(sD)
            layernorm("3", x2_t, x2_b, xb2_t, xb2_b, sD)
            ws = kb.stream("wstr")
            for g in range(G):
                t, b = lsb(f"bq{g}", [128, T], BF16)
                qb_t.append(t)
                qb_b.append(b)
            for g in range(4):
                t, b = lsb(f"bk{g}", [128, T], BF16)
                kbh_t.append(t)
                kbh_b.append(b)
            vtm_t, vtm_b = [], []
            for kvh in range(KVH):
                t, b = lsb(f"bvt{kvh}", [128, NCH * 65], BF16)
                vtm_t.append(t)
                vtm_b.append(b)
                kb.op("dve", lambda e, t=t: e.memset(t[:], 1.0), writes=[b])
            wst = [lsb("bwst0", [128, D], BF16), lsb("bwst1", [128, D], BF16)]
            vfs_t, vfs_b = lsb("bvfs", [128, TBW], BF16)
            sqb_t, sqb_b = lsb("bsqb", [128, TBW], BF16)
            nrm_t, nrm_b = lsb("bnrm", [128, 32], F32)
            nbb_t, nbb_b = lsb("bnbb", [128, 16], F32)
            wcol_t, wcol_b = lsb("bwcol", [128, 2], F32)
            mw_t, mw_b = lsb("bmw", [1, 1], F32)
            mwb_t, mwb_b = lsb("bmwb", [1, 1], BF16)
            ones2_t, ones2_b = lsb("bones2", [128, 2], BF16)
            e8_t, e8_b = lsb("be8", [32, 16], BF16)
            kb.dma("sp", ws, mw_t[:], dr["b_mw"], writes=[mw_b])
            kb.dma("pool", ws, ones2_t[:], dr["c_ones2"], writes=[ones2_b])
            kb.dma("pool", ws, e8_t[:], dr["b_e8"], writes=[e8_b])

            sD1 = contextlib.ExitStack()
            lps1 = PS(sD1)
            pprj, pprj_b = lps1("bpprj", [128, TBW], F32)
            pprj2, pprj2_b = lps1("bpprj2", [128, TBW], F32)
            ptv, ptv_b = lps1("bptv", [128, 1024], BF16)
            pnrm, pnrm_b = lps1("bpnrm", [128, 32], F32)

            def stream_proj(wnm, ngo, outfn):
                for go in range(ngo):
                    wtile, wbuf = wst[go % 2]
                    kb.dma("pool", ws, wtile[:].rearrange("p (gi m) -> p gi m", m=128),
                           dr[wnm][go].rearrange("gi p m -> p gi m"),
                           writes=[wbuf])
                    for tb in range(NTB):
                        pt, pb = (pprj, pprj_b) if tb % 2 == 0 else \
                            (pprj2, pprj2_b)
                        sl = slice(tb * TBW, (tb + 1) * TBW)
                        proj_mms(pt[:], pb, wtile, wbuf, xb2_t, xb2_b, sl)
                        outfn(go, tb, sl, pt, pb)

            stream_proj("b_Wq", G, lambda go, tb, sl, pt, pb: kb.op(
                "act", lambda e, go=go, sl=sl, pt=pt: e.activation(
                    qb_t[go][:, sl], pt[:], AF.Identity),
                reads=[pb], writes=[qb_b[go]], accum=True))
            stream_proj("b_Wk", 4, lambda go, tb, sl, pt, pb: kb.op(
                "act", lambda e, go=go, sl=sl, pt=pt: e.activation(
                    kbh_t[go][:, sl], pt[:], AF.Identity),
                reads=[pb], writes=[kbh_b[go]], accum=True))

            def v_out(go, tb, sl, pt, pb):
                kb.op("act", lambda e, pt=pt: e.activation(
                    vfs_t[:], pt[:], AF.Identity),
                    reads=[pb], writes=[vfs_b])
                for j in range(4):
                    tc = tb * 4 + j
                    kb.op("pe", lambda e, j=j: e.transpose(
                        ptv[:, j * 256 : j * 256 + 128],
                        vfs_t[:, j * 128 : (j + 1) * 128], eye_bf[:]),
                        reads=[vfs_b, eye_bf_b], writes=[ptv_b], accum=True)
                    for half in range(2):
                        kvh = 2 * go + half
                        kb.op("dve", lambda e, j=j, half=half, kvh=kvh,
                              tc=tc: e.tensor_copy(
                                  vtm_t[kvh][:, tc * 65 : tc * 65 + 64],
                                  ptv[:, j * 256 + half * 64 :
                                      j * 256 + half * 64 + 64]),
                              reads=[ptv_b], writes=[vtm_b[kvh]],
                              accum=True)
            stream_proj("b_Wv", 4, v_out)

            kb.op("dve", lambda e: e.memset(nrm_t[:], 0.0), writes=[nrm_b])
            for src_t, src_b, ncols, base in ((qb_t, qb_b, G, 0),
                                              (kbh_t, kbh_b, 4, 16)):
                for g in range(ncols):
                    for tb in range(NTB):
                        sl = slice(tb * TBW, (tb + 1) * TBW)
                        kb.op("act", lambda e, g=g, sl=sl, src_t=src_t:
                              e.activation(sqb_t[:], src_t[g][:, sl],
                                           AF.Square),
                              reads=[src_b[g]], writes=[sqb_b])
                        for j in range(4):
                            kb.op("pe", lambda e, j=j, g=g, base=base:
                                  e.matmul(
                                      pnrm[:, base + 2 * g :
                                           base + 2 * g + 2],
                                      sqb_t[:, j * 128 : (j + 1) * 128],
                                      ones2_t[:],
                                      start=True, stop=True),
                                  reads=[sqb_b, ones2_b], writes=[pnrm_b])
                            kb.op("dve", lambda e, g=g, base=base:
                                  e.tensor_tensor(
                                      nrm_t[:, base + 2 * g :
                                            base + 2 * g + 2],
                                      nrm_t[:, base + 2 * g :
                                            base + 2 * g + 2],
                                      pnrm[:, base + 2 * g :
                                           base + 2 * g + 2], ALU.max),
                                  reads=[nrm_b, pnrm_b], writes=[nrm_b],
                                  accum=True)
            kb.op("dve", lambda e: e.tensor_copy(sqb_t[:, 0:32], nrm_t[:]),
                  reads=[nrm_b], writes=[sqb_b])
            kb.op("pe", lambda e: e.transpose(
                ptv[0:32, 0:128], sqb_t[:, 0:32], eye_bf[:]),
                reads=[sqb_b, eye_bf_b], writes=[ptv_b])
            kb.op("dve", lambda e: e.tensor_reduce(
                nrm_t[0:32, 0:1], ptv[0:32, 0:128], AX.X, ALU.max),
                reads=[ptv_b], writes=[nrm_b])
            kb.op("act", lambda e: e.activation(
                nrm_t[0:32, 0:1], nrm_t[0:32, 0:1], AF.Sqrt),
                reads=[nrm_b], writes=[nrm_b], accum=True)
            kb.op("dve", lambda e: e.tensor_copy(
                sqb_t[0:32, 0:1], nrm_t[0:32, 0:1]),
                reads=[nrm_b], writes=[sqb_b])
            kb.op("pe", lambda e: e.matmul(
                pnrm[0:16, 2:3], e8_t[:], sqb_t[0:32, 0:1],
                start=True, stop=True),
                reads=[e8_b, sqb_b], writes=[pnrm_b])
            kb.op("dve", lambda e: e.tensor_tensor(
                nrm_t[0:16, 1:2], nrm_t[0:16, 0:1], pnrm[0:16, 2:3],
                ALU.mult),
                reads=[nrm_b, pnrm_b], writes=[nrm_b], accum=True)
            kb.op("dve", lambda e: e.tensor_scalar(
                nrm_t[0:16, 2:3], nrm_t[0:16, 1:2], -0.125, -1.0,
                ALU.mult, ALU.add),
                reads=[nrm_b], writes=[nrm_b], accum=True)
            kb.op("dve", lambda e: e.tensor_copy(
                sqb_t[0:16, 0:1], nrm_t[0:16, 2:3]),
                reads=[nrm_b], writes=[sqb_b])
            kb.op("pe", lambda e: e.transpose(
                ptv[0:1, 0:16], sqb_t[0:16, 0:1], eye_bf[0:16, 0:16]),
                reads=[sqb_b, eye_bf_b], writes=[ptv_b], accum=True)
            kb.op("dve", lambda e: e.tensor_copy(
                sqb_t[0:1, 0:16], ptv[0:1, 0:16]),
                reads=[ptv_b], writes=[sqb_b])
            kb.op("pe", lambda e: e.matmul(
                pnrm[:, 16:32], ones_row[:], sqb_t[0:1, 0:16],
                start=True, stop=True),
                reads=[ones_row_b, sqb_b], writes=[pnrm_b])
            kb.op("dve", lambda e: e.tensor_copy(nbb_t[:], pnrm[:, 16:32]),
                  reads=[pnrm_b], writes=[nbb_b])
            kb.op("act", lambda e: e.activation(
                mwb_t[:], mw_t[:], AF.Sigmoid), reads=[mw_b], writes=[mwb_b])
            kb.op("pe", lambda e: e.matmul(
                pnrm[:, 0:1], ones_row[:], mwb_t[:], start=True, stop=True),
                reads=[ones_row_b, mwb_b], writes=[pnrm_b])
            kb.op("dve", lambda e: e.tensor_copy(
                wcol_t[:, 0:1], pnrm[:, 0:1]),
                reads=[pnrm_b], writes=[wcol_b])
            kb.op("dve", lambda e: e.tensor_scalar(
                wcol_t[:, 1:2], wcol_t[:, 0:1], -1.0, 1.0,
                ALU.mult, ALU.add),
                reads=[wcol_b], writes=[wcol_b], accum=True)
            kb.tap("bounds", [128, 16], [(nbb_t[:], nbb_b, lambda t: t)])
            kb.barrier()
            sD1.close()

            with contextlib.ExitStack() as sD2:
                lsb2, lps2 = SB(sD2), PS(sD2)
                mem_tiles = []
                for kc in range(4):
                    t, b = lsb2(f"bmem{kc}", [128, KV], BF16)
                    kb.dma("pool", ws, t[:], dr["b_mem"][kc], writes=[b])
                    mem_tiles.append((t, b))
                memn_tiles = []
                for kc in range(4):
                    t, b = lsb2(f"bmemn{kc}", [128, 1], BF16)
                    kb.dma("pool", ws, t[:], dr["b_memn"][kc], writes=[b])
                    memn_tiles.append((t, b))
                el1_t, el1_b = lsb2("bel1", [128, T], BF16)
                numt_t, numt_b = lsb2("bnum", [128, NCH * KV], BF16)
                rden_t, rden_b = lsb2("brden", [128, NCH], F32)
                etile2 = [lsb2(f"betile{i}", [128, T], BF16)
                          for i in range(2)]
                asb_t, asb_b = lsb2("basb", [128, NCH * 65], F32)
                rdn_t, rdn_b = lsb2("brdn", [128, NCH], F32)
                t1_t, t1_b = lsb2("bt1", [128, DHB], BF16)
                t2_t, t2_b = lsb2("bt2", [128, DHB], BF16)
                psc = [lps2("bpsc0", [128, TBW], F32),
                       lps2("bpsc1", [128, TBW], F32)]
                pad3 = [lps2(f"bpad{k}", [128, 512], F32) for k in range(3)]
                pnum, pnum_b = lps2("bpnum", [128, KV], F32)
                pden, pden_b = lps2("bpden", [128, NCH], F32)
                ptrx, ptrx_b = lps2("bptrx", [128, 128], BF16)

                qloc = {}
                for j in range(G):
                    qloc[QA[j]] = (j, 0)
                    qloc[QB[j]] = (j, 64)

                sigma_t = xb2_t[:4]
                sigma_b = xb2_b[:4]

                def blend_head(o, par):
                    kb.op("dve", lambda e: e.reciprocal(
                        rdn_t[:],
                        asb_t[:].rearrange("p (s c) -> p s c", c=65)
                        [:, :, 64]),
                        reads=[asb_b], writes=[rdn_b])
                    for sc in range(NCH):
                        kb.op("dve", lambda e, sc=sc: e.tensor_scalar(
                            t1_t[:], asb_t[:, sc * 65 : sc * 65 + 64],
                            rdn_t[:, sc : sc + 1], wcol_t[:, 1:2],
                            ALU.mult, ALU.mult),
                            reads=[asb_b], sreads=[rdn_b, wcol_b],
                            writes=[t1_b])
                        kb.op("dve", lambda e, sc=sc, o=o:
                              e.tensor_scalar(
                                  t2_t[:],
                                  numt_t[:, sc * KV + (o % 8) * 64 :
                                         sc * KV + (o % 8) * 64 + 64],
                                  rden_t[:, sc : sc + 1],
                                  wcol_t[:, 0:1], ALU.mult, ALU.mult),
                              reads=[numt_b],
                              sreads=[rden_b, wcol_b],
                              writes=[t2_b])
                        kb.op("dve", lambda e: e.tensor_tensor(
                            t1_t[:], t1_t[:], t2_t[:], ALU.add),
                            reads=[t1_b, t2_b], writes=[t1_b],
                            accum=True)
                        kb.op("pe", lambda e: e.transpose(
                            ptrx[0:64, 0:128], t1_t[:], eye_bf[:]),
                            reads=[t1_b, eye_bf_b], writes=[ptrx_b])
                        g = o // 2
                        rbase = (o % 2) * 64
                        kb.op("dve", lambda e, g=g, rbase=rbase, sc=sc:
                              e.tensor_tensor(
                                  x2_t[g][rbase : rbase + 64,
                                          sc * 128 : (sc + 1) * 128],
                                  x2_t[g][rbase : rbase + 64,
                                          sc * 128 : (sc + 1) * 128],
                                  ptrx[0:64, 0:128], ALU.add),
                              reads=[x2_b[g], ptrx_b],
                              writes=[x2_b[g]], accum=True)

                pend_blend = None
                for nq in range(2):
                    if pend_blend is not None:
                        pend_blend()
                        pend_blend = None
                    for kc in range(4):
                        for half in range(2):
                            o = 8 * nq + 2 * kc + half
                            j, base = qloc[o]
                            qsl = qb_t[j][base : base + 64, :]
                            e1 = el1_t[base : base + 64, :]
                            dsl = sigma_t[kc][64 * half : 64 * half + 64, :]
                            kb.op("dve", lambda e, qsl=qsl, e1=e1:
                                  e.tensor_scalar(
                                      e1, qsl, 0.0, None, ALU.min),
                                  reads=[qb_b[j]], writes=[el1_b])
                            kb.op("act", lambda e, e1=e1: e.activation(
                                e1, e1, AF.Exp),
                                reads=[el1_b], writes=[el1_b], accum=True)
                            kb.op("dve", lambda e, qsl=qsl, e1=e1:
                                  e.scalar_tensor_tensor(
                                      e1, qsl, 0.0, e1, ALU.max, ALU.add),
                                  reads=[qb_b[j], el1_b], writes=[el1_b],
                                  accum=True)
                            kb.op("dve", lambda e, e1=e1, dsl=dsl:
                                  e.tensor_scalar(
                                      dsl, e1, -1.0, None, ALU.add),
                                  reads=[el1_b], writes=[sigma_b[kc]],
                                  accum=True)
                    for tc in range(NCH):
                        tsl = slice(tc * 128, (tc + 1) * 128)
                        for kc in range(4):
                            mt, mb = mem_tiles[kc]
                            kb.op("pe", lambda e, kc=kc, tsl=tsl, mt=mt:
                                  e.matmul(pnum[:], sigma_t[kc][:, tsl],
                                           mt[:], start=(kc == 0),
                                           stop=(kc == 3)),
                                  reads=[sigma_b[kc], mb], writes=[pnum_b],
                                  accum=(kc > 0))
                            nt, nb = memn_tiles[kc]
                            kb.op("pe", lambda e, kc=kc, tc=tc, tsl=tsl,
                                  nt=nt: e.matmul(
                                      pden[:, tc : tc + 1],
                                      sigma_t[kc][:, tsl], nt[:],
                                      start=(kc == 0), stop=(kc == 3)),
                                  reads=[sigma_b[kc], nb], writes=[pden_b],
                                  accum=(kc > 0 or tc > 0))
                        kb.op("dve", lambda e, tc=tc: e.tensor_copy(
                            numt_t[:, tc * KV : (tc + 1) * KV], pnum[:]),
                            reads=[pnum_b], writes=[numt_b], accum=True)
                    kb.op("dve", lambda e: e.reciprocal(
                        rden_t[:], pden[:]),
                        reads=[pden_b], writes=[rden_b])
                    for qh8 in range(8):
                        o = 8 * nq + qh8
                        kvh = o // 2
                        j, base = qloc[o]
                        slot = 2 * j + (0 if base == 0 else 1)
                        for tc in range(NCH):
                            tsl = slice(tc * 128, (tc + 1) * 128)
                            etile_t, etile_b = etile2[tc % 2]
                            for sb2 in range(4):
                                ssl = slice(sb2 * TBW, (sb2 + 1) * TBW)
                                pt, pb = psc[sb2 % 2]
                                kb.op("pe", lambda e, tsl=tsl, ssl=ssl,
                                      j=j, base=base, kvh=kvh, pt=pt:
                                      e.matmul(
                                          pt[:],
                                          kbh_t[kvh // 2][
                                              (kvh % 2) * 64 :
                                              (kvh % 2) * 64 + 64, tsl],
                                          qb_t[j][base : base + 64, ssl],
                                          start=True, stop=True),
                                      reads=[kbh_b[kvh // 2], qb_b[j]],
                                      writes=[pb])
                                kb.op("act", lambda e, ssl=ssl, pt=pt,
                                      slot=slot: e.activation(
                                          etile_t[:, ssl], pt[:], AF.Exp,
                                          bias=nbb_t[:, slot : slot + 1],
                                          scale=0.125),
                                      reads=[pb, nbb_b], writes=[etile_b],
                                      accum=True)
                            for sc in range(NCH):
                                pk_t, pk_b = pad3[sc // 7]
                                off = (sc % 7) * 65
                                first = (tc == 0 and sc % 7 == 0)
                                kb.op("pe", lambda e, sc=sc, tc=tc, kvh=kvh,
                                      first=first, pk_t=pk_t, off=off:
                                      e.matmul(
                                          pk_t[:, off : off + 65],
                                          etile_t[:, sc * 128 :
                                                  (sc + 1) * 128],
                                          vtm_t[kvh][:, tc * 65 :
                                                     (tc + 1) * 65],
                                          start=first,
                                          stop=(tc == NCH - 1)),
                                      reads=[etile_b, vtm_b[kvh]],
                                      writes=[pk_b], accum=(not first))
                        if pend_blend is not None:
                            pend_blend()
                        for k3 in range(3):
                            w3 = min(455, NCH * 65 - 455 * k3)
                            kb.op("dve", lambda e, k3=k3, w3=w3:
                                  e.tensor_copy(
                                      asb_t[:, 455 * k3 : 455 * k3 + w3],
                                      pad3[k3][0][:, 0:w3]),
                                  reads=[pad3[k3][1]], writes=[asb_b],
                                  accum=(k3 > 0))
                        pend_blend = (lambda o=o: blend_head(o, 0))
                if pend_blend is not None:
                    pend_blend()
                    pend_blend = None
                kb.barrier()
            kb.barrier()

    if PHASE >= 5:
        attnb_phase()
    if PHASE >= 6:
        ffn_phase("4")

    kb.tap("x3", [D, T], [(x2_t[g][:], x2_b[g],
                           (lambda t, g=g: t[g * 128:(g + 1) * 128]))
                          for g in range(G)])

    with contextlib.ExitStack() as sO:
        lps4, lsb4 = PS(sO), SB(sO)
        pout = [lps4("pout0", [128, 512], F32), lps4("pout1", [128, 512], F32)]
        otile = [lsb4("otile0", [128, D], F32), lsb4("otile1", [128, D], F32)]
        outs = kb.stream("out_store")
        for tc in range(NCH):
            ot, ob = otile[tc % 2]
            for g in range(G):
                pt, pb = pout[g % 2]
                kb.op("pe", lambda e, tc=tc, g=g, pt=pt: e.transpose(
                    pt[:, 0:128], x2_t[g][:, tc * 128 : (tc + 1) * 128],
                    eye_f[:]),
                    reads=[x2_b[g], eye_f_b], writes=[pb])
                kb.op("dve", lambda e, g=g, pt=pt, ot=ot: e.tensor_copy(
                    ot[:, g * 128 : (g + 1) * 128], pt[:, 0:128]),
                    reads=[pb], writes=[ob], accum=True)
            kb.dma("sp", outs, out_ap[tc * 128 : (tc + 1) * 128, :],
                   ot[:], reads=[ob])
    kb.finish()
    st.close()
    return kb


def kernel(**inputs):
    per_core, shared = prep_inputs(inputs)
    kb = build_program()
    in_maps = [dict(pc, **shared) for pc in per_core]
    res = run_bass_kernel_spmd(kb.nc, in_maps, core_ids=list(range(8)))
    out = np.stack([r["out"] for r in res.results], axis=0)
    if DBG:
        kernel.dbg = [{k: r["dbg_" + k] for k in DBG} for r in res.results]
    return out
